# version 19
# baseline (speedup 1.0000x reference)
import numpy as np
from contextlib import ExitStack

import concourse.bass as bass
import concourse.mybir as mybir
from concourse.bass_utils import run_bass_kernel_spmd

F32 = mybir.dt.float32
BF16 = mybir.dt.bfloat16
U8 = mybir.dt.uint8
AF = mybir.ActivationFunctionType
ALU = mybir.AluOpType
AX = mybir.AxisListType

LN_EPS = 1e-5
RMS_EPS = 1e-6
THETA = 10000.0


class Sig:
    def __init__(self, sem):
        self.sem = sem
        self.count = 0

    def inc(self, inst, k=1):
        inst.then_inc(self.sem, k)
        self.count += k
        return (self, self.count)


class SyncCtx:
    def __init__(self):
        self.waited = {}

    def wait(self, eng, token):
        if token is None:
            return
        sig, val = token
        e = getattr(eng, "_e", eng)
        m = self.waited.setdefault(id(e), {})
        if val > m.get(id(sig), 0):
            eng.wait_ge(sig.sem, val)
            m[id(sig)] = val

    def wait_many(self, eng, tokens):
        best = {}
        for t in tokens:
            if t is None:
                continue
            sig, val = t
            if id(sig) not in best or val > best[id(sig)][1]:
                best[id(sig)] = (sig, val)
        for sig, val in best.values():
            self.wait(eng, (sig, val))


class LastW:
    _SKIP = {"wait_ge", "wait_op", "sem_inc", "nop", "engine_nop", "drain",
             "wait_many"}

    def __init__(self, eng):
        object.__setattr__(self, "_e", eng)
        object.__setattr__(self, "last", None)

    def __getattr__(self, n):
        a = getattr(object.__getattribute__(self, "_e"), n)
        if callable(a) and not n.startswith("_") and n not in LastW._SKIP:
            def f(*args, **kw):
                r = a(*args, **kw)
                if hasattr(r, "then_inc"):
                    object.__setattr__(self, "last", r)
                return r
            return f
        return a


def full_cfg():
    return dict(D=4096, QR=1536, KVR=512, RD=64, ND=128, VD=128, NH=32,
                S=2048, Q=1024, NE=8, RANK=16, DO=4096)


def small_cfg():
    return dict(D=1024, QR=512, KVR=256, RD=64, ND=128, VD=128, NH=2,
                S=256, Q=256, NE=8, RANK=16, DO=1024)


def prep_weights(cfg, w):
    D, QR, KVR, RD = cfg["D"], cfg["QR"], cfg["KVR"], cfg["RD"]
    ND, VD, NH, S = cfg["ND"], cfg["VD"], cfg["NH"], cfg["S"]
    NE, RANK, DO = cfg["NE"], cfg["RANK"], cfg["DO"]
    DK, QRK, KVK = D // 128, QR // 128, KVR // 128
    NP = NH // 2
    QK = ND + RD
    ER = NE * RANK
    c = np.ascontiguousarray
    out = {}
    nqb = max(1, QR // 384) if QR % 384 == 0 else 1
    out["wqa"] = c(w["w_qa"].reshape(DK, 128, nqb, QR // nqb).transpose(2, 1, 0, 3))
    out["wkva"] = c(w["w_kva"].reshape(DK, 128, KVR + RD).transpose(1, 0, 2))
    t = w["w_qb"].reshape(QRK, 128, NP, 2, QK)
    wqb = np.concatenate([t[:, :, :, 0, :ND], t[:, :, :, 1, :ND],
                          t[:, :, :, 0, ND:], t[:, :, :, 1, ND:]], axis=-1)
    out["wqb"] = c(wqb.transpose(2, 1, 0, 3))
    wkvb = w["w_kvb"].reshape(KVK, 128, NP, 2 * (ND + VD))
    out["wkvb"] = c(wkvb.transpose(2, 1, 0, 3))
    wob = max(1, D // 512)
    out["wo"] = c(w["w_o"].reshape(NH, 128, wob, D // wob).transpose(2, 1, 0, 3))
    fob = max(1, DO // 512)
    out["wbt"] = c(w["W_base"].T.reshape(DK, 128, fob, DO // fob).transpose(2, 1, 0, 3))
    out["wrt"] = c(w["W_router"].T.reshape(DK, 128, NE).transpose(1, 0, 2))
    out["at"] = c(w["A_stack"].reshape(ER, D).T.reshape(DK, 128, ER).transpose(1, 0, 2))
    out["bflat"] = c(w["B_stack"].transpose(0, 2, 1).reshape(ER, DO))
    out["bbase"] = c(w["b_base"].reshape(1, DO))
    out["gqa"] = c(w["g_qa"].reshape(QRK, 128).T)
    out["gkva"] = c(w["g_kva"].reshape(KVK, 128).T)
    inv_freq = 1.0 / THETA ** (np.arange(0, RD, 2, dtype=np.float64) / RD)
    tt = np.arange(S, dtype=np.float64)
    emb = np.concatenate([np.outer(tt, inv_freq)] * 2, -1)
    out["_cos"] = np.cos(emb).astype(np.float32)
    out["_sin"] = np.sin(emb).astype(np.float32)
    out["emat"] = c(np.repeat(np.eye(NE, dtype=np.float32), RANK, axis=1))
    out["ident"] = np.eye(128, dtype=np.float32)
    return out


def build_program(cfg, dbg=False):
    D, QR, KVR, RD = cfg["D"], cfg["QR"], cfg["KVR"], cfg["RD"]
    ND, VD, NH, S, Q = cfg["ND"], cfg["VD"], cfg["NH"], cfg["S"], cfg["Q"]
    NE, RANK, DO = cfg["NE"], cfg["RANK"], cfg["DO"]
    DK, QRK, KVK = D // 128, QR // 128, KVR // 128
    NP = NH // 2
    ER = NE * RANK
    KT, QT = S // 128, Q // 128
    nqb = max(1, QR // 384) if QR % 384 == 0 else 1
    NOB = max(1, D // 512)
    OBW = D // NOB
    FOB = max(1, DO // 512)
    FBW = DO // FOB
    KH = max(1, NH // 16)
    KHN = NH // KH
    DKH = max(1, DK // 8)
    DKHN = DK // DKH
    cscale = float((ND + RD) ** -0.5)
    qch = [(s0, min(512, Q - s0)) for s0 in range(0, Q, 512)]
    sch = [(s0, min(512, S - s0)) for s0 in range(0, S, 512)]
    DGRP = 2 if QT >= 2 else 1
    GT = QT // DGRP
    HLF = RD // 2

    nc = bass.Bass("TRN2", target_bir_lowering=False, debug=False)

    def din(name, shape):
        return nc.dram_tensor(name, list(shape), F32, kind="ExternalInput")

    xb_d = din("xb", [S, D])
    wqa_d = din("wqa", [nqb, 128, DK, QR // nqb])
    wkva_d = din("wkva", [128, DK, KVR + RD])
    wqb_d = din("wqb", [NP, 128, QRK, 2 * ND + 2 * RD])
    wkvb_d = din("wkvb", [NP, 128, KVK, 2 * (ND + VD)])
    wo_d = din("wo", [NOB, 128, NH, OBW])
    wbt_d = din("wbt", [FOB, 128, DK, FBW])
    wrt_d = din("wrt", [128, DK, NE])
    at_d = din("at", [128, DK, ER])
    bflat_d = din("bflat", [ER, DO])
    bbase_d = din("bbase", [1, DO])
    gqa_d = din("gqa", [128, QRK])
    gkva_d = din("gkva", [128, KVK])
    cosk_d = din("cosk", [128, KT, RD])
    sink_d = din("sink", [128, KT, RD])
    cosq2_d = din("cosq2", [RD, 2 * Q])
    sinq2_d = din("sinq2", [RD, 2 * Q])
    emat_d = din("emat", [NE, 128])
    ident_d = din("ident", [128, 128])
    out_d = nc.dram_tensor("out", [Q, D], F32, kind="ExternalOutput")

    ctx = ExitStack()
    S_ = SyncCtx()

    def mksig(name):
        return Sig(ctx.enter_context(nc.semaphore(name)))

    dX = [mksig("dX0"), mksig("dX1")]
    dW = [mksig("dW0"), mksig("dW1")]
    dW2 = [mksig("dW20"), mksig("dW21")]
    dC = mksig("dC")
    dCg = mksig("dCg")
    dO = [mksig("dO0"), mksig("dO1")]
    sigT = mksig("sT")
    sigV = mksig("sV")
    sigS = mksig("sS")

    PB = (nc.sbuf_bytes_remaining - 2048) // 32 * 32
    pool = nc.alloc_sbuf_tensor("pool", [128, PB], U8)

    offreg = {}

    def view(off, part, shape, dt):
        esz = 2 if dt == BF16 else 4
        n = 1
        for s in shape:
            n *= s
        assert off % 32 == 0 and off + n * esz <= PB, (off, shape, dt)
        ap = pool[0:part, off:off + n * esz].bitcast(dt)
        if len(shape) > 1:
            pat = " ".join(f"d{i}" for i in range(len(shape)))
            kw = {f"d{i}": shape[i] for i in range(len(shape))}
            ap = ap.rearrange(f"p ({pat}) -> p {pat}", **kw)
        offreg[id(ap)] = (off, n * esz)
        return ap

    class Cursor:
        def __init__(self, start, limit):
            self.off = start
            self.limit = limit

        def alloc(self, part, shape, dt):
            esz = 2 if dt == BF16 else 4
            n = 1
            for s in shape:
                n *= s
            off = self.off
            self.off = (off + n * esz + 31) // 32 * 32
            assert self.off <= self.limit, (off, shape, self.off, self.limit)
            return view(off, part, shape, dt)

    cc = Cursor(0, 3072)

    ident_b = cc.alloc(128, [128], BF16)
    ident_f = cc.alloc(128, [128], F32)
    ones_col = cc.alloc(128, [1], BF16)
    ones_row = cc.alloc(1, [128], BF16)
    emat_b = cc.alloc(NE, [128], BF16)
    epsln = cc.alloc(128, [1], F32)
    col_a = [cc.alloc(128, [1], F32) for _ in range(2)]
    col_b = [cc.alloc(128, [1], F32) for _ in range(2)]
    col_c = [cc.alloc(128, [1], F32) for _ in range(2)]
    mv = [cc.alloc(128, [2], F32) for _ in range(2)]
    stats = [cc.alloc(128, [48], F32) for _ in range(2)]
    negB_b = cc.alloc(1, [2], BF16)
    negB_col = cc.alloc(128, [1], F32)
    gm_sb = cc.alloc(1, [4], F32)
    rs_x = [cc.alloc(128, [1], F32) for _ in range(2)]
    rs_t = [cc.alloc(128, [1], F32) for _ in range(2)]
    magic_i = cc.alloc(128, [1], F32)
    gqa_b = cc.alloc(128, [QRK], F32)
    gkva_b = cc.alloc(128, [KVK], F32)

    ct = Cursor((cc.off + 255) // 256 * 256, PB)
    cosk_b = ct.alloc(128, [KT, RD], F32)
    sink_b = ct.alloc(128, [KT, RD], F32)
    cosq2_b = ct.alloc(RD, [2 * Q], BF16)
    sinq2_b = ct.alloc(RD, [2 * Q], BF16)
    k_cT = ct.alloc(128, [KVK, S], BF16)
    k_ropeT = ct.alloc(RD, [S], BF16)
    q_cT = ct.alloc(128, [QRK, Q], BF16)
    V0 = (ct.off + 511) // 512 * 512

    ZLOW = (cc.off + 255) // 256 * 256
    tok = {}

    with nc.Block() as blk:
        @blk.sync
        def _(sy):
            sy = LastW(sy)
            dC.inc(sy.dma_start(out=cosk_b[:], in_=cosk_d[:]), 16)
            dC.inc(sy.dma_start(out=sink_b[:], in_=sink_d[:]), 16)
            dC.inc(sy.dma_start(out=gqa_b[:], in_=gqa_d[:]), 16)
            dC.inc(sy.dma_start(out=gkva_b[:], in_=gkva_d[:]), 16)
            dC.inc(sy.dma_start(out=ident_f[:], in_=ident_d[:]), 16)

        @blk.gpsimd
        def _(g):
            g = LastW(g)
            dCg.inc(g.dma_start(out=ident_b[:], in_=ident_d[:]), 16)
            dCg.inc(g.dma_start(out=emat_b[:], in_=emat_d[:]), 16)
            dCg.inc(g.dma_start(out=cosq2_b[:], in_=cosq2_d[:]), 16)
            dCg.inc(g.dma_start(out=sinq2_b[:], in_=sinq2_d[:]), 16)

        @blk.vector
        def _(v):
            v = LastW(v)
            v.memset(ones_col[:], 1.0)
            v.memset(ones_row[:], 1.0)
            v.memset(magic_i[:].bitcast(mybir.dt.int32), 0x5F3759DF)
            tok["vconsts"] = sigV.inc(v.memset(epsln[:], LN_EPS))
    tok["consts_f"] = (dC, 80)
    tok["consts_b"] = (dCg, 64)
    tok["ident_ready"] = (dCg, 64)

    def v_rsqrt(v, px, src_ap, eps):
        v.tensor_scalar(out=rs_x[px][:], in0=src_ap, scalar1=float(eps),
                        scalar2=None, op0=ALU.add)
        t = sigV.inc(v.last)
        S_.wait(v, t)
        xi = rs_x[px][:].bitcast(mybir.dt.int32)
        yi = col_a[px][:].bitcast(mybir.dt.int32)
        v.tensor_scalar(out=yi, in0=xi, scalar1=1, scalar2=None,
                        op0=ALU.arith_shift_right)
        t = sigV.inc(v.last)
        S_.wait(v, t)
        v.scalar_tensor_tensor(out=yi, in0=yi, scalar=-1.0,
                               in1=magic_i[:].bitcast(mybir.dt.int32),
                               op0=ALU.mult, op1=ALU.add)
        t = sigV.inc(v.last)
        S_.wait(v, t)
        for it_ in range(3):
            v.tensor_tensor(out=rs_t[px][:], in0=col_a[px][:],
                            in1=col_a[px][:], op=ALU.mult)
            t = sigV.inc(v.last)
            S_.wait(v, t)
            v.tensor_tensor(out=rs_t[px][:], in0=rs_t[px][:], in1=rs_x[px][:],
                            op=ALU.mult)
            t = sigV.inc(v.last)
            S_.wait(v, t)
            v.tensor_scalar(out=rs_t[px][:], in0=rs_t[px][:], scalar1=-0.5,
                            scalar2=1.5, op0=ALU.mult, op1=ALU.add)
            t = sigV.inc(v.last)
            S_.wait(v, t)
            v.tensor_tensor(out=col_a[px][:], in0=col_a[px][:],
                            in1=rs_t[px][:], op=ALU.mult)
            if it_ < 2:
                t = sigV.inc(v.last)
                S_.wait(v, t)

    def v_rsqrtN(v, src_ap, out_ap, tx, tt, mg, eps):
        v.tensor_scalar(out=tx, in0=src_ap, scalar1=float(eps),
                        scalar2=None, op0=ALU.add)
        t = sigV.inc(v.last)
        S_.wait(v, t)
        xi = tx.bitcast(mybir.dt.int32)
        yi = out_ap.bitcast(mybir.dt.int32)
        v.tensor_scalar(out=yi, in0=xi, scalar1=1, scalar2=None,
                        op0=ALU.arith_shift_right)
        t = sigV.inc(v.last)
        S_.wait(v, t)
        v.scalar_tensor_tensor(out=yi, in0=yi, scalar=-1.0,
                               in1=mg.bitcast(mybir.dt.int32),
                               op0=ALU.mult, op1=ALU.add)
        t = sigV.inc(v.last)
        S_.wait(v, t)
        for it_ in range(3):
            v.tensor_tensor(out=tt, in0=out_ap, in1=out_ap, op=ALU.mult)
            t = sigV.inc(v.last)
            S_.wait(v, t)
            v.tensor_tensor(out=tt, in0=tt, in1=tx, op=ALU.mult)
            t = sigV.inc(v.last)
            S_.wait(v, t)
            v.tensor_scalar(out=tt, in0=tt, scalar1=-0.5, scalar2=1.5,
                            op0=ALU.mult, op1=ALU.add)
            t = sigV.inc(v.last)
            S_.wait(v, t)
            v.tensor_tensor(out=out_ap, in0=out_ap, in1=tt, op=ALU.mult)
            if it_ < 2:
                t = sigV.inc(v.last)
                S_.wait(v, t)

    def ln_stats(v, src, width, key, px):
        nch = max(1, width // 512)
        cw = width // nch
        for i in range(nch):
            v.bn_stats(out=stats[px][:, 6 * i:6 * i + 6],
                       in_=src[:, i * cw:(i + 1) * cw])
        t = sigV.inc(v.last)
        S_.wait(v, t)
        v.bn_aggr(out=mv[px][:], in_=stats[px][:, 0:6 * nch])
        tok[key] = sigV.inc(v.last)

    def ln_finish(v, key_sqrt, px):
        S_.wait(v, tok[key_sqrt])
        v.tensor_scalar(out=col_b[px][:], in0=mv[px][:, 0:1], scalar1=col_a[px][:],
                        scalar2=-1.0, op0=ALU.mult, op1=ALU.mult)
        t = sigV.inc(v.last)
        S_.wait(v, t)
        return t

    va = Cursor(V0, PB)
    ln1T_own = va.alloc(128, [QT, DK * 128], BF16)
    x_buf0 = va.alloc(128, [D], F32)
    ln1_bufs = [va.alloc(128, [D], BF16) for _ in range(2)]
    ln1T_tmps = [va.alloc(128, [DK * 128], BF16)]
    wkva_b = va.alloc(128, [DK, KVR + RD], BF16)
    kc_tmps = [va.alloc(128, [KVR], BF16) for _ in range(2)]
    kr_tmpA = [va.alloc(128, [RD], F32) for _ in range(2)]
    kr_tmpB = [va.alloc(128, [RD], F32) for _ in range(2)]
    krb_tmps = [va.alloc(128, [RD], BF16) for _ in range(2)]
    _qo, _qn = offreg[id(q_cT)]
    if _qn >= D * 4 + DK * 128 * 2:
        x_buf1 = view(_qo, 128, [D], F32)
        ln1T_tmps.append(view(_qo + D * 4, 128, [DK * 128], BF16))
    else:
        x_buf1 = va.alloc(128, [D], F32)
        ln1T_tmps.append(va.alloc(128, [DK * 128], BF16))
    x_bufs = [x_buf0, x_buf1]

    psA = ExitStack()
    trP = [psA.enter_context(nc.psum_tensor(f"trP{i}", [128, 2048], BF16))
           for i in range(2)]
    psA1 = ExitStack()
    kvP = [psA1.enter_context(nc.psum_tensor(f"kvP{i}", [128, KVR + RD], F32))
           for i in range(2)]

    with nc.Block() as blk:
        @blk.gpsimd
        def _(g):
            g = LastW(g)
            tok["wkva"] = dW[0].inc(g.dma_start(out=wkva_b[:], in_=wkva_d[:]), 16)

    xv = xb_d.rearrange("(t p) d -> t p d", p=128)
    kvsplits = [(0, min(512, KVR))]
    if KVR > 512:
        kvsplits.append((512, KVR - 512))

    def a1_xload(blk, i):
        par = i % 2
        if True:
            @blk.sync
            def _(sy):
                sy = LastW(sy)
                S_.wait(sy, tok.get(("lnapply", i - 2)))
                tok[("xload", i)] = dX[par].inc(
                    sy.dma_start(out=x_bufs[par][:], in_=xv[i, :, :]), 16)

    def a1_ln(blk, i):
        par = i % 2
        if True:
            @blk.vector
            def _(v):
                v = LastW(v)
                S_.wait(v, tok[("xload", i)])
                S_.wait(v, tok["vconsts"])
                ln_stats(v, x_bufs[par], D, ("ln1s", i), par)

            @blk.vector
            def _(v):
                v = LastW(v)
                S_.wait(v, tok[("ln1s", i)])
                S_.wait(v, tok.get(("lnapply", i - 2)))
                v_rsqrt(v, par, mv[par][:, 1:2], LN_EPS)
                tok[("ln1sq", i)] = sigV.inc(v.last)

            @blk.vector
            def _(v):
                v = LastW(v)
                tok[("ln1fin", i)] = ln_finish(v, ("ln1sq", i), par)

    def a1_transpose(blk, i):
        par = i % 2
        if True:
            @blk.tensor
            def _(t):
                t = LastW(t)
                S_.wait(t, tok[("lnapply", i)])
                S_.wait(t, tok["ident_ready"])
                nw = (DK + 15) // 16
                S_.wait(t, tok.get(("lnT_evac", i - 2, nw - 1)))
                S_.wait(t, tok.get(("kcT_evac", i - 2)))
                for wv in range(nw):
                    if wv > 0:
                        S_.wait(t, tok[("lnT_evac", i, wv - 1)])
                    for k in range(wv * 16, min(wv * 16 + 16, DK)):
                        t.transpose(trP[par][:, (k % 16) * 128:(k % 16) * 128 + 128],
                                    ln1_bufs[par][:, k * 128:(k + 1) * 128], ident_b[:])
                    tok[("lnT_mm", i, wv)] = sigT.inc(t.last)
                tok[("lnT_mm_all", i)] = tok[("lnT_mm", i, nw - 1)]

    def a1_kvmm(blk, i):
        par = i % 2
        if True:
            @blk.tensor
            def _(t):
                t = LastW(t)
                S_.wait(t, tok[("lnT_done", i)])
                S_.wait(t, tok["wkva"])
                S_.wait(t, tok.get(("kv_used", i - 2)))
                if i < QT:
                    lsrc = ln1T_own[:, i, :]
                else:
                    lsrc = ln1T_tmps[par][:]
                for (c0, w) in kvsplits:
                    for k in range(DK):
                        t.matmul(kvP[par][:, c0:c0 + w],
                                 lsrc[:, k * 128:(k + 1) * 128],
                                 wkva_b[:, k, c0:c0 + w],
                                 start=(k == 0), stop=(k == DK - 1))
                for k in range(DK):
                    t.matmul(kvP[par][:, KVR:KVR + RD],
                             lsrc[:, k * 128:(k + 1) * 128],
                             wkva_b[:, k, KVR:KVR + RD],
                             start=(k == 0), stop=(k == DK - 1))
                tok[("kv_mm", i)] = sigT.inc(t.last)

    def a1_kvchain(blk, i):
        par = i % 2
        if True:
            @blk.vector
            def _(v):
                v = LastW(v)
                S_.wait(v, tok[("kv_mm", i)])
                nch = max(1, KVR // 512)
                cw = KVR // nch
                for c_ in range(nch):
                    v.bn_stats(out=stats[par][:, 6 * c_:6 * c_ + 6],
                               in_=kvP[par][:, c_ * cw:(c_ + 1) * cw])
                t = sigV.inc(v.last)
                S_.wait(v, t)
                v.bn_aggr(out=mv[par][:], in_=stats[par][:, 0:6 * nch])
                t = sigV.inc(v.last)
                S_.wait(v, t)
                v.tensor_scalar(out=col_c[par][:], in0=mv[par][:, 0:1],
                                scalar1=mv[par][:, 0:1], scalar2=None, op0=ALU.mult)
                t = sigV.inc(v.last)
                S_.wait(v, t)
                v.tensor_tensor(out=col_c[par][:], in0=col_c[par][:],
                                in1=mv[par][:, 1:2], op=ALU.add)
                tok[("kv_ms", i)] = sigV.inc(v.last)

            @blk.vector
            def _(v):
                v = LastW(v)
                S_.wait(v, tok[("kv_ms", i)])
                v_rsqrtN(v, col_c[par][:], col_c[par][:], rs_x[par][:],
                         rs_t[par][:], magic_i[:], 1e-30)
                tok[("kv_sq", i)] = sigV.inc(v.last)

            @blk.vector
            def _(v):
                v = LastW(v)
                S_.wait(v, tok[("kv_sq", i)])
                v.tensor_tensor(out=col_c[par][:], in0=col_c[par][:],
                                in1=rs_x[par][:], op=ALU.mult)
                t = sigV.inc(v.last)
                S_.wait(v, t)
                v.tensor_scalar(out=col_c[par][:], in0=col_c[par][:],
                                scalar1=RMS_EPS, scalar2=None, op0=ALU.add)
                t = sigV.inc(v.last)
                S_.wait(v, t)
                v.reciprocal(out=col_c[par][:], in_=col_c[par][:])
                t = sigV.inc(v.last)
                S_.wait(v, t)
                S_.wait(v, tok.get(("kcT_mm", i - 2)))
                S_.wait(v, tok["consts_f"])
                v.tensor_scalar(out=kc_tmps[par][:], in0=kvP[par][:, 0:KVR],
                                scalar1=col_c[par][:], scalar2=None, op0=ALU.mult)
                v.tensor_tensor(out=kr_tmpA[par][:], in0=kvP[par][:, KVR:KVR + RD],
                                in1=cosk_b[:, i, :], op=ALU.mult)
                v.tensor_scalar(out=kr_tmpB[par][:, 0:HLF],
                                in0=kvP[par][:, KVR + HLF:KVR + RD],
                                scalar1=-1.0, scalar2=None, op0=ALU.mult)
                v.tensor_copy(out=kr_tmpB[par][:, HLF:RD],
                              in_=kvP[par][:, KVR:KVR + HLF])
                t = sigV.inc(v.last)
                S_.wait(v, t)
                v.tensor_tensor(out=kr_tmpB[par][:], in0=kr_tmpB[par][:],
                                in1=sink_b[:, i, :], op=ALU.mult)
                t = sigV.inc(v.last)
                S_.wait(v, t)
                v.tensor_tensor(out=krb_tmps[par][:], in0=kr_tmpA[par][:],
                                in1=kr_tmpB[par][:], op=ALU.add)
                tok[("kv_rdy", i)] = sigV.inc(v.last)
                tok[("kv_used", i)] = tok[("kv_rdy", i)]

            @blk.tensor
            def _(t):
                t = LastW(t)
                S_.wait(t, tok[("kv_rdy", i)])
                S_.wait(t, tok.get(("kcT_evac", i - 2)))
                S_.wait(t, tok.get(("lnT_done", i)))
                for k in range(KVK):
                    t.transpose(trP[par][:, k * 128:(k + 1) * 128],
                                kc_tmps[par][:, k * 128:(k + 1) * 128], ident_b[:])
                t.transpose(trP[par][0:RD, KVK * 128:KVK * 128 + 128],
                            krb_tmps[par][:], ident_b[:])
                tok[("kcT_mm", i)] = sigT.inc(t.last)



    nwA = (DK + 15) // 16
    cnt = sigS.count
    for i in range(KT + 2):
        if i >= 2:
            tok[("kcT_evac", i - 2)] = (sigS, cnt + 1)
            cnt += 1
        if i < KT:
            tok[("lnapply", i)] = (sigS, cnt + 1)
            for wv in range(nwA):
                tok[("lnT_evac", i, wv)] = (sigS, cnt + 2 + wv)
            tok[("lnT_done", i)] = tok[("lnT_evac", i, nwA - 1)]
            cnt += 1 + nwA

    with nc.Block() as blk:
        a1_xload(blk, 0)
        for it in range(KT + 2):
            if it + 1 < KT:
                a1_xload(blk, it + 1)
            if it < KT:
                a1_ln(blk, it)
            if 1 <= it <= KT:
                a1_kvmm(blk, it - 1)
            if 2 <= it:
                a1_kvchain(blk, it - 2)
            if it < KT:
                a1_transpose(blk, it)

        @blk.scalar
        def _(s):
            s = LastW(s)
            for i in range(KT + 2):
                par = i % 2
                if i >= 2:
                    ik = i - 2
                    pk = ik % 2
                    S_.wait(s, tok[("kcT_mm", ik)])
                    for k in range(KVK):
                        s.activation(out=k_cT[:, k, ik * 128:(ik + 1) * 128],
                                     in_=trP[pk][:, k * 128:(k + 1) * 128],
                                     func=AF.Identity,
                                     scale=gkva_b[:, k:k + 1], bias=0.0)
                    s.copy(out=k_ropeT[:, ik * 128:(ik + 1) * 128],
                           in_=trP[pk][0:RD, KVK * 128:KVK * 128 + 128])
                    tk = sigS.inc(s.last)
                    assert tk == tok[("kcT_evac", ik)], (tk, tok[("kcT_evac", ik)])
                if i < KT:
                    S_.wait(s, tok[("ln1fin", i)])
                    S_.wait(s, tok[("xload", i)])
                    S_.wait(s, tok.get(("lnT_mm_all", i - 2)))
                    s.activation(out=ln1_bufs[par][:], in_=x_bufs[par][:],
                                 func=AF.Identity,
                                 scale=col_a[par][:], bias=col_b[par][:])
                    tk = sigS.inc(s.last)
                    assert tk == tok[("lnapply", i)]
                    for wv in range(nwA):
                        S_.wait(s, tok[("lnT_mm", i, wv)])
                        nk = min(16, DK - wv * 16)
                        if i < QT:
                            dst = ln1T_own[:, i, wv * 16 * 128:(wv * 16 + nk) * 128]
                        else:
                            dst = ln1T_tmps[par][:, wv * 16 * 128:(wv * 16 + nk) * 128]
                        s.copy(out=dst, in_=trP[par][:, 0:nk * 128])
                        tk = sigS.inc(s.last)
                        assert tk == tok[("lnT_evac", i, wv)]
    psA1.close()
    vb = Cursor(V0, PB)
    wqb_buf = [vb.alloc(128, [QRK, 2 * ND + 2 * RD], BF16) for _ in range(2)]
    wkvb_buf = [vb.alloc(128, [KVK, 2 * (ND + VD)], BF16) for _ in range(2)]
    q_nT = [[vb.alloc(128, [Q], BF16) for _ in range(2)] for _ in range(2)]
    q_rT = [vb.alloc(RD, [2 * Q], BF16) for _ in range(2)]
    ropes_b = vb.alloc(RD, [2 * Q], BF16)
    rot_b = vb.alloc(RD, [2 * Q], BF16)
    ropet1 = vb.alloc(RD, [2 * Q], BF16)
    k_nT = [vb.alloc(128, [S], BF16) for _ in range(2)]
    v_sb = [vb.alloc(128, [KT, VD], BF16) for _ in range(2)]
    PT_buf = [vb.alloc(128, [512], BF16) for _ in range(2)]
    sq_s = vb.alloc(128, [S // 8], BF16)
    sq_q = vb.alloc(128, [Q // 8], BF16)
    sq_krs = vb.alloc(RD, [S // 8], BF16)
    sq_qrs = vb.alloc(RD, [Q // 8], BF16)
    rsumc_f = vb.alloc(128, [Q // 128], F32)
    rsumc = vb.alloc(128, [Q // 128], BF16)
    bcast_sb = vb.alloc(128, [Q], F32)
    if KT * RD * 8 >= Q * 6:
        srow_f = view(ZLOW, 1, [Q], F32)
        rsum_b = view(ZLOW + Q * 4, 1, [Q], BF16)
    else:
        srow_f = vb.alloc(1, [Q], F32)
        rsum_b = vb.alloc(1, [Q], BF16)
    negB2 = [vb.alloc(128, [1], F32) for _ in range(2)]
    negB2b = [vb.alloc(1, [2], BF16) for _ in range(2)]
    out_all = vb.alloc(128, [NH, Q], BF16)
    assert vb.off <= PB, (vb.off, PB)

    def b_preamble(blk):
        @blk.scalar
        def _(s):
            s = LastW(s)
            S_.wait(s, tok[("kcT_evac", KT - 1)])
            S_.wait(s, tok[("qevac", nqb - 1, QT - 1)])
            s.activation(out=sq_krs[:], in_=k_ropeT[:, 0:S:8], func=AF.Square)
            tok["sq_krs"] = sigS.inc(s.last)

        @blk.gpsimd
        def _(g):
            g = LastW(g)
            S_.wait(g, tok[("qevac", nqb - 1, QT - 1)])
            for pp in range(min(2, NP)):
                tok[("wqb", pp)] = dW[pp].inc(
                    g.dma_start(out=wqb_buf[pp][:], in_=wqb_d[pp, :, :, :]), 16)
                tok[("wkvb", pp)] = dW2[pp].inc(
                    g.dma_start(out=wkvb_buf[pp][:], in_=wkvb_d[pp, :, :, :]), 16)

    NUQ = Q // 512 if Q >= 512 else 1
    UQW = Q // NUQ
    units = [(kt, c) for kt in range(KT) for c in range(NUQ)]
    NKC = max(1, S // 512)
    KCW = S // NKC
    NVG = max(1, KT // 4)
    VGW = KT // NVG

    def b_prefetch(blk, p1):
        @blk.gpsimd
        def _(g, p1=p1):
            g = LastW(g)
            S_.wait(g, tok[("prep_mm", 2 * (p1 - 2), 3 * NUQ - 1)])
            S_.wait(g, tok[("v_mm", 2 * p1 - 3, NVG - 1)])
            tok[("wqb", p1)] = dW[p1 % 2].inc(
                g.dma_start(out=wqb_buf[p1 % 2][:], in_=wqb_d[p1, :, :, :]), 16)
            tok[("wkvb", p1)] = dW2[p1 % 2].inc(
                g.dma_start(out=wkvb_buf[p1 % 2][:], in_=wkvb_d[p1, :, :, :]), 16)

    def b_prep_a(blk, h):
        p, j = h // 2, h % 2
        par = p % 2
        hpar = h % 2
        ppar = p % 2
        sbase = sigS.count
        if j == 0:
            for u in range(3 * NUQ):
                tok[("prep_ev", 2 * p, u)] = (sigS, sbase + u + 1)
            sbase += 3 * NUQ
        for c in range(NKC):
            tok[("knT_ev", h, c)] = (sigS, sbase + c + 1)

        if j == 0:
            for jj in (0, 1):
                for c in range(NUQ):
                    u = jj * NUQ + c

                    @blk.tensor
                    def _(t, p=p, par=par, jj=jj, c=c, u=u):
                        t = LastW(t)
                        S_.wait(t, tok[("wqb", p)])
                        S_.wait(t, tok[("qcT_evac", QT - 1)])
                        if u >= 2:
                            S_.wait(t, tok[("prep_ev", 2 * p, u - 2)])
                        S_.wait(t, tok.get(("prep_done", 2 * p - 1)))
                        for k in range(QRK):
                            t.matmul(PEp[u % 2][:, 0:UQW],
                                     wqb_buf[par][:, k, jj * ND:jj * ND + ND],
                                     q_cT[:, k, c * UQW:(c + 1) * UQW],
                                     start=(k == 0), stop=(k == QRK - 1))
                        tok[("prep_mm", 2 * p, u)] = sigT.inc(t.last)
            for c in range(NUQ):
                u = 2 * NUQ + c

                @blk.tensor
                def _(t, p=p, par=par, c=c, u=u):
                    t = LastW(t)
                    S_.wait(t, tok[("prep_ev", 2 * p, u - 2)])
                    for k in range(QRK):
                        t.matmul(PEp[u % 2][:, 0:UQW],
                                 wqb_buf[par][:, k, 2 * ND:2 * ND + 2 * RD],
                                 q_cT[:, k, c * UQW:(c + 1) * UQW],
                                 start=(k == 0), stop=(k == QRK - 1))
                    tok[("prep_mm", 2 * p, u)] = sigT.inc(t.last)

        for c in range(NKC):
            @blk.tensor
            def _(t, h=h, p=p, j=j, par=par, c=c):
                t = LastW(t)
                S_.wait(t, tok[("wkvb", p)])
                if c >= 2:
                    S_.wait(t, tok[("knT_ev", h, c - 2)])
                elif j == 0:
                    lastu = max(u for u in range(3 * NUQ) if u % 2 == c % 2)
                    S_.wait(t, tok[("prep_ev", 2 * p, lastu)])
                if c == 0 and j == 1:
                    S_.wait(t, tok.get(("prep_done", h - 1)))
                koff = j * (ND + VD)
                for k in range(KVK):
                    t.matmul(PEp[c % 2][:, 0:KCW],
                             wkvb_buf[par][:, k, koff:koff + ND],
                             k_cT[:, k, c * KCW:(c + 1) * KCW],
                             start=(k == 0), stop=(k == KVK - 1))
                tok[("knT_mm", h, c)] = sigT.inc(t.last)


    def b_prep_b1(blk, h):
        p, j = h // 2, h % 2
        par = p % 2
        hpar = h % 2
        ppar = p % 2
        for g_ in range(NVG):
            @blk.tensor
            def _(t, h=h, p=p, j=j, par=par, c=g_):
                t = LastW(t)
                if c >= 2:
                    S_.wait(t, tok[("v_ev", h, c - 2)])
                elif c == 0:
                    S_.wait(t, tok[("knT_ev", h, NKC - 1)])
                koff = j * (ND + VD)
                for c_ in range(VGW):
                    kt = c * VGW + c_
                    for k in range(KVK):
                        t.matmul(PEp[c % 2][:, c_ * 128:(c_ + 1) * 128],
                                 k_cT[:, k, kt * 128:(kt + 1) * 128],
                                 wkvb_buf[par][:, k, koff + ND:koff + ND + VD],
                                 start=(k == 0), stop=(k == KVK - 1))
                tok[("v_mm", h, c)] = sigT.inc(t.last)

            @blk.vector
            def _(v, h=h, hpar=hpar, c=g_):
                v = LastW(v)
                S_.wait(v, tok[("v_mm", h, c)])
                v.tensor_copy(
                    out=v_sb[hpar].rearrange("p t d -> p (t d)")[
                        :, c * VGW * VD:(c + 1) * VGW * VD],
                    in_=PEp[c % 2][:, 0:VGW * 128])
                tok[("v_ev", h, c)] = sigV.inc(v.last)

        if j == 0:
            @blk.vector
            def _(v, p=p, ppar=ppar):
                v = LastW(v)
                S_.wait(v, tok[("prep_ev", 2 * p, 3 * NUQ - 1)])
                v.tensor_scalar(out=rot_b[0:HLF, :], in0=ropes_b[HLF:RD, :],
                                scalar1=-1.0, scalar2=None, op0=ALU.mult)
                v.tensor_copy(out=rot_b[HLF:RD, :], in_=ropes_b[0:HLF, :])
                t = sigV.inc(v.last)
                S_.wait(v, t)
                v.tensor_tensor(out=ropet1[:], in0=ropes_b[:], in1=cosq2_b[:],
                                op=ALU.mult)
                v.tensor_tensor(out=rot_b[:], in0=rot_b[:], in1=sinq2_b[:],
                                op=ALU.mult)
                t = sigV.inc(v.last)
                S_.wait(v, t)
                v.tensor_tensor(out=q_rT[ppar][:], in0=ropet1[:],
                                in1=rot_b[:], op=ALU.add)
                tok[("rope_rdy", p)] = sigV.inc(v.last)
                tok[("rope_used", p)] = tok[("rope_rdy", p)]

    def b_prep_b2(blk, h):
        p, j = h // 2, h % 2
        par = p % 2
        hpar = h % 2
        ppar = p % 2
        @blk.scalar
        def _(s, h=h, p=p, j=j, hpar=hpar, ppar=ppar):
            s = LastW(s)
            if j == 0:
                for jj in (0, 1):
                    for c in range(NUQ):
                        u = jj * NUQ + c
                        S_.wait(s, tok[("prep_mm", 2 * p, u)])
                        s.copy(out=q_nT[ppar][jj][:, c * UQW:(c + 1) * UQW],
                               in_=PEp[u % 2][:, 0:UQW])
                        tk = sigS.inc(s.last)
                        assert tk == tok[("prep_ev", 2 * p, u)]
                for c in range(NUQ):
                    u = 2 * NUQ + c
                    S_.wait(s, tok[("prep_mm", 2 * p, u)])
                    S_.wait(s, tok.get(("rope_used", p - 1)))
                    s.copy(out=ropes_b[:, c * UQW:(c + 1) * UQW],
                           in_=PEp[u % 2][0:RD, 0:UQW])
                    s.copy(out=ropes_b[:, Q + c * UQW:Q + (c + 1) * UQW],
                           in_=PEp[u % 2][RD:2 * RD, 0:UQW])
                    tk = sigS.inc(s.last)
                    assert tk == tok[("prep_ev", 2 * p, u)]
            for c in range(NKC):
                S_.wait(s, tok[("knT_mm", h, c)])
                s.copy(out=k_nT[hpar][:, c * KCW:(c + 1) * KCW],
                       in_=PEp[c % 2][:, 0:KCW])
                tk = sigS.inc(s.last)
                assert tk == tok[("knT_ev", h, c)]
            S_.wait(s, tok[("knT_ev", h, NKC - 1)])
            S_.wait(s, tok.get(("guard_mm", h - 1)))
            s.activation(out=sq_s[:], in_=k_nT[hpar][:, 0:S:8],
                         func=AF.Square)
            S_.wait(s, tok[("prep_ev", 2 * p, (1 + j) * NUQ - 1)])
            S_.wait(s, tok[("rope_rdy", p)])
            s.activation(out=sq_q[:], in_=q_nT[p % 2][j][:, 0:Q:8],
                         func=AF.Square)
            s.activation(out=sq_qrs[:],
                         in_=q_rT[p % 2][:, j * Q:(j + 1) * Q:8],
                         func=AF.Square)
            tok[("sqk", h)] = sigS.inc(s.last)

        @blk.tensor
        def _(t, h=h):
            t = LastW(t)
            S_.wait(t, tok[("sqk", h)])
            S_.wait(t, tok["sq_krs"])
            S_.wait(t, tok[("v_ev", h, NVG - 1)])
            t.matmul(PEp[0][0:1, 0:S // 8], ones_col[:], sq_s[:],
                     start=True, stop=False)
            t.matmul(PEp[0][0:1, 0:S // 8], ones_col[0:RD, :], sq_krs[:],
                     start=False, stop=True)
            t.matmul(PEp[1][0:1, 0:Q // 8], ones_col[:], sq_q[:],
                     start=True, stop=False)
            t.matmul(PEp[1][0:1, 0:Q // 8], ones_col[0:RD, :], sq_qrs[:],
                     start=False, stop=True)
            tok[("guard_mm", h)] = sigT.inc(t.last)

        @blk.vector
        def _(v, h=h):
            v = LastW(v)
            S_.wait(v, tok[("guard_mm", h)])
            v.reduce_max(out=gm_sb[:, 0:1], in_=PEp[0][0:1, 0:S // 8],
                         axis=AX.X)
            v.reduce_max(out=gm_sb[:, 1:2], in_=PEp[1][0:1, 0:Q // 8],
                         axis=AX.X)
            t = sigV.inc(v.last)
            S_.wait(v, t)
            v.tensor_tensor(out=gm_sb[:, 2:3], in0=gm_sb[:, 0:1],
                            in1=gm_sb[:, 1:2], op=ALU.add)
            t = sigV.inc(v.last)
            S_.wait(v, t)
            v.tensor_scalar(out=negB2b[h % 2][:, 0:1], in0=gm_sb[:, 2:3],
                            scalar1=-cscale / 2.0, scalar2=None,
                            op0=ALU.mult)
            tok[("negB", h)] = sigV.inc(v.last)

        @blk.tensor
        def _(t, h=h):
            t = LastW(t)
            S_.wait(t, tok[("negB", h)])
            t.matmul(PEp[0][:, 0:1], ones_row[:], negB2b[h % 2][:, 0:1],
                     start=True, stop=True)
            tok[("negB_mm", h)] = sigT.inc(t.last)

        @blk.vector
        def _(v, h=h):
            v = LastW(v)
            S_.wait(v, tok[("negB_mm", h)])
            v.tensor_copy(out=negB2[h % 2][:], in_=PEp[0][:, 0:1])
            t = sigV.inc(v.last)
            tok[("negB_col", h)] = t
            tok[("prep_done", h)] = t


    def b_prep(blk, h):
        b_prep_a(blk, h)
        b_prep_b1(blk, h)
        b_prep_b2(blk, h)

    def kt_main(blk, h):
        p, j = h // 2, h % 2
        hpar = h % 2
        qroff = j * Q
        sS0 = sigS.count

        @blk.tensor
        def _(t, h=h, p=p, j=j, hpar=hpar, qroff=qroff, sS0=sS0):
            t = LastW(t)
            S_.wait(t, tok[("knT_ev", h, NKC - 1)])
            S_.wait(t, tok[("negB_col", h)])
            S_.wait(t, tok[("v_ev", h, NVG - 1)])
            for ui, (kt, c) in enumerate(units):
                if ui == 1:
                    S_.wait(t, tok.get(("out_evac", h - 1)))
                if ui >= 2:
                    S_.wait(t, (sigS, sS0 + ui - 1))
                t.matmul(sPP[ui % 2][:, 0:UQW],
                         k_nT[hpar][:, kt * 128:(kt + 1) * 128],
                         q_nT[p % 2][j][:, c * UQW:(c + 1) * UQW],
                         start=True, stop=False)
                t.matmul(sPP[ui % 2][:, 0:UQW],
                         k_ropeT[:, kt * 128:(kt + 1) * 128],
                         q_rT[p % 2][:, qroff + c * UQW:qroff + (c + 1) * UQW],
                         start=False, stop=True)
                tok[("sT_mm", h, ui)] = sigT.inc(t.last)
                if ui >= 1:
                    up = ui - 1
                    ktp, cp = units[up]
                    S_.wait(t, (sigS, sS0 + up + 1))
                    t.matmul(PDq[0:1, cp * UQW:(cp + 1) * UQW], ones_col[:],
                             PT_buf[up % 2][:, 0:UQW],
                             start=(ktp == 0), stop=(ktp == KT - 1))
                    t.matmul(PCq[:, cp * UQW:(cp + 1) * UQW],
                             v_sb[hpar][:, ktp, :], PT_buf[up % 2][:, 0:UQW],
                             start=(ktp == 0), stop=(ktp == KT - 1))
            up = len(units) - 1
            ktp, cp = units[up]
            S_.wait(t, (sigS, sS0 + up + 1))
            t.matmul(PDq[0:1, cp * UQW:(cp + 1) * UQW], ones_col[:],
                     PT_buf[up % 2][:, 0:UQW],
                     start=(ktp == 0), stop=True)
            t.matmul(PCq[:, cp * UQW:(cp + 1) * UQW],
                     v_sb[hpar][:, ktp, :], PT_buf[up % 2][:, 0:UQW],
                     start=(ktp == 0), stop=True)
            tok[("pv_done", h)] = sigT.inc(t.last)
            tok[("sums_done", h)] = tok[("pv_done", h)]

        @blk.scalar
        def _(s, h=h, sS0=sS0):
            s = LastW(s)
            for ui in range(len(units)):
                S_.wait(s, tok[("sT_mm", h, ui)])
                s.activation(out=PT_buf[ui % 2][:, 0:UQW],
                             in_=sPP[ui % 2][:, 0:UQW],
                             func=AF.Exp, bias=negB2[h % 2][:], scale=cscale)
                tk = sigS.inc(s.last)
                tok[("exp", h, ui)] = tk
                assert tk[1] == sS0 + ui + 1

    def kt_tail_v1(blk, h):
        @blk.vector
        def _(v, h=h):
            v = LastW(v)
            S_.wait(v, tok[("sums_done", h)])
            v.tensor_copy(out=srow_f[:], in_=PDq[0:1, 0:Q])
            tok[("srow", h)] = sigV.inc(v.last)

    def kt_tail_t1(blk, h):
        @blk.tensor
        def _(t, h=h):
            t = LastW(t)
            S_.wait(t, tok[("srow", h)])
            S_.wait(t, tok["consts_f"])
            for qb in range(Q // 128):
                t.transpose(PDq[:, qb:qb + 1],
                            srow_f[:, qb * 128:(qb + 1) * 128],
                            ident_f[0:1, 0:1])
            tok[("scol_mm", h)] = sigT.inc(t.last)

    def kt_tail_v2(blk, h):
        @blk.vector
        def _(v, h=h):
            v = LastW(v)
            S_.wait(v, tok[("scol_mm", h)])
            v.reciprocal(out=rsumc_f[:], in_=PDq[:, 0:Q // 128])
            tok[("rsum_done", h)] = sigV.inc(v.last)

    def kt_tail_t2(blk, h):
        @blk.tensor
        def _(t, h=h):
            t = LastW(t)
            S_.wait(t, tok[("rsum_done", h)])
            for qb in range(Q // 128):
                t.transpose(PDq[0:1, qb * 128:(qb + 1) * 128],
                            rsumc_f[:, qb:qb + 1], ident_f[:])
            tok[("rsumT_mm", h)] = sigT.inc(t.last)

    def kt_tail_v3(blk, h):
        @blk.vector
        def _(v, h=h):
            v = LastW(v)
            S_.wait(v, tok[("rsumT_mm", h)])
            v.tensor_copy(out=rsum_b[:], in_=PDq[0:1, 0:Q])
            tok[("rsum8_done", h)] = sigV.inc(v.last)

    def kt_tail_t3(blk, h):
        @blk.tensor
        def _(t, h=h):
            t = LastW(t)
            S_.wait(t, tok[("rsum8_done", h)])
            for qb in range(Q // 128):
                t.matmul(PDq[:, qb * 128:(qb + 1) * 128], ones_row[:],
                         rsum_b[:, qb * 128:(qb + 1) * 128],
                         start=True, stop=True)
            tok[("bcast_mm", h)] = sigT.inc(t.last)

    def kt_tail_v4(blk, h):
        @blk.vector
        def _(v, h=h):
            v = LastW(v)
            S_.wait(v, tok[("bcast_mm", h)])
            v.tensor_copy(out=bcast_sb[:], in_=PDq[:, 0:Q])
            t = sigV.inc(v.last)
            S_.wait(v, t)
            v.tensor_tensor(out=out_all[:, h, :], in0=PCq[:, 0:Q],
                            in1=bcast_sb[:], op=ALU.mult)
            tok[("out_evac", h)] = sigV.inc(v.last)
            tok[("bcast_used", h)] = tok[("out_evac", h)]

    def kt_tail(blk, h):
        kt_tail_v1(blk, h)
        kt_tail_t1(blk, h)
        kt_tail_v2(blk, h)
        kt_tail_t2(blk, h)
        kt_tail_v3(blk, h)
        kt_tail_t3(blk, h)
        kt_tail_v4(blk, h)

    _o, _n = offreg[id(ln1T_own)]
    va2 = Cursor((_o + _n + 31) // 32 * 32, PB)
    QBW2 = QR // nqb
    wqa_buf = [va2.alloc(128, [DK, QBW2], BF16) for _ in range(2)]
    q_raw = va2.alloc(128, [QT, QR], BF16)
    qc_tmp = va2.alloc(128, [QR], BF16)
    psA2 = ExitStack()
    qP = [psA2.enter_context(nc.psum_tensor(f"qP{i}", [128, 512], F32))
          for i in range(2)]

    lastA1 = [("kcT_evac", KT - 1), ("kv_mm", KT - 1), ("lnT_done", KT - 1),
              ("lnapply", KT - 1)]

    with nc.Block() as blk:
        @blk.gpsimd
        def _(g):
            g = LastW(g)
            for sb in range(min(2, nqb)):
                S_.wait_many(g, [tok[k] for k in lastA1 if k in tok])
                tok[("wqa", sb)] = dW[sb % 2].inc(
                    g.dma_start(out=wqa_buf[sb % 2][:], in_=wqa_d[sb, :, :, :]), 16)

    with nc.Block() as blk:
      for sb in range(nqb):
        par = sb % 2
        if True:
            eS0 = sigS.count
            for ti in range(QT):
                tok[("qevac", sb, ti)] = (sigS, eS0 + ti + 1)

            @blk.tensor
            def _(t, sb=sb, par=par):
                t = LastW(t)
                S_.wait(t, tok[("wqa", sb)])
                for ti in range(QT):
                    if ti >= 2:
                        S_.wait(t, tok[("qevac", sb, ti - 2)])
                    S_.wait(t, tok.get(("qevac", sb - 1, QT - 1)))
                    S_.wait(t, tok.get(("qevac", sb - 1, QT - 2)))
                    for k in range(DK):
                        t.matmul(qP[ti % 2][:, 0:QBW2],
                                 ln1T_own[:, ti, k * 128:(k + 1) * 128],
                                 wqa_buf[par][:, k, :],
                                 start=(k == 0), stop=(k == DK - 1))
                    tok[("qmm", sb, ti)] = sigT.inc(t.last)

            @blk.scalar
            def _(s, sb=sb):
                s = LastW(s)
                for ti in range(QT):
                    S_.wait(s, tok[("qmm", sb, ti)])
                    if sb == 0 and ti == 0:
                        S_.wait_many(s, [tok[k] for k in lastA1 if k in tok])
                    s.copy(out=q_raw[:, ti, sb * QBW2:(sb + 1) * QBW2],
                           in_=qP[ti % 2][:, 0:QBW2])
                    tk = sigS.inc(s.last)
                    assert tk == tok[("qevac", sb, ti)]

            @blk.gpsimd
            def _(g, sb=sb, par=par):
                g = LastW(g)
                if sb + 2 < nqb:
                    S_.wait(g, tok[("qmm", sb, QT - 1)])
                    tok[("wqa", sb + 2)] = dW[par].inc(
                        g.dma_start(out=wqa_buf[par][:],
                                    in_=wqa_d[sb + 2, :, :, :]), 16)

      for ti in range(QT):
        par = ti % 2
        if True:
            @blk.vector
            def _(v, ti=ti):
                v = LastW(v)
                for sb in range(nqb):
                    S_.wait(v, tok[("qevac", sb, ti)])
                px = ti % 2
                nch = max(1, QR // 512)
                cw = QR // nch
                for c_ in range(nch):
                    v.bn_stats(out=stats[px][:, 6 * c_:6 * c_ + 6],
                               in_=q_raw[:, ti, c_ * cw:(c_ + 1) * cw])
                t = sigV.inc(v.last)
                S_.wait(v, t)
                v.bn_aggr(out=mv[px][:], in_=stats[px][:, 0:6 * nch])
                t = sigV.inc(v.last)
                S_.wait(v, t)
                v.tensor_scalar(out=col_c[px][:], in0=mv[px][:, 0:1], scalar1=mv[px][:, 0:1],
                                scalar2=None, op0=ALU.mult)
                t = sigV.inc(v.last)
                S_.wait(v, t)
                v.tensor_tensor(out=col_c[px][:], in0=col_c[px][:], in1=mv[px][:, 1:2],
                                op=ALU.add)
                tok[("q_ms", ti)] = sigV.inc(v.last)

            @blk.vector
            def _(v, ti=ti):
                v = LastW(v)
                S_.wait(v, tok[("q_ms", ti)])
                v_rsqrt(v, ti % 2, col_c[ti % 2][:], 1e-30)
                tok[("q_sq", ti)] = sigV.inc(v.last)

            @blk.vector
            def _(v, ti=ti):
                v = LastW(v)
                S_.wait(v, tok[("q_sq", ti)])
                px = ti % 2
                v.tensor_tensor(out=col_a[px][:], in0=col_a[px][:],
                                in1=col_c[px][:], op=ALU.mult)
                t = sigV.inc(v.last)
                S_.wait(v, t)
                v.tensor_scalar(out=col_a[px][:], in0=col_a[px][:], scalar1=RMS_EPS,
                                scalar2=None, op0=ALU.add)
                t = sigV.inc(v.last)
                S_.wait(v, t)
                v.reciprocal(out=col_a[px][:], in_=col_a[px][:])
                t = sigV.inc(v.last)
                S_.wait(v, t)
                S_.wait(v, tok.get(("qcT_mm", ti - 1)))
                v.tensor_scalar(out=qc_tmp[:], in0=q_raw[:, ti, :],
                                scalar1=col_a[px][:], scalar2=None, op0=ALU.mult)
                tok[("qc_rdy", ti)] = sigV.inc(v.last)

            @blk.tensor
            def _(t, ti=ti, par=par):
                t = LastW(t)
                S_.wait(t, tok[("qc_rdy", ti)])
                S_.wait(t, tok.get(("qcT_evac", ti - 2)))
                if ti == 0:
                    S_.wait(t, tok[("lnapply", KT - 1)])
                for k in range(QRK):
                    t.transpose(trP[par][:, k * 128:(k + 1) * 128],
                                qc_tmp[:, k * 128:(k + 1) * 128], ident_b[:])
                tok[("qcT_mm", ti)] = sigT.inc(t.last)

            @blk.vector
            def _(v, ti=ti, par=par):
                v = LastW(v)
                S_.wait(v, tok[("qcT_mm", ti)])
                S_.wait(v, tok["consts_f"])
                for k in range(QRK):
                    v.tensor_scalar(out=q_cT[:, k, ti * 128:(ti + 1) * 128],
                                    in0=trP[par][:, k * 128:(k + 1) * 128],
                                    scalar1=gqa_b[:, k:k + 1], scalar2=None,
                                    op0=ALU.mult)
                tok[("qcT_evac", ti)] = sigV.inc(v.last)

      b_overlap_ok = False

    if dbg:
        dbg_kcT = nc.dram_tensor("dbg_kcT", [128, KVK * S], BF16, kind="ExternalOutput")
        dbg_krT = nc.dram_tensor("dbg_krT", [RD, S], BF16, kind="ExternalOutput")
        dbg_qcT = nc.dram_tensor("dbg_qcT", [128, QRK * Q], BF16, kind="ExternalOutput")
        with nc.Block() as blk:
            @blk.sync
            def _(sy):
                S_.wait(sy, tok[("kcT_evac", KT - 1)])
                S_.wait(sy, tok[("qcT_evac", QT - 1)])
                sy.dma_start(out=dbg_kcT[:], in_=k_cT.rearrange("p a b -> p (a b)")).then_inc(dO[0].sem, 16)
                sy.dma_start(out=dbg_krT[:], in_=k_ropeT[:]).then_inc(dO[0].sem, 16)
                sy.dma_start(out=dbg_qcT[:], in_=q_cT.rearrange("p a b -> p (a b)")).then_inc(dO[0].sem, 16)
                dO[0].count += 48
                sy.wait_ge(dO[0].sem, dO[0].count)
    psA2.close()
    psA.close()

    psB = ExitStack()
    sPP = [psB.enter_context(nc.psum_tensor(f"sPP{i}", [128, 512], F32))
           for i in range(2)]
    PCq = psB.enter_context(nc.psum_tensor("PCq", [128, 1024], F32))
    PDq = psB.enter_context(nc.psum_tensor("PDq", [128, 1024], F32))
    PEp = [psB.enter_context(nc.psum_tensor(f"PEp{i}", [128, 512], F32))
           for i in range(2)]
    with nc.Block() as blk:
        b_preamble(blk)
        b_prep(blk, 0)
    for h in range(NH):
        with nc.Block() as blk:
            if h % 2 == 1 and (h + 3) // 2 < NP:
                b_prefetch(blk, (h + 3) // 2)
            if h >= 1:
                kt_tail_v1(blk, h - 1)
            if h + 1 < NH:
                b_prep_a(blk, h + 1)
            if h >= 1:
                kt_tail_t1(blk, h - 1)
                kt_tail_v2(blk, h - 1)
            if h + 1 < NH:
                b_prep_b1(blk, h + 1)
            if h >= 1:
                kt_tail_t2(blk, h - 1)
                kt_tail_v3(blk, h - 1)
                kt_tail_t3(blk, h - 1)
                kt_tail_v4(blk, h - 1)
            if h + 1 < NH:
                b_prep_b2(blk, h + 1)
            kt_main(blk, h)
    with nc.Block() as blk:
        kt_tail(blk, NH - 1)

    if dbg:
        dbg_oall = nc.dram_tensor("dbg_oall", [128, NH * Q], BF16, kind="ExternalOutput")
        with nc.Block() as blk:
            @blk.sync
            def _(sy):
                S_.wait(sy, tok[("out_evac", NH - 1)])
                sy.dma_start(out=dbg_oall[:], in_=out_all.rearrange("p a b -> p (a b)")).then_inc(dO[0].sem, 16)
                dO[0].count += 16
                sy.wait_ge(dO[0].sem, dO[0].count)
    psB.close()

    x1_sb = view(V0, 128, [QT, D], BF16)
    _oa0, _oan = offreg[id(out_all)]
    zc = Cursor(ZLOW, V0)
    zgap = Cursor(V0 + QT * D * 2, _oa0)

    def calloc(part, shape, dt):
        esz = 2 if dt == BF16 else 4
        n = 1
        for s in shape:
            n *= s
        if zc.off + n * esz <= zc.limit:
            return zc.alloc(part, shape, dt)
        return zgap.alloc(part, shape, dt)

    wo_buf = [calloc(128, [KHN, OBW], BF16) for _ in range(2)]
    xr_buf0 = calloc(128, [D], F32)
    xr_buf = [xr_buf0, xr_buf0]
    ln3st = calloc(128, [QT, 6 * NOB], F32)
    mvv3 = calloc(128, [QT, 2], F32)
    rstd8 = calloc(128, [QT], F32)
    negmb8 = calloc(128, [QT], F32)
    rs8x = calloc(128, [QT], F32)
    rs8t = calloc(128, [QT], F32)
    magic8 = calloc(128, [QT], F32)
    t1_buf = calloc(128, [D], BF16)
    lastB = [("out_evac", NH - 1), ("bcast_used", NH - 1)]

    psC = ExitStack()
    oP = [psC.enter_context(nc.psum_tensor(f"oP{i}", [128, OBW], F32))
          for i in range(QT)]

    cpairs = [(b, kh) for b in range(NOB) for kh in range(KH)]

    woB = [dW[0].count, dW[1].count]
    for pi in range(len(cpairs)):
        tok[("wo", pi)] = (dW[pi % 2], woB[pi % 2] + 16 * (pi // 2 + 1))
    cS0 = sigS.count
    for b in range(NOB):
        for ti in range(QT):
            tok[("o_evac", b, ti)] = (sigS, cS0 + b * QT + ti + 1)
    aS0 = cS0 + NOB * QT
    for ti in range(QT):
        tok[("ln3ap", ti)] = (sigS, aS0 + ti + 1)
    xrB = [dX[0].count, dX[1].count]
    for ti in range(QT):
        tok[("xr", ti)] = (dX[ti % 2], xrB[ti % 2] + 16 * (ti // 2 + 1))

    with nc.Block() as blk:
        @blk.tensor
        def _(t):
            t = LastW(t)
            for pi, (b, kh) in enumerate(cpairs):
                S_.wait(t, tok[("wo", pi)])
                if kh == 0:
                    for ti in range(QT):
                        S_.wait(t, tok.get(("o_evac", b - 1, ti)))
                for k2 in range(KHN):
                    k = kh * KHN + k2
                    for ti in range(QT):
                        t.matmul(oP[ti][:],
                                 out_all[:, k, ti * 128:(ti + 1) * 128],
                                 wo_buf[pi % 2][:, k2, :],
                                 start=(k == 0), stop=(k == NH - 1))
                tok[("o_mmp", pi)] = sigT.inc(t.last)
                if kh == KH - 1:
                    tok[("o_mm", b)] = tok[("o_mmp", pi)]

        @blk.vector
        def _(v):
            v = LastW(v)
            v.memset(magic8[:].bitcast(mybir.dt.int32), 0x5F3759DF)
            for b in range(NOB):
                for ti in range(QT):
                    S_.wait(v, tok[("o_evac", b, ti)])
                    v.bn_stats(out=ln3st[:, ti, 6 * b:6 * b + 6],
                               in_=x1_sb[:, ti, b * OBW:(b + 1) * OBW])
            t = sigV.inc(v.last)
            S_.wait(v, t)
            for ti in range(QT):
                v.bn_aggr(out=mvv3[:, ti, :], in_=ln3st[:, ti, :])
            t = sigV.inc(v.last)
            S_.wait(v, t)
            mvf = mvv3.rearrange("p a b -> p (a b)")
            v_rsqrtN(v, mvf[:, 1:2 * QT:2], rstd8[:], rs8x[:], rs8t[:],
                     magic8[:], LN_EPS)
            t = sigV.inc(v.last)
            S_.wait(v, t)
            v.tensor_tensor(out=negmb8[:], in0=mvf[:, 0:2 * QT:2],
                            in1=rstd8[:], op=ALU.mult)
            t = sigV.inc(v.last)
            S_.wait(v, t)
            v.tensor_scalar(out=negmb8[:], in0=negmb8[:], scalar1=-1.0,
                            scalar2=None, op0=ALU.mult)
            tok["ln3rdy"] = sigV.inc(v.last)
            for ti in range(QT):
                par = ti % 2
                S_.wait(v, tok[("ln3ap", ti)])
                S_.wait(v, tok[("xr", ti)])
                v.tensor_tensor(out=x1_sb[:, ti, :], in0=t1_buf[:],
                                in1=xr_buf[par][:], op=ALU.add)
                tok[("x1_done", ti)] = sigV.inc(v.last)

        @blk.scalar
        def _(s):
            s = LastW(s)
            for b in range(NOB):
                S_.wait(s, tok[("o_mm", b)])
                if b == 0:
                    S_.wait_many(s, [tok[k] for k in lastB])
                for ti in range(QT):
                    s.copy(out=x1_sb[:, ti, b * OBW:(b + 1) * OBW],
                           in_=oP[ti][:])
                    tk = sigS.inc(s.last)
                    assert tk == tok[("o_evac", b, ti)]
            for ti in range(QT):
                S_.wait(s, tok["ln3rdy"])
                S_.wait(s, tok.get(("x1_done", ti - 1)))
                s.activation(out=t1_buf[:], in_=x1_sb[:, ti, :],
                             func=AF.Identity,
                             scale=rstd8[:, ti:ti + 1],
                             bias=negmb8[:, ti:ti + 1])
                tk = sigS.inc(s.last)
                assert tk == tok[("ln3ap", ti)]

        @blk.gpsimd
        def _(g):
            g = LastW(g)
            S_.wait_many(g, [tok[k] for k in lastB])
            for pi, (b, kh) in enumerate(cpairs):
                if pi >= 2:
                    S_.wait(g, tok[("o_mmp", pi - 2)])
                tk = dW[pi % 2].inc(
                    g.dma_start(out=wo_buf[pi % 2][:],
                                in_=wo_d[b, :, kh * KHN:(kh + 1) * KHN, :]), 16)
                assert tk == tok[("wo", pi)]

        @blk.sync
        def _(sy):
            sy = LastW(sy)
            S_.wait_many(sy, [tok[k] for k in lastB])
            for ti in range(QT):
                S_.wait(sy, tok.get(("x1_done", ti - 1)))
                tk = dX[ti % 2].inc(
                    sy.dma_start(out=xr_buf[ti % 2][:], in_=xv[ti, :, :]), 16)
                assert tk == tok[("xr", ti)]

    if dbg:
        dbg_o = nc.dram_tensor("dbg_o", [128, QT * D], BF16, kind="ExternalOutput")
        dbg_x1 = nc.dram_tensor("dbg_x1", [128, QT * D], BF16, kind="ExternalOutput")
        with nc.Block() as blk:
            @blk.sync
            def _(sy):
                S_.wait(sy, tok[("x1_done", QT - 1)])
                sy.dma_start(out=dbg_x1[:], in_=x1_sb.rearrange("p a b -> p (a b)")).then_inc(dO[0].sem, 16)
                dO[0].count += 16
                sy.wait_ge(dO[0].sem, dO[0].count)
    psC.close()

    zd = Cursor(ZLOW, V0)
    zd2 = Cursor(V0 + QT * D * 2, PB)

    def dalloc(part, shape, dt):
        esz = 2 if dt == BF16 else 4
        n = 1
        for s in shape:
            n *= s
        if zd.off + n * esz <= zd.limit:
            return zd.alloc(part, shape, dt)
        return zd2.alloc(part, shape, dt)

    ln2T_g = dalloc(128, [DK, GT * 128], BF16)
    at_b = dalloc(128, [DK, ER], BF16)
    bflat_b = dalloc(ER, [DO], BF16)
    wrt_b = dalloc(128, [DK, NE], BF16)
    bbase_b = dalloc(1, [DO], BF16)
    lg_sb = dalloc(128, [NE], F32)
    m1_sb = dalloc(128, [1], F32)
    m2_sb = dalloc(128, [1], F32)
    msk_sb = dalloc(128, [NE], F32)
    expl_sb = dalloc(128, [NE], F32)
    comb_g = dalloc(128, [GT, NE], BF16)
    combT_sb = dalloc(NE, [GT * 128], BF16)
    gate_sb = dalloc(128, [GT * 128], BF16)
    hg_sb = dalloc(128, [GT * 128], BF16)
    wbt_buf = [zd2.alloc(128, [DKHN, FBW], BF16) for _ in range(2)]
    ffn_g = zd2.alloc(128, [GT, DO], BF16)
    ln2_tmp = dalloc(128, [D], BF16)
    ln4_tmp = dalloc(128, [DO], BF16)
    och_buf = [dalloc(128, [min(D, 512)], F32) for _ in range(2)]
    ln4st = dalloc(128, [GT, 6 * FOB], F32)
    mvv4 = dalloc(128, [GT, 2], F32)
    rstd4 = dalloc(128, [GT], F32)
    negmb4 = dalloc(128, [GT], F32)
    rs4x = dalloc(128, [GT], F32)
    rs4t = dalloc(128, [GT], F32)
    magic4 = dalloc(128, [GT], F32)
    rstd2g = dalloc(128, [QT], F32)
    negmb2g = dalloc(128, [QT], F32)
    ln2mv = dalloc(128, [2], F32)
    rs2x = dalloc(128, [1], F32)
    rs2t = dalloc(128, [1], F32)
    magic2 = dalloc(128, [1], F32)
    assert zd2.off <= PB

    def d_ln2_chain(blk, ti):
        @blk.vector
        def _(v, ti=ti):
            v = LastW(v)
            S_.wait(v, tok[("x1_done", ti)])
            v.memset(magic2[:].bitcast(mybir.dt.int32), 0x5F3759DF)
            nch = max(1, D // 512)
            for c_ in range(nch):
                v.bn_stats(out=stats[0][:, 6 * c_:6 * c_ + 6],
                           in_=x1_sb[:, ti, c_ * 512:(c_ + 1) * 512])
            t = sigV.inc(v.last)
            S_.wait(v, t)
            v.bn_aggr(out=ln2mv[:], in_=stats[0][:, 0:6 * nch])
            t = sigV.inc(v.last)
            S_.wait(v, t)
            v_rsqrtN(v, ln2mv[:, 1:2], rstd2g[:, ti:ti + 1], rs2x[:], rs2t[:],
                     magic2[:], LN_EPS)
            t = sigV.inc(v.last)
            S_.wait(v, t)
            v.tensor_tensor(out=negmb2g[:, ti:ti + 1], in0=ln2mv[:, 0:1],
                            in1=rstd2g[:, ti:ti + 1], op=ALU.mult)
            t = sigV.inc(v.last)
            S_.wait(v, t)
            v.tensor_scalar(out=negmb2g[:, ti:ti + 1], in0=negmb2g[:, ti:ti + 1],
                            scalar1=-1.0, scalar2=None, op0=ALU.mult)
            tok[("ln2fin", ti)] = sigV.inc(v.last)

    rcol_a = [gqa_b[:, 0:1], gqa_b[:, 1:2]]
    rcol_b = [gqa_b[:, 2:3], gqa_b[:, 3:4]]

    lastC = [("x1_done", QT - 1)]
    with nc.Block() as blk:
        @blk.gpsimd
        def _(g):
            g = LastW(g)
            S_.wait_many(g, [tok[k] for k in lastC])
            dW2[0].inc(g.dma_start(out=at_b[:], in_=at_d[:]), 16)
            dW2[0].inc(g.dma_start(out=bflat_b[:], in_=bflat_d[:]), 16)
            dW2[0].inc(g.dma_start(out=wrt_b[:], in_=wrt_d[:]), 16)
            tok["ffn_w"] = dW2[0].inc(
                g.dma_start(out=bbase_b[:], in_=bbase_d[:]), 16)

    ov = out_d.rearrange("(t p) d -> t p d", p=128)
    OCH = max(1, D // 512)
    ocw = D // OCH
    nw = (DK + 3) // 4
    dpairs = [(b, kh) for b in range(FOB) for kh in range(DKH)]

    def d_tail_sections(blk, grp, want_dbg):
        @blk.vector
        def _(v, grp=grp):
            v = LastW(v)
            for tg in range(GT):
                ti = grp * GT + tg
                S_.wait(v, tok[("ln4ap", ti)])
                dOb = [dO[0].count, dO[1].count]
                for c_ in range(OCH):
                    cpar = c_ % 2
                    if c_ >= 2:
                        S_.wait(v, (dO[cpar], dOb[cpar] + 16 * ((c_ - 2) // 2 + 1)))
                    else:
                        S_.wait(v, tok.get(("odma", ti - 1, OCH - 1)))
                        S_.wait(v, tok.get(("odma", ti - 1, OCH - 2)))
                    v.tensor_tensor(out=och_buf[cpar][:, 0:ocw],
                                    in0=ln4_tmp[:, c_ * ocw:(c_ + 1) * ocw],
                                    in1=x1_sb[:, ti, c_ * ocw:(c_ + 1) * ocw],
                                    op=ALU.add)
                    tok[("och", ti, c_)] = sigV.inc(v.last)
                for c_ in range(OCH):
                    tok[("odma", ti, c_)] = (dO[c_ % 2],
                                             dOb[c_ % 2] + 16 * (c_ // 2 + 1))
                    dO[c_ % 2].count += 16

        @blk.scalar
        def _(s, grp=grp):
            s = LastW(s)
            for tg in range(GT):
                ti = grp * GT + tg
                S_.wait(s, tok[("ln4rdy", grp)])
                if want_dbg:
                    S_.wait(s, tok[("dbgf", grp)])
                S_.wait(s, tok.get(("och", ti - 1, OCH - 1)))
                s.activation(out=ln4_tmp[:], in_=ffn_g[:, tg, :],
                             func=AF.Identity,
                             scale=rstd4[:, tg:tg + 1],
                             bias=negmb4[:, tg:tg + 1])
                tk = sigS.inc(s.last)
                assert tk == tok[("ln4ap", ti)], (tk, tok[("ln4ap", ti)])

        @blk.sync
        def _(sy, grp=grp):
            sy = LastW(sy)
            for tg in range(GT):
                ti = grp * GT + tg
                for c_ in range(OCH):
                    cpar = c_ % 2
                    S_.wait(sy, tok[("och", ti, c_)])
                    inst = sy.dma_start(out=ov[ti, :, c_ * ocw:(c_ + 1) * ocw],
                                        in_=och_buf[cpar][:, 0:ocw])
                    inst.then_inc(dO[cpar].sem, 16)
                    assert tok[("odma", ti, c_)][0] is dO[cpar]

    for grp in range(DGRP):
        psD = ExitStack()
        trD = [psD.enter_context(nc.psum_tensor(f"trD{grp}{i}", [128, 512], BF16))
               for i in range(2)]
        lgP = psD.enter_context(nc.psum_tensor(f"lgP{grp}", [128, NE], F32))
        hP = psD.enter_context(nc.psum_tensor(f"hP{grp}", [128, GT * 128], F32))
        gateP = psD.enter_context(nc.psum_tensor(f"gateP{grp}", [128, GT * 128], F32))

        with nc.Block() as blk:
            pS0 = sigS.count
            off = 0
            if grp > 0:
                for tg in range(GT):
                    tok[("ln4ap", (grp - 1) * GT + tg)] = (sigS, pS0 + tg + 1)
                off = GT
            for tg in range(GT):
                ti = grp * GT + tg
                base = pS0 + off + tg * (nw + 1)
                tok[("ln2a", ti)] = (sigS, base + 1)
                for wv in range(nw):
                    tok[("ln2T_ev", ti, wv)] = (sigS, base + 2 + wv)
                tok[("ln2T_done", ti)] = tok[("ln2T_ev", ti, nw - 1)]
            for tg in range(GT):
                ti = grp * GT + tg
                tok[("tk_exp", ti)] = (sigS, pS0 + off + GT * (nw + 1) + tg + 1)
            pT0 = sigT.count
            for tg in range(GT):
                ti = grp * GT + tg
                tb = pT0 + tg * nw
                for wv in range(nw):
                    tok[("ln2T_mm", ti, wv)] = (sigT, tb + wv + 1)
                tok[("ln2T_mm_all", ti)] = tok[("ln2T_mm", ti, nw - 1)]
                tok[("lg_mm", ti)] = (sigT, pT0 + GT * nw + tg + 1)
            tb2 = pT0 + GT * (nw + 1)
            for tg in range(GT):
                tok[("combT_mm", grp * GT + tg)] = (sigT, tb2 + tg + 1)
            tok[("gate_mm", grp)] = (sigT, tb2 + GT + 1)
            tok[("h_mm", grp)] = (sigT, tb2 + GT + 2)

            if grp > 0:
                d_tail_sections(blk, grp - 1, dbg)

            for tg in range(GT):
                d_ln2_chain(blk, grp * GT + tg)

            @blk.vector
            def _(v, grp=grp):
                v = LastW(v)
                for tg in range(GT):
                    ti = grp * GT + tg
                    par = ti % 2
                    S_.wait(v, tok[("lg_mm", ti)])
                    S_.wait(v, tok.get(("tk_exp", ti - 1)))
                    v.tensor_copy(out=lg_sb[:], in_=lgP[:])
                    t = sigV.inc(v.last)
                    tok[("lg_used", ti)] = t
                    S_.wait(v, t)
                    v.reduce_max(out=m1_sb[:], in_=lg_sb[:], axis=AX.X)
                    t = sigV.inc(v.last)
                    S_.wait(v, t)
                    v.tensor_scalar(out=msk_sb[:], in0=lg_sb[:], scalar1=m1_sb[:],
                                    scalar2=None, op0=ALU.is_equal)
                    t = sigV.inc(v.last)
                    S_.wait(v, t)
                    v.scalar_tensor_tensor(out=msk_sb[:], in0=msk_sb[:],
                                           scalar=-1e30, in1=lg_sb[:],
                                           op0=ALU.mult, op1=ALU.add)
                    t = sigV.inc(v.last)
                    S_.wait(v, t)
                    v.reduce_max(out=m2_sb[:], in_=msk_sb[:], axis=AX.X)
                    t = sigV.inc(v.last)
                    S_.wait(v, t)
                    S_.wait(v, tok.get(("tk_exp", ti - 2)))
                    v.tensor_scalar(out=rcol_a[par][:], in0=m1_sb[:], scalar1=-1.0,
                                    scalar2=None, op0=ALU.mult)
                    t = sigV.inc(v.last)
                    S_.wait(v, t)
                    v.tensor_tensor(out=rcol_b[par][:], in0=m2_sb[:],
                                    in1=rcol_a[par][:], op=ALU.add)
                    tok[("tk1", ti)] = sigV.inc(v.last)
                    S_.wait(v, tok[("tk_exp", ti)])
                    v.tensor_scalar(out=col_c[par][:], in0=col_c[par][:],
                                    scalar1=1.0, scalar2=None, op0=ALU.add)
                    t = sigV.inc(v.last)
                    S_.wait(v, t)
                    v.reciprocal(out=col_c[par][:], in_=col_c[par][:])
                    t = sigV.inc(v.last)
                    S_.wait(v, t)
                    v.tensor_scalar(out=msk_sb[:], in0=lg_sb[:], scalar1=m2_sb[:],
                                    scalar2=None, op0=ALU.is_ge)
                    t = sigV.inc(v.last)
                    S_.wait(v, t)
                    v.tensor_tensor(out=expl_sb[:], in0=expl_sb[:], in1=msk_sb[:],
                                    op=ALU.mult)
                    t = sigV.inc(v.last)
                    S_.wait(v, t)
                    v.tensor_scalar(out=comb_g[:, tg, :], in0=expl_sb[:],
                                    scalar1=col_c[par][:], scalar2=2.0,
                                    op0=ALU.mult, op1=ALU.mult)
                    tok[("comb", ti)] = sigV.inc(v.last)

            vV0 = sigV.count
            for tg in range(GT):
                tok[("combT_ev", grp * GT + tg)] = (sigV, vV0 + tg + 1)
            tok[("gate_ev", grp)] = (sigV, vV0 + GT + 1)
            tok[("hg", grp)] = (sigV, vV0 + GT + 2)

            @blk.tensor
            def _(t, grp=grp):
                t = LastW(t)
                for tg in range(GT):
                    ti = grp * GT + tg
                    S_.wait(t, tok[("ln2a", ti)])
                    for wv in range(nw):
                        if wv >= 2:
                            S_.wait(t, tok[("ln2T_ev", ti, wv - 2)])
                        elif tg > 0:
                            S_.wait(t, tok[("ln2T_ev", ti - 1, nw - 2 + wv)])
                        for k in range(wv * 4, min(wv * 4 + 4, DK)):
                            t.transpose(trD[wv % 2][:, (k % 4) * 128:(k % 4 + 1) * 128],
                                        ln2_tmp[:, k * 128:(k + 1) * 128],
                                        ident_b[:])
                        tk = sigT.inc(t.last)
                        assert tk == tok[("ln2T_mm", ti, wv)]
                for tg in range(GT):
                    ti = grp * GT + tg
                    S_.wait(t, tok[("ln2T_done", ti)])
                    S_.wait(t, tok["ffn_w"])
                    S_.wait(t, tok.get(("lg_used", ti - 1)))
                    for k in range(DK):
                        t.matmul(lgP[:], ln2T_g[:, k, tg * 128:(tg + 1) * 128],
                                 wrt_b[:, k, :], start=(k == 0), stop=(k == DK - 1))
                    tk = sigT.inc(t.last)
                    assert tk == tok[("lg_mm", ti)]

            @blk.tensor
            def _(t, grp=grp):
                t = LastW(t)
                S_.wait(t, tok[("ln2T_ev", grp * GT + GT - 1, nw - 1)])
                for tg in range(GT):
                    ti = grp * GT + tg
                    S_.wait(t, tok[("comb", ti)])
                    S_.wait(t, tok.get(("combT_ev", ti - 1)))
                    t.transpose(trD[0][0:NE, 0:128], comb_g[:, tg, :], ident_b[:])
                    tk = sigT.inc(t.last)
                    assert tk == tok[("combT_mm", ti)]
                for tg in range(GT):
                    S_.wait(t, tok[("combT_ev", grp * GT + tg)])
                t.matmul(gateP[:, 0:GT * 128], emat_b[:], combT_sb[:],
                         start=True, stop=True)
                tk = sigT.inc(t.last)
                assert tk == tok[("gate_mm", grp)]
                for (g0, gw) in [(s0, min(512, GT * 128 - s0))
                                 for s0 in range(0, GT * 128, 512)]:
                    for k in range(DK):
                        t.matmul(hP[:, g0:g0 + gw], at_b[:, k, :],
                                 ln2T_g[:, k, g0:g0 + gw],
                                 start=(k == 0), stop=(k == DK - 1))
                tk = sigT.inc(t.last)
                assert tk == tok[("h_mm", grp)]

            @blk.vector
            def _(v, grp=grp):
                v = LastW(v)
                for tg in range(GT):
                    ti = grp * GT + tg
                    S_.wait(v, tok[("combT_mm", ti)])
                    v.tensor_copy(out=combT_sb[:, tg * 128:(tg + 1) * 128],
                                  in_=trD[0][0:NE, 0:128])
                    tk = sigV.inc(v.last)
                    assert tk == tok[("combT_ev", ti)]
                S_.wait(v, tok[("gate_mm", grp)])
                v.tensor_copy(out=gate_sb[:], in_=gateP[:, 0:GT * 128])
                tk = sigV.inc(v.last)
                assert tk == tok[("gate_ev", grp)]
                S_.wait(v, tk)
                S_.wait(v, tok[("h_mm", grp)])
                v.tensor_tensor(out=hg_sb[:], in0=hP[:, 0:GT * 128],
                                in1=gate_sb[:], op=ALU.mult)
                tk = sigV.inc(v.last)
                assert tk == tok[("hg", grp)]

            @blk.scalar
            def _(s, grp=grp):
                s = LastW(s)
                for tg in range(GT):
                    ti = grp * GT + tg
                    par = ti % 2
                    S_.wait(s, tok[("ln2fin", ti)])
                    S_.wait(s, tok.get(("ln2T_mm_all", ti - 1)))
                    s.activation(out=ln2_tmp[:], in_=x1_sb[:, ti, :],
                                 func=AF.Identity,
                                 scale=rstd2g[:, ti:ti + 1],
                                 bias=negmb2g[:, ti:ti + 1])
                    tk = sigS.inc(s.last)
                    assert tk == tok[("ln2a", ti)]
                    for wv in range(nw):
                        S_.wait(s, tok[("ln2T_mm", ti, wv)])
                        nk = min(4, DK - wv * 4)
                        for k2 in range(nk):
                            s.copy(out=ln2T_g[:, wv * 4 + k2, tg * 128:(tg + 1) * 128],
                                   in_=trD[wv % 2][:, k2 * 128:(k2 + 1) * 128])
                        tk = sigS.inc(s.last)
                        assert tk == tok[("ln2T_ev", ti, wv)]
                for tg in range(GT):
                    ti = grp * GT + tg
                    par = ti % 2
                    S_.wait(s, tok[("tk1", ti)])
                    S_.wait(s, tok.get(("comb", ti - 1)))
                    s.activation(out=expl_sb[:], in_=lg_sb[:], func=AF.Exp,
                                 bias=rcol_a[par][:], scale=1.0)
                    s.activation(out=col_c[par][:], in_=rcol_b[par][:], func=AF.Exp)
                    tk = sigS.inc(s.last)
                    assert tk == tok[("tk_exp", ti)]

            @blk.gpsimd
            def _(g, grp=grp):
                g = LastW(g)
                wbB = [dW[0].count, dW[1].count]
                for pi in range(min(2, len(dpairs))):
                    b, kh = dpairs[pi]
                    S_.wait(g, tok.get(("wbt_used", grp - 1)))
                    if grp == 0 and pi == 0:
                        S_.wait(g, tok.get(("x1_done", QT - 1)))
                    tok[("wbt", grp, pi)] = dW[pi % 2].inc(
                        g.dma_start(out=wbt_buf[pi % 2][:],
                                    in_=wbt_d[b, :, kh * DKHN:(kh + 1) * DKHN, :]), 16)

        psD.close()

        psD2 = ExitStack()
        fP = [psD2.enter_context(nc.psum_tensor(f"fP{grp}{i}", [128, FBW], F32))
              for i in range(GT)]

        mS0 = sigS.count
        for b in range(FOB):
            for tg in range(GT):
                tok[("f_evac", grp, b, tg)] = (sigS, mS0 + b * GT + tg + 1)
        wbB = [dW[0].count, dW[1].count]
        for pi in range(2, len(dpairs)):
            tok[("wbt", grp, pi)] = (dW[pi % 2], wbB[pi % 2] + 16 * ((pi - 2) // 2 + 1))

        with nc.Block() as blk:
            @blk.tensor
            def _(t, grp=grp):
                t = LastW(t)
                S_.wait(t, tok[("hg", grp)])
                S_.wait(t, tok["ffn_w"])
                for pi, (b, kh) in enumerate(dpairs):
                    S_.wait(t, tok[("wbt", grp, pi)])
                    if kh == 0:
                        for tg in range(GT):
                            S_.wait(t, tok.get(("f_evac", grp, b - 1, tg)))
                    for k2 in range(DKHN):
                        k = kh * DKHN + k2
                        for tg in range(GT):
                            t.matmul(fP[tg][:],
                                     ln2T_g[:, k, tg * 128:(tg + 1) * 128],
                                     wbt_buf[pi % 2][:, k2, :],
                                     start=(k == 0), stop=False)
                    if kh == DKH - 1:
                        for tg in range(GT):
                            t.matmul(fP[tg][:], hg_sb[:, tg * 128:(tg + 1) * 128],
                                     bflat_b[:, b * FBW:(b + 1) * FBW],
                                     start=False, stop=False)
                            t.matmul(fP[tg][:], ones_row[:],
                                     bbase_b[:, b * FBW:(b + 1) * FBW],
                                     start=False, stop=True)
                    tok[("f_mmp", grp, pi)] = sigT.inc(t.last)
                    if kh == DKH - 1:
                        tok[("f_mm", grp, b)] = tok[("f_mmp", grp, pi)]
                tok[("wbt_used", grp)] = tok[("f_mmp", grp, len(dpairs) - 1)]

            @blk.scalar
            def _(s, grp=grp):
                s = LastW(s)
                for b in range(FOB):
                    S_.wait(s, tok[("f_mm", grp, b)])
                    for tg in range(GT):
                        s.copy(out=ffn_g[:, tg, b * FBW:(b + 1) * FBW],
                               in_=fP[tg][:])
                        tk = sigS.inc(s.last)
                        assert tk == tok[("f_evac", grp, b, tg)]

            @blk.gpsimd
            def _(g, grp=grp):
                g = LastW(g)
                for pi in range(2, len(dpairs)):
                    b2, kh2 = dpairs[pi]
                    S_.wait(g, tok[("f_mmp", grp, pi - 2)])
                    tk = dW[pi % 2].inc(
                        g.dma_start(out=wbt_buf[pi % 2][:],
                                    in_=wbt_d[b2, :, kh2 * DKHN:(kh2 + 1) * DKHN, :]), 16)
                    assert tk == tok[("wbt", grp, pi)]

            @blk.vector
            def _(v, grp=grp):
                v = LastW(v)
                v.memset(magic4[:].bitcast(mybir.dt.int32), 0x5F3759DF)
                for b in range(FOB):
                    for tg in range(GT):
                        S_.wait(v, tok[("f_evac", grp, b, tg)])
                        v.bn_stats(out=ln4st[:, tg, 6 * b:6 * b + 6],
                                   in_=ffn_g[:, tg, b * FBW:(b + 1) * FBW])
                t = sigV.inc(v.last)
                S_.wait(v, t)
                for tg in range(GT):
                    v.bn_aggr(out=mvv4[:, tg, :], in_=ln4st[:, tg, :])
                t = sigV.inc(v.last)
                S_.wait(v, t)
                mvf4 = mvv4.rearrange("p a b -> p (a b)")
                v_rsqrtN(v, mvf4[:, 1:2 * GT:2], rstd4[:], rs4x[:], rs4t[:],
                         magic4[:], LN_EPS)
                t = sigV.inc(v.last)
                S_.wait(v, t)
                v.tensor_tensor(out=negmb4[:], in0=mvf4[:, 0:2 * GT:2],
                                in1=rstd4[:], op=ALU.mult)
                t = sigV.inc(v.last)
                S_.wait(v, t)
                v.tensor_scalar(out=negmb4[:], in0=negmb4[:], scalar1=-1.0,
                                scalar2=None, op0=ALU.mult)
                tok[("ln4rdy", grp)] = sigV.inc(v.last)

            if grp == DGRP - 1:
                pS0f = sigS.count
                for tg in range(GT):
                    tok[("ln4ap", grp * GT + tg)] = (sigS, pS0f + tg + 1)
                d_tail_sections(blk, grp, False)
        psD2.close()

        if dbg:
            dbg_f = nc.dram_tensor(f"dbg_f{grp}", [128, GT * DO], BF16,
                                   kind="ExternalOutput")
            with nc.Block() as blk:
                @blk.sync
                def _(sy, grp=grp, dbg_f=dbg_f):
                    for b in range(FOB):
                        for tg in range(GT):
                            S_.wait(sy, tok[("f_evac", grp, b, tg)])
                    sy.dma_start(out=dbg_f[:], in_=ffn_g.rearrange("p a b -> p (a b)")).then_inc(dO[0].sem, 16)
                    dO[0].count += 16
                    sy.wait_ge(dO[0].sem, dO[0].count)
                    tok[("dbgf", grp)] = (dO[0], dO[0].count)

    with nc.Block() as blk:
        @blk.sync
        def _(sy):
            sy.wait_ge(dO[0].sem, dO[0].count)
            sy.wait_ge(dO[1].sem, dO[1].count)

    nc._layout = {"ZLOW": ZLOW, "V0": V0,
                  "ln2T_g": offreg[id(ln2T_g)], "at_b": offreg[id(at_b)],
                  "x1_sb": offreg[id(x1_sb)], "ln2_tmp": offreg[id(ln2_tmp)],
                  "t1_buf": offreg[id(t1_buf)]}
    ctx.close()
    return nc


def make_in_maps(cfg, inputs, n_cores=None):
    w = dict(
        w_qa=inputs["L0_attn_w_qa"], g_qa=inputs["L0_attn_g_qa"],
        w_qb=inputs["L0_attn_w_qb"], w_kva=inputs["L0_attn_w_kva"],
        g_kva=inputs["L0_attn_g_kva"], w_kvb=inputs["L0_attn_w_kvb"],
        w_o=inputs["L0_attn_w_o"], W_base=inputs["L0_ffn_W_base"],
        b_base=inputs["L0_ffn_b_base"], W_router=inputs["L0_ffn_W_router"],
        A_stack=inputs["L0_ffn_A_stack"], B_stack=inputs["L0_ffn_B_stack"],
    )
    wp = prep_weights(cfg, w)
    x = np.asarray(inputs["L0_attn_hidden_states"])
    Q, S, RD = cfg["Q"], cfg["S"], cfg["RD"]
    halves = S // Q
    if n_cores is None:
        n_cores = x.shape[0] * halves
    c = np.ascontiguousarray
    in_maps = []
    for ci in range(n_cores):
        b, h = divmod(ci, halves)
        xf = x[b]
        perm = np.concatenate([np.arange(h * Q, (h + 1) * Q),
                               np.arange(0, h * Q),
                               np.arange((h + 1) * Q, S)]).astype(np.int64)
        xb_perm = c(xf[perm])
        coskp = c(wp["_cos"][perm].reshape(S // 128, 128, RD).transpose(1, 0, 2))
        sinkp = c(wp["_sin"][perm].reshape(S // 128, 128, RD).transpose(1, 0, 2))
        cosq2 = c(np.concatenate([wp["_cos"][h * Q:(h + 1) * Q].T] * 2, axis=1))
        sinq2 = c(np.concatenate([wp["_sin"][h * Q:(h + 1) * Q].T] * 2, axis=1))
        in_maps.append(dict(
            xb=xb_perm, wqa=wp["wqa"], wkva=wp["wkva"], wqb=wp["wqb"],
            wkvb=wp["wkvb"], wo=wp["wo"], wbt=wp["wbt"], wrt=wp["wrt"],
            at=wp["at"], bflat=wp["bflat"], bbase=wp["bbase"],
            gqa=wp["gqa"], gkva=wp["gkva"], cosk=coskp, sink=sinkp,
            cosq2=cosq2, sinq2=sinq2, emat=wp["emat"], ident=wp["ident"]))
    return in_maps


def kernel(**inputs):
    cfg = full_cfg()
    in_maps = make_in_maps(cfg, inputs)
    nc = build_program(cfg)
    res = run_bass_kernel_spmd(nc, in_maps, core_ids=list(range(8)))
    x = np.asarray(inputs["L0_attn_hidden_states"])
    Q, S = cfg["Q"], cfg["S"]
    out = np.empty(x.shape, dtype=np.float32)
    for ci in range(8):
        b, h = divmod(ci, S // Q)
        out[b, h * Q:(h + 1) * Q, :] = res.results[ci]["out"]
    return out


# version 20 (final)
# speedup vs baseline: 1.0093x; 1.0093x over previous
import numpy as np
from contextlib import ExitStack

import concourse.bass as bass
import concourse.mybir as mybir
from concourse.bass_utils import run_bass_kernel_spmd

F32 = mybir.dt.float32
BF16 = mybir.dt.bfloat16
U8 = mybir.dt.uint8
AF = mybir.ActivationFunctionType
ALU = mybir.AluOpType
AX = mybir.AxisListType

LN_EPS = 1e-5
RMS_EPS = 1e-6
THETA = 10000.0


class Sig:
    def __init__(self, sem):
        self.sem = sem
        self.count = 0

    def inc(self, inst, k=1):
        inst.then_inc(self.sem, k)
        self.count += k
        return (self, self.count)


class SyncCtx:
    def __init__(self):
        self.waited = {}

    def wait(self, eng, token):
        if token is None:
            return
        sig, val = token
        e = getattr(eng, "_e", eng)
        m = self.waited.setdefault(id(e), {})
        if val > m.get(id(sig), 0):
            eng.wait_ge(sig.sem, val)
            m[id(sig)] = val

    def wait_many(self, eng, tokens):
        best = {}
        for t in tokens:
            if t is None:
                continue
            sig, val = t
            if id(sig) not in best or val > best[id(sig)][1]:
                best[id(sig)] = (sig, val)
        for sig, val in best.values():
            self.wait(eng, (sig, val))


class LastW:
    _SKIP = {"wait_ge", "wait_op", "sem_inc", "nop", "engine_nop", "drain",
             "wait_many"}

    def __init__(self, eng):
        object.__setattr__(self, "_e", eng)
        object.__setattr__(self, "last", None)

    def __getattr__(self, n):
        a = getattr(object.__getattribute__(self, "_e"), n)
        if callable(a) and not n.startswith("_") and n not in LastW._SKIP:
            def f(*args, **kw):
                r = a(*args, **kw)
                if hasattr(r, "then_inc"):
                    object.__setattr__(self, "last", r)
                return r
            return f
        return a


def full_cfg():
    return dict(D=4096, QR=1536, KVR=512, RD=64, ND=128, VD=128, NH=32,
                S=2048, Q=1024, NE=8, RANK=16, DO=4096)


def small_cfg():
    return dict(D=1024, QR=512, KVR=256, RD=64, ND=128, VD=128, NH=2,
                S=256, Q=256, NE=8, RANK=16, DO=1024)


def prep_weights(cfg, w):
    D, QR, KVR, RD = cfg["D"], cfg["QR"], cfg["KVR"], cfg["RD"]
    ND, VD, NH, S = cfg["ND"], cfg["VD"], cfg["NH"], cfg["S"]
    NE, RANK, DO = cfg["NE"], cfg["RANK"], cfg["DO"]
    DK, QRK, KVK = D // 128, QR // 128, KVR // 128
    NP = NH // 2
    QK = ND + RD
    ER = NE * RANK
    c = np.ascontiguousarray
    out = {}
    nqb = max(1, QR // 384) if QR % 384 == 0 else 1
    out["wqa"] = c(w["w_qa"].reshape(DK, 128, nqb, QR // nqb).transpose(2, 1, 0, 3))
    out["wkva"] = c(w["w_kva"].reshape(DK, 128, KVR + RD).transpose(1, 0, 2))
    t = w["w_qb"].reshape(QRK, 128, NP, 2, QK)
    wqb = np.concatenate([t[:, :, :, 0, :ND], t[:, :, :, 1, :ND],
                          t[:, :, :, 0, ND:], t[:, :, :, 1, ND:]], axis=-1)
    out["wqb"] = c(wqb.transpose(2, 1, 0, 3))
    wkvb = w["w_kvb"].reshape(KVK, 128, NP, 2 * (ND + VD))
    out["wkvb"] = c(wkvb.transpose(2, 1, 0, 3))
    wob = max(1, D // 512)
    out["wo"] = c(w["w_o"].reshape(NH, 128, wob, D // wob).transpose(2, 1, 0, 3))
    fob = max(1, DO // 512)
    out["wbt"] = c(w["W_base"].T.reshape(DK, 128, fob, DO // fob).transpose(2, 1, 0, 3))
    out["wrt"] = c(w["W_router"].T.reshape(DK, 128, NE).transpose(1, 0, 2))
    out["at"] = c(w["A_stack"].reshape(ER, D).T.reshape(DK, 128, ER).transpose(1, 0, 2))
    out["bflat"] = c(w["B_stack"].transpose(0, 2, 1).reshape(ER, DO))
    out["bbase"] = c(w["b_base"].reshape(1, DO))
    out["gqa"] = c(w["g_qa"].reshape(QRK, 128).T)
    out["gkva"] = c(w["g_kva"].reshape(KVK, 128).T)
    inv_freq = 1.0 / THETA ** (np.arange(0, RD, 2, dtype=np.float64) / RD)
    tt = np.arange(S, dtype=np.float64)
    emb = np.concatenate([np.outer(tt, inv_freq)] * 2, -1)
    out["_cos"] = np.cos(emb).astype(np.float32)
    out["_sin"] = np.sin(emb).astype(np.float32)
    out["emat"] = c(np.repeat(np.eye(NE, dtype=np.float32), RANK, axis=1))
    out["ident"] = np.eye(128, dtype=np.float32)
    return out


def build_program(cfg, dbg=False):
    D, QR, KVR, RD = cfg["D"], cfg["QR"], cfg["KVR"], cfg["RD"]
    ND, VD, NH, S, Q = cfg["ND"], cfg["VD"], cfg["NH"], cfg["S"], cfg["Q"]
    NE, RANK, DO = cfg["NE"], cfg["RANK"], cfg["DO"]
    DK, QRK, KVK = D // 128, QR // 128, KVR // 128
    NP = NH // 2
    ER = NE * RANK
    KT, QT = S // 128, Q // 128
    nqb = max(1, QR // 384) if QR % 384 == 0 else 1
    NOB = max(1, D // 512)
    OBW = D // NOB
    FOB = max(1, DO // 512)
    FBW = DO // FOB
    KH = max(1, NH // 16)
    KHN = NH // KH
    DKH = max(1, DK // 8)
    DKHN = DK // DKH
    cscale = float((ND + RD) ** -0.5)
    qch = [(s0, min(512, Q - s0)) for s0 in range(0, Q, 512)]
    sch = [(s0, min(512, S - s0)) for s0 in range(0, S, 512)]
    DGRP = 2 if QT >= 2 else 1
    GT = QT // DGRP
    HLF = RD // 2

    nc = bass.Bass("TRN2", target_bir_lowering=False, debug=False)

    def din(name, shape):
        return nc.dram_tensor(name, list(shape), F32, kind="ExternalInput")

    xb_d = din("xb", [S, D])
    wqa_d = din("wqa", [nqb, 128, DK, QR // nqb])
    wkva_d = din("wkva", [128, DK, KVR + RD])
    wqb_d = din("wqb", [NP, 128, QRK, 2 * ND + 2 * RD])
    wkvb_d = din("wkvb", [NP, 128, KVK, 2 * (ND + VD)])
    wo_d = din("wo", [NOB, 128, NH, OBW])
    wbt_d = din("wbt", [FOB, 128, DK, FBW])
    wrt_d = din("wrt", [128, DK, NE])
    at_d = din("at", [128, DK, ER])
    bflat_d = din("bflat", [ER, DO])
    bbase_d = din("bbase", [1, DO])
    gqa_d = din("gqa", [128, QRK])
    gkva_d = din("gkva", [128, KVK])
    cosk_d = din("cosk", [128, KT, RD])
    sink_d = din("sink", [128, KT, RD])
    cosq2_d = din("cosq2", [RD, 2 * Q])
    sinq2_d = din("sinq2", [RD, 2 * Q])
    emat_d = din("emat", [NE, 128])
    ident_d = din("ident", [128, 128])
    out_d = nc.dram_tensor("out", [Q, D], F32, kind="ExternalOutput")

    ctx = ExitStack()
    S_ = SyncCtx()

    def mksig(name):
        return Sig(ctx.enter_context(nc.semaphore(name)))

    dX = [mksig("dX0"), mksig("dX1")]
    dW = [mksig("dW0"), mksig("dW1")]
    dW2 = [mksig("dW20"), mksig("dW21")]
    dC = mksig("dC")
    dCg = mksig("dCg")
    dO = [mksig("dO0"), mksig("dO1")]
    sigT = mksig("sT")
    sigV = mksig("sV")
    sigS = mksig("sS")

    PB = (nc.sbuf_bytes_remaining - 2048) // 32 * 32
    pool = nc.alloc_sbuf_tensor("pool", [128, PB], U8)

    offreg = {}

    def view(off, part, shape, dt):
        esz = 2 if dt == BF16 else 4
        n = 1
        for s in shape:
            n *= s
        assert off % 32 == 0 and off + n * esz <= PB, (off, shape, dt)
        ap = pool[0:part, off:off + n * esz].bitcast(dt)
        if len(shape) > 1:
            pat = " ".join(f"d{i}" for i in range(len(shape)))
            kw = {f"d{i}": shape[i] for i in range(len(shape))}
            ap = ap.rearrange(f"p ({pat}) -> p {pat}", **kw)
        offreg[id(ap)] = (off, n * esz)
        return ap

    class Cursor:
        def __init__(self, start, limit):
            self.off = start
            self.limit = limit

        def alloc(self, part, shape, dt):
            esz = 2 if dt == BF16 else 4
            n = 1
            for s in shape:
                n *= s
            off = self.off
            self.off = (off + n * esz + 31) // 32 * 32
            assert self.off <= self.limit, (off, shape, self.off, self.limit)
            return view(off, part, shape, dt)

    cc = Cursor(0, 3072)

    ident_b = cc.alloc(128, [128], BF16)
    ident_f = cc.alloc(128, [128], F32)
    ones_col = cc.alloc(128, [1], BF16)
    ones_row = cc.alloc(1, [128], BF16)
    emat_b = cc.alloc(NE, [128], BF16)
    epsln = cc.alloc(128, [1], F32)
    col_a = [cc.alloc(128, [1], F32) for _ in range(2)]
    col_b = [cc.alloc(128, [1], F32) for _ in range(2)]
    col_c = [cc.alloc(128, [1], F32) for _ in range(2)]
    mv = [cc.alloc(128, [2], F32) for _ in range(2)]
    stats = [cc.alloc(128, [48], F32) for _ in range(2)]
    negB_b = cc.alloc(1, [2], BF16)
    negB_col = cc.alloc(128, [1], F32)
    gm_sb = cc.alloc(1, [4], F32)
    rs_x = [cc.alloc(128, [1], F32) for _ in range(2)]
    rs_t = [cc.alloc(128, [1], F32) for _ in range(2)]
    magic_i = cc.alloc(128, [1], F32)
    gqa_b = cc.alloc(128, [QRK], F32)
    gkva_b = cc.alloc(128, [KVK], F32)

    ct = Cursor((cc.off + 255) // 256 * 256, PB)
    cosk_b = ct.alloc(128, [KT, RD], F32)
    sink_b = ct.alloc(128, [KT, RD], F32)
    cosq2_b = ct.alloc(RD, [2 * Q], BF16)
    sinq2_b = ct.alloc(RD, [2 * Q], BF16)
    k_cT = ct.alloc(128, [KVK, S], BF16)
    k_ropeT = ct.alloc(RD, [S], BF16)
    q_cT = ct.alloc(128, [QRK, Q], BF16)
    V0 = (ct.off + 511) // 512 * 512

    ZLOW = (cc.off + 255) // 256 * 256
    tok = {}

    with nc.Block() as blk:
        @blk.sync
        def _(sy):
            sy = LastW(sy)
            dC.inc(sy.dma_start(out=cosk_b[:], in_=cosk_d[:]), 16)
            dC.inc(sy.dma_start(out=sink_b[:], in_=sink_d[:]), 16)
            dC.inc(sy.dma_start(out=gqa_b[:], in_=gqa_d[:]), 16)
            dC.inc(sy.dma_start(out=gkva_b[:], in_=gkva_d[:]), 16)
            dC.inc(sy.dma_start(out=ident_f[:], in_=ident_d[:]), 16)

        @blk.gpsimd
        def _(g):
            g = LastW(g)
            dCg.inc(g.dma_start(out=ident_b[:], in_=ident_d[:]), 16)
            dCg.inc(g.dma_start(out=emat_b[:], in_=emat_d[:]), 16)
            dCg.inc(g.dma_start(out=cosq2_b[:], in_=cosq2_d[:]), 16)
            dCg.inc(g.dma_start(out=sinq2_b[:], in_=sinq2_d[:]), 16)

        @blk.vector
        def _(v):
            v = LastW(v)
            v.memset(ones_col[:], 1.0)
            v.memset(ones_row[:], 1.0)
            v.memset(magic_i[:].bitcast(mybir.dt.int32), 0x5F3759DF)
            tok["vconsts"] = sigV.inc(v.memset(epsln[:], LN_EPS))
    tok["consts_f"] = (dC, 80)
    tok["consts_b"] = (dCg, 64)
    tok["ident_ready"] = (dCg, 64)

    def v_rsqrt(v, px, src_ap, eps):
        v.tensor_scalar(out=rs_x[px][:], in0=src_ap, scalar1=float(eps),
                        scalar2=None, op0=ALU.add)
        t = sigV.inc(v.last)
        S_.wait(v, t)
        xi = rs_x[px][:].bitcast(mybir.dt.int32)
        yi = col_a[px][:].bitcast(mybir.dt.int32)
        v.tensor_scalar(out=yi, in0=xi, scalar1=1, scalar2=None,
                        op0=ALU.arith_shift_right)
        t = sigV.inc(v.last)
        S_.wait(v, t)
        v.scalar_tensor_tensor(out=yi, in0=yi, scalar=-1.0,
                               in1=magic_i[:].bitcast(mybir.dt.int32),
                               op0=ALU.mult, op1=ALU.add)
        t = sigV.inc(v.last)
        S_.wait(v, t)
        for it_ in range(3):
            v.tensor_tensor(out=rs_t[px][:], in0=col_a[px][:],
                            in1=col_a[px][:], op=ALU.mult)
            t = sigV.inc(v.last)
            S_.wait(v, t)
            v.tensor_tensor(out=rs_t[px][:], in0=rs_t[px][:], in1=rs_x[px][:],
                            op=ALU.mult)
            t = sigV.inc(v.last)
            S_.wait(v, t)
            v.tensor_scalar(out=rs_t[px][:], in0=rs_t[px][:], scalar1=-0.5,
                            scalar2=1.5, op0=ALU.mult, op1=ALU.add)
            t = sigV.inc(v.last)
            S_.wait(v, t)
            v.tensor_tensor(out=col_a[px][:], in0=col_a[px][:],
                            in1=rs_t[px][:], op=ALU.mult)
            if it_ < 2:
                t = sigV.inc(v.last)
                S_.wait(v, t)

    def v_rsqrtN(v, src_ap, out_ap, tx, tt, mg, eps):
        v.tensor_scalar(out=tx, in0=src_ap, scalar1=float(eps),
                        scalar2=None, op0=ALU.add)
        t = sigV.inc(v.last)
        S_.wait(v, t)
        xi = tx.bitcast(mybir.dt.int32)
        yi = out_ap.bitcast(mybir.dt.int32)
        v.tensor_scalar(out=yi, in0=xi, scalar1=1, scalar2=None,
                        op0=ALU.arith_shift_right)
        t = sigV.inc(v.last)
        S_.wait(v, t)
        v.scalar_tensor_tensor(out=yi, in0=yi, scalar=-1.0,
                               in1=mg.bitcast(mybir.dt.int32),
                               op0=ALU.mult, op1=ALU.add)
        t = sigV.inc(v.last)
        S_.wait(v, t)
        for it_ in range(3):
            v.tensor_tensor(out=tt, in0=out_ap, in1=out_ap, op=ALU.mult)
            t = sigV.inc(v.last)
            S_.wait(v, t)
            v.tensor_tensor(out=tt, in0=tt, in1=tx, op=ALU.mult)
            t = sigV.inc(v.last)
            S_.wait(v, t)
            v.tensor_scalar(out=tt, in0=tt, scalar1=-0.5, scalar2=1.5,
                            op0=ALU.mult, op1=ALU.add)
            t = sigV.inc(v.last)
            S_.wait(v, t)
            v.tensor_tensor(out=out_ap, in0=out_ap, in1=tt, op=ALU.mult)
            if it_ < 2:
                t = sigV.inc(v.last)
                S_.wait(v, t)

    def ln_stats(v, src, width, key, px):
        nch = max(1, width // 512)
        cw = width // nch
        for i in range(nch):
            v.bn_stats(out=stats[px][:, 6 * i:6 * i + 6],
                       in_=src[:, i * cw:(i + 1) * cw])
        t = sigV.inc(v.last)
        S_.wait(v, t)
        v.bn_aggr(out=mv[px][:], in_=stats[px][:, 0:6 * nch])
        tok[key] = sigV.inc(v.last)

    def ln_finish(v, key_sqrt, px):
        S_.wait(v, tok[key_sqrt])
        v.tensor_scalar(out=col_b[px][:], in0=mv[px][:, 0:1], scalar1=col_a[px][:],
                        scalar2=-1.0, op0=ALU.mult, op1=ALU.mult)
        t = sigV.inc(v.last)
        S_.wait(v, t)
        return t

    va = Cursor(V0, PB)
    ln1T_own = va.alloc(128, [QT, DK * 128], BF16)
    x_buf0 = va.alloc(128, [D], F32)
    ln1_bufs = [va.alloc(128, [D], BF16) for _ in range(2)]
    ln1T_tmps = [va.alloc(128, [DK * 128], BF16)]
    wkva_b = va.alloc(128, [DK, KVR + RD], BF16)
    kc_tmps = [va.alloc(128, [KVR], BF16) for _ in range(2)]
    kr_tmpA = [va.alloc(128, [RD], F32) for _ in range(2)]
    kr_tmpB = [va.alloc(128, [RD], F32) for _ in range(2)]
    krb_tmps = [va.alloc(128, [RD], BF16) for _ in range(2)]
    _qo, _qn = offreg[id(q_cT)]
    if _qn >= D * 4 + DK * 128 * 2:
        x_buf1 = view(_qo, 128, [D], F32)
        ln1T_tmps.append(view(_qo + D * 4, 128, [DK * 128], BF16))
    else:
        x_buf1 = va.alloc(128, [D], F32)
        ln1T_tmps.append(va.alloc(128, [DK * 128], BF16))
    x_bufs = [x_buf0, x_buf1]

    psA = ExitStack()
    trP = [psA.enter_context(nc.psum_tensor(f"trP{i}", [128, 2048], BF16))
           for i in range(2)]
    psA1 = ExitStack()
    kvP = [psA1.enter_context(nc.psum_tensor(f"kvP{i}", [128, KVR + RD], F32))
           for i in range(2)]

    with nc.Block() as blk:
        @blk.gpsimd
        def _(g):
            g = LastW(g)
            tok["wkva"] = dW[0].inc(g.dma_start(out=wkva_b[:], in_=wkva_d[:]), 16)

    xv = xb_d.rearrange("(t p) d -> t p d", p=128)
    kvsplits = [(0, min(512, KVR))]
    if KVR > 512:
        kvsplits.append((512, KVR - 512))

    def a1_xload(blk, i):
        par = i % 2
        if True:
            @blk.sync
            def _(sy):
                sy = LastW(sy)
                S_.wait(sy, tok.get(("lnapply", i - 2)))
                tok[("xload", i)] = dX[par].inc(
                    sy.dma_start(out=x_bufs[par][:], in_=xv[i, :, :]), 16)

    def a1_ln(blk, i):
        par = i % 2
        if True:
            @blk.vector
            def _(v):
                v = LastW(v)
                S_.wait(v, tok[("xload", i)])
                S_.wait(v, tok["vconsts"])
                ln_stats(v, x_bufs[par], D, ("ln1s", i), par)

            @blk.vector
            def _(v):
                v = LastW(v)
                S_.wait(v, tok[("ln1s", i)])
                v_rsqrt(v, par, mv[par][:, 1:2], LN_EPS)
                tok[("ln1sq", i)] = sigV.inc(v.last)

            @blk.vector
            def _(v):
                v = LastW(v)
                tok[("ln1fin", i)] = ln_finish(v, ("ln1sq", i), par)

    def a1_transpose(blk, i):
        par = i % 2
        if True:
            @blk.tensor
            def _(t):
                t = LastW(t)
                S_.wait(t, tok[("lnapply", i)])
                S_.wait(t, tok["ident_ready"])
                nw = (DK + 15) // 16
                S_.wait(t, tok.get(("lnT_evac", i - 2, nw - 1)))
                S_.wait(t, tok.get(("kcT_evac", i - 2)))
                for wv in range(nw):
                    if wv > 0:
                        S_.wait(t, tok[("lnT_evac", i, wv - 1)])
                    for k in range(wv * 16, min(wv * 16 + 16, DK)):
                        t.transpose(trP[par][:, (k % 16) * 128:(k % 16) * 128 + 128],
                                    ln1_bufs[par][:, k * 128:(k + 1) * 128], ident_b[:])
                    tok[("lnT_mm", i, wv)] = sigT.inc(t.last)
                tok[("lnT_mm_all", i)] = tok[("lnT_mm", i, nw - 1)]

    def a1_kvmm(blk, i):
        par = i % 2
        if True:
            @blk.tensor
            def _(t):
                t = LastW(t)
                S_.wait(t, tok[("lnT_done", i)])
                S_.wait(t, tok["wkva"])
                S_.wait(t, tok.get(("kv_used", i - 2)))
                if i < QT:
                    lsrc = ln1T_own[:, i, :]
                else:
                    lsrc = ln1T_tmps[par][:]
                for (c0, w) in kvsplits:
                    for k in range(DK):
                        t.matmul(kvP[par][:, c0:c0 + w],
                                 lsrc[:, k * 128:(k + 1) * 128],
                                 wkva_b[:, k, c0:c0 + w],
                                 start=(k == 0), stop=(k == DK - 1))
                for k in range(DK):
                    t.matmul(kvP[par][:, KVR:KVR + RD],
                             lsrc[:, k * 128:(k + 1) * 128],
                             wkva_b[:, k, KVR:KVR + RD],
                             start=(k == 0), stop=(k == DK - 1))
                tok[("kv_mm", i)] = sigT.inc(t.last)

    def a1_kvchain(blk, i):
        par = i % 2
        if True:
            @blk.vector
            def _(v):
                v = LastW(v)
                S_.wait(v, tok[("kv_mm", i)])
                nch = max(1, KVR // 512)
                cw = KVR // nch
                for c_ in range(nch):
                    v.bn_stats(out=stats[par][:, 6 * c_:6 * c_ + 6],
                               in_=kvP[par][:, c_ * cw:(c_ + 1) * cw])
                t = sigV.inc(v.last)
                S_.wait(v, t)
                v.bn_aggr(out=mv[par][:], in_=stats[par][:, 0:6 * nch])
                t = sigV.inc(v.last)
                S_.wait(v, t)
                v.tensor_scalar(out=col_c[par][:], in0=mv[par][:, 0:1],
                                scalar1=mv[par][:, 0:1], scalar2=None, op0=ALU.mult)
                t = sigV.inc(v.last)
                S_.wait(v, t)
                v.tensor_tensor(out=col_c[par][:], in0=col_c[par][:],
                                in1=mv[par][:, 1:2], op=ALU.add)
                tok[("kv_ms", i)] = sigV.inc(v.last)

            @blk.vector
            def _(v):
                v = LastW(v)
                S_.wait(v, tok[("kv_ms", i)])
                if i + 2 < KT:
                    S_.wait(v, tok[("lnapply", i + 2)])
                v_rsqrt(v, par, col_c[par][:], 1e-30)
                tok[("kv_sq", i)] = sigV.inc(v.last)

            @blk.vector
            def _(v):
                v = LastW(v)
                S_.wait(v, tok[("kv_sq", i)])
                v.tensor_tensor(out=col_a[par][:], in0=col_a[par][:],
                                in1=col_c[par][:], op=ALU.mult)
                t = sigV.inc(v.last)
                S_.wait(v, t)
                v.tensor_scalar(out=col_a[par][:], in0=col_a[par][:],
                                scalar1=RMS_EPS, scalar2=None, op0=ALU.add)
                t = sigV.inc(v.last)
                S_.wait(v, t)
                v.reciprocal(out=col_a[par][:], in_=col_a[par][:])
                t = sigV.inc(v.last)
                S_.wait(v, t)
                S_.wait(v, tok.get(("kcT_mm", i - 2)))
                S_.wait(v, tok["consts_f"])
                v.tensor_scalar(out=kc_tmps[par][:], in0=kvP[par][:, 0:KVR],
                                scalar1=col_a[par][:], scalar2=None, op0=ALU.mult)
                v.tensor_tensor(out=kr_tmpA[par][:], in0=kvP[par][:, KVR:KVR + RD],
                                in1=cosk_b[:, i, :], op=ALU.mult)
                v.tensor_scalar(out=kr_tmpB[par][:, 0:HLF],
                                in0=kvP[par][:, KVR + HLF:KVR + RD],
                                scalar1=-1.0, scalar2=None, op0=ALU.mult)
                v.tensor_copy(out=kr_tmpB[par][:, HLF:RD],
                              in_=kvP[par][:, KVR:KVR + HLF])
                t = sigV.inc(v.last)
                S_.wait(v, t)
                v.tensor_tensor(out=kr_tmpB[par][:], in0=kr_tmpB[par][:],
                                in1=sink_b[:, i, :], op=ALU.mult)
                t = sigV.inc(v.last)
                S_.wait(v, t)
                v.tensor_tensor(out=krb_tmps[par][:], in0=kr_tmpA[par][:],
                                in1=kr_tmpB[par][:], op=ALU.add)
                tok[("kv_rdy", i)] = sigV.inc(v.last)
                tok[("kv_used", i)] = tok[("kv_rdy", i)]

            @blk.tensor
            def _(t):
                t = LastW(t)
                S_.wait(t, tok[("kv_rdy", i)])
                S_.wait(t, tok.get(("kcT_evac", i - 2)))
                S_.wait(t, tok.get(("lnT_done", i)))
                for k in range(KVK):
                    t.transpose(trP[par][:, k * 128:(k + 1) * 128],
                                kc_tmps[par][:, k * 128:(k + 1) * 128], ident_b[:])
                t.transpose(trP[par][0:RD, KVK * 128:KVK * 128 + 128],
                            krb_tmps[par][:], ident_b[:])
                tok[("kcT_mm", i)] = sigT.inc(t.last)

            @blk.vector
            def _(v):
                v = LastW(v)
                S_.wait(v, tok[("kcT_mm", i)])
                for k in range(KVK):
                    v.tensor_scalar(out=k_cT[:, k, i * 128:(i + 1) * 128],
                                    in0=trP[par][:, k * 128:(k + 1) * 128],
                                    scalar1=gkva_b[:, k:k + 1], scalar2=None,
                                    op0=ALU.mult)
                v.tensor_copy(out=k_ropeT[:, i * 128:(i + 1) * 128],
                              in_=trP[par][0:RD, KVK * 128:KVK * 128 + 128])
                tok[("kcT_evac", i)] = sigV.inc(v.last)

    nwA = (DK + 15) // 16
    aS0 = sigS.count
    for i in range(KT):
        base = aS0 + i * (nwA + 1)
        tok[("lnapply", i)] = (sigS, base + 1)
        for wv in range(nwA):
            tok[("lnT_evac", i, wv)] = (sigS, base + 2 + wv)
        tok[("lnT_done", i)] = tok[("lnT_evac", i, nwA - 1)]

    with nc.Block() as blk:
        a1_xload(blk, 0)
        for it in range(KT + 2):
            if it + 1 < KT:
                a1_xload(blk, it + 1)
            if it < KT:
                a1_ln(blk, it)
            if 1 <= it <= KT:
                a1_kvmm(blk, it - 1)
            if 2 <= it:
                a1_kvchain(blk, it - 2)
            if it < KT:
                a1_transpose(blk, it)

        @blk.scalar
        def _(s):
            s = LastW(s)
            for i in range(KT):
                par = i % 2
                S_.wait(s, tok[("ln1fin", i)])
                S_.wait(s, tok[("xload", i)])
                S_.wait(s, tok.get(("lnT_mm_all", i - 2)))
                s.activation(out=ln1_bufs[par][:], in_=x_bufs[par][:],
                             func=AF.Identity,
                             scale=col_a[par][:], bias=col_b[par][:])
                tk = sigS.inc(s.last)
                assert tk == tok[("lnapply", i)]
                for wv in range(nwA):
                    S_.wait(s, tok[("lnT_mm", i, wv)])
                    nk = min(16, DK - wv * 16)
                    if i < QT:
                        dst = ln1T_own[:, i, wv * 16 * 128:(wv * 16 + nk) * 128]
                    else:
                        dst = ln1T_tmps[par][:, wv * 16 * 128:(wv * 16 + nk) * 128]
                    s.copy(out=dst, in_=trP[par][:, 0:nk * 128])
                    tk = sigS.inc(s.last)
                    assert tk == tok[("lnT_evac", i, wv)]
    psA1.close()
    vb = Cursor(V0, PB)
    wqb_buf = [vb.alloc(128, [QRK, 2 * ND + 2 * RD], BF16) for _ in range(2)]
    wkvb_buf = [vb.alloc(128, [KVK, 2 * (ND + VD)], BF16) for _ in range(2)]
    q_nT = [[vb.alloc(128, [Q], BF16) for _ in range(2)] for _ in range(2)]
    q_rT = [vb.alloc(RD, [2 * Q], BF16) for _ in range(2)]
    ropes_b = vb.alloc(RD, [2 * Q], BF16)
    rot_b = vb.alloc(RD, [2 * Q], BF16)
    ropet1 = vb.alloc(RD, [2 * Q], BF16)
    k_nT = [vb.alloc(128, [S], BF16) for _ in range(2)]
    v_sb = [vb.alloc(128, [KT, VD], BF16) for _ in range(2)]
    PT_buf = [vb.alloc(128, [512], BF16) for _ in range(2)]
    sq_s = vb.alloc(128, [S // 8], BF16)
    sq_q = vb.alloc(128, [Q // 8], BF16)
    sq_krs = vb.alloc(RD, [S // 8], BF16)
    sq_qrs = vb.alloc(RD, [Q // 8], BF16)
    rsumc_f = vb.alloc(128, [Q // 128], F32)
    rsumc = vb.alloc(128, [Q // 128], BF16)
    bcast_sb = vb.alloc(128, [Q], F32)
    if KT * RD * 8 >= Q * 6:
        srow_f = view(ZLOW, 1, [Q], F32)
        rsum_b = view(ZLOW + Q * 4, 1, [Q], BF16)
    else:
        srow_f = vb.alloc(1, [Q], F32)
        rsum_b = vb.alloc(1, [Q], BF16)
    negB2 = [vb.alloc(128, [1], F32) for _ in range(2)]
    negB2b = [vb.alloc(1, [2], BF16) for _ in range(2)]
    out_all = vb.alloc(128, [NH, Q], BF16)
    assert vb.off <= PB, (vb.off, PB)

    def b_preamble(blk):
        @blk.scalar
        def _(s):
            s = LastW(s)
            S_.wait(s, tok[("kcT_evac", KT - 1)])
            S_.wait(s, tok[("qevac", nqb - 1, QT - 1)])
            s.activation(out=sq_krs[:], in_=k_ropeT[:, 0:S:8], func=AF.Square)
            tok["sq_krs"] = sigS.inc(s.last)

        @blk.gpsimd
        def _(g):
            g = LastW(g)
            S_.wait(g, tok[("qevac", nqb - 1, QT - 1)])
            for pp in range(min(2, NP)):
                tok[("wqb", pp)] = dW[pp].inc(
                    g.dma_start(out=wqb_buf[pp][:], in_=wqb_d[pp, :, :, :]), 16)
                tok[("wkvb", pp)] = dW2[pp].inc(
                    g.dma_start(out=wkvb_buf[pp][:], in_=wkvb_d[pp, :, :, :]), 16)

    NUQ = Q // 512 if Q >= 512 else 1
    UQW = Q // NUQ
    units = [(kt, c) for kt in range(KT) for c in range(NUQ)]
    NKC = max(1, S // 512)
    KCW = S // NKC
    NVG = max(1, KT // 4)
    VGW = KT // NVG

    def b_prefetch(blk, p1):
        @blk.gpsimd
        def _(g, p1=p1):
            g = LastW(g)
            S_.wait(g, tok[("prep_mm", 2 * (p1 - 2), 3 * NUQ - 1)])
            S_.wait(g, tok[("v_mm", 2 * p1 - 3, NVG - 1)])
            tok[("wqb", p1)] = dW[p1 % 2].inc(
                g.dma_start(out=wqb_buf[p1 % 2][:], in_=wqb_d[p1, :, :, :]), 16)
            tok[("wkvb", p1)] = dW2[p1 % 2].inc(
                g.dma_start(out=wkvb_buf[p1 % 2][:], in_=wkvb_d[p1, :, :, :]), 16)

    def b_prep_a(blk, h):
        p, j = h // 2, h % 2
        par = p % 2
        hpar = h % 2
        ppar = p % 2
        sbase = sigS.count
        if j == 0:
            for u in range(3 * NUQ):
                tok[("prep_ev", 2 * p, u)] = (sigS, sbase + u + 1)
            sbase += 3 * NUQ
        for c in range(NKC):
            tok[("knT_ev", h, c)] = (sigS, sbase + c + 1)

        if j == 0:
            for jj in (0, 1):
                for c in range(NUQ):
                    u = jj * NUQ + c

                    @blk.tensor
                    def _(t, p=p, par=par, jj=jj, c=c, u=u):
                        t = LastW(t)
                        S_.wait(t, tok[("wqb", p)])
                        S_.wait(t, tok[("qcT_evac", QT - 1)])
                        if u >= 2:
                            S_.wait(t, tok[("prep_ev", 2 * p, u - 2)])
                        S_.wait(t, tok.get(("prep_done", 2 * p - 1)))
                        for k in range(QRK):
                            t.matmul(PEp[u % 2][:, 0:UQW],
                                     wqb_buf[par][:, k, jj * ND:jj * ND + ND],
                                     q_cT[:, k, c * UQW:(c + 1) * UQW],
                                     start=(k == 0), stop=(k == QRK - 1))
                        tok[("prep_mm", 2 * p, u)] = sigT.inc(t.last)
            for c in range(NUQ):
                u = 2 * NUQ + c

                @blk.tensor
                def _(t, p=p, par=par, c=c, u=u):
                    t = LastW(t)
                    S_.wait(t, tok[("prep_ev", 2 * p, u - 2)])
                    for k in range(QRK):
                        t.matmul(PEp[u % 2][:, 0:UQW],
                                 wqb_buf[par][:, k, 2 * ND:2 * ND + 2 * RD],
                                 q_cT[:, k, c * UQW:(c + 1) * UQW],
                                 start=(k == 0), stop=(k == QRK - 1))
                    tok[("prep_mm", 2 * p, u)] = sigT.inc(t.last)

        for c in range(NKC):
            @blk.tensor
            def _(t, h=h, p=p, j=j, par=par, c=c):
                t = LastW(t)
                S_.wait(t, tok[("wkvb", p)])
                if c >= 2:
                    S_.wait(t, tok[("knT_ev", h, c - 2)])
                elif j == 0:
                    lastu = max(u for u in range(3 * NUQ) if u % 2 == c % 2)
                    S_.wait(t, tok[("prep_ev", 2 * p, lastu)])
                if c == 0 and j == 1:
                    S_.wait(t, tok.get(("prep_done", h - 1)))
                koff = j * (ND + VD)
                for k in range(KVK):
                    t.matmul(PEp[c % 2][:, 0:KCW],
                             wkvb_buf[par][:, k, koff:koff + ND],
                             k_cT[:, k, c * KCW:(c + 1) * KCW],
                             start=(k == 0), stop=(k == KVK - 1))
                tok[("knT_mm", h, c)] = sigT.inc(t.last)


    def b_prep_b1(blk, h):
        p, j = h // 2, h % 2
        par = p % 2
        hpar = h % 2
        ppar = p % 2
        for g_ in range(NVG):
            @blk.tensor
            def _(t, h=h, p=p, j=j, par=par, c=g_):
                t = LastW(t)
                if c >= 2:
                    S_.wait(t, tok[("v_ev", h, c - 2)])
                elif c == 0:
                    S_.wait(t, tok[("knT_ev", h, NKC - 1)])
                koff = j * (ND + VD)
                for c_ in range(VGW):
                    kt = c * VGW + c_
                    for k in range(KVK):
                        t.matmul(PEp[c % 2][:, c_ * 128:(c_ + 1) * 128],
                                 k_cT[:, k, kt * 128:(kt + 1) * 128],
                                 wkvb_buf[par][:, k, koff + ND:koff + ND + VD],
                                 start=(k == 0), stop=(k == KVK - 1))
                tok[("v_mm", h, c)] = sigT.inc(t.last)

            @blk.vector
            def _(v, h=h, hpar=hpar, c=g_):
                v = LastW(v)
                S_.wait(v, tok[("v_mm", h, c)])
                v.tensor_copy(
                    out=v_sb[hpar].rearrange("p t d -> p (t d)")[
                        :, c * VGW * VD:(c + 1) * VGW * VD],
                    in_=PEp[c % 2][:, 0:VGW * 128])
                tok[("v_ev", h, c)] = sigV.inc(v.last)

        if j == 0:
            @blk.vector
            def _(v, p=p, ppar=ppar):
                v = LastW(v)
                S_.wait(v, tok[("prep_ev", 2 * p, 3 * NUQ - 1)])
                v.tensor_scalar(out=rot_b[0:HLF, :], in0=ropes_b[HLF:RD, :],
                                scalar1=-1.0, scalar2=None, op0=ALU.mult)
                v.tensor_copy(out=rot_b[HLF:RD, :], in_=ropes_b[0:HLF, :])
                t = sigV.inc(v.last)
                S_.wait(v, t)
                v.tensor_tensor(out=ropet1[:], in0=ropes_b[:], in1=cosq2_b[:],
                                op=ALU.mult)
                v.tensor_tensor(out=rot_b[:], in0=rot_b[:], in1=sinq2_b[:],
                                op=ALU.mult)
                t = sigV.inc(v.last)
                S_.wait(v, t)
                v.tensor_tensor(out=q_rT[ppar][:], in0=ropet1[:],
                                in1=rot_b[:], op=ALU.add)
                tok[("rope_rdy", p)] = sigV.inc(v.last)
                tok[("rope_used", p)] = tok[("rope_rdy", p)]

    def b_prep_b2(blk, h):
        p, j = h // 2, h % 2
        par = p % 2
        hpar = h % 2
        ppar = p % 2
        @blk.scalar
        def _(s, h=h, p=p, j=j, hpar=hpar, ppar=ppar):
            s = LastW(s)
            if j == 0:
                for jj in (0, 1):
                    for c in range(NUQ):
                        u = jj * NUQ + c
                        S_.wait(s, tok[("prep_mm", 2 * p, u)])
                        s.copy(out=q_nT[ppar][jj][:, c * UQW:(c + 1) * UQW],
                               in_=PEp[u % 2][:, 0:UQW])
                        tk = sigS.inc(s.last)
                        assert tk == tok[("prep_ev", 2 * p, u)]
                for c in range(NUQ):
                    u = 2 * NUQ + c
                    S_.wait(s, tok[("prep_mm", 2 * p, u)])
                    S_.wait(s, tok.get(("rope_used", p - 1)))
                    s.copy(out=ropes_b[:, c * UQW:(c + 1) * UQW],
                           in_=PEp[u % 2][0:RD, 0:UQW])
                    s.copy(out=ropes_b[:, Q + c * UQW:Q + (c + 1) * UQW],
                           in_=PEp[u % 2][RD:2 * RD, 0:UQW])
                    tk = sigS.inc(s.last)
                    assert tk == tok[("prep_ev", 2 * p, u)]
            for c in range(NKC):
                S_.wait(s, tok[("knT_mm", h, c)])
                s.copy(out=k_nT[hpar][:, c * KCW:(c + 1) * KCW],
                       in_=PEp[c % 2][:, 0:KCW])
                tk = sigS.inc(s.last)
                assert tk == tok[("knT_ev", h, c)]
            S_.wait(s, tok[("knT_ev", h, NKC - 1)])
            S_.wait(s, tok.get(("guard_mm", h - 1)))
            s.activation(out=sq_s[:], in_=k_nT[hpar][:, 0:S:8],
                         func=AF.Square)
            S_.wait(s, tok[("prep_ev", 2 * p, (1 + j) * NUQ - 1)])
            S_.wait(s, tok[("rope_rdy", p)])
            s.activation(out=sq_q[:], in_=q_nT[p % 2][j][:, 0:Q:8],
                         func=AF.Square)
            s.activation(out=sq_qrs[:],
                         in_=q_rT[p % 2][:, j * Q:(j + 1) * Q:8],
                         func=AF.Square)
            tok[("sqk", h)] = sigS.inc(s.last)

        @blk.tensor
        def _(t, h=h):
            t = LastW(t)
            S_.wait(t, tok[("sqk", h)])
            S_.wait(t, tok["sq_krs"])
            S_.wait(t, tok[("v_ev", h, NVG - 1)])
            t.matmul(PEp[0][0:1, 0:S // 8], ones_col[:], sq_s[:],
                     start=True, stop=False)
            t.matmul(PEp[0][0:1, 0:S // 8], ones_col[0:RD, :], sq_krs[:],
                     start=False, stop=True)
            t.matmul(PEp[1][0:1, 0:Q // 8], ones_col[:], sq_q[:],
                     start=True, stop=False)
            t.matmul(PEp[1][0:1, 0:Q // 8], ones_col[0:RD, :], sq_qrs[:],
                     start=False, stop=True)
            tok[("guard_mm", h)] = sigT.inc(t.last)

        @blk.vector
        def _(v, h=h):
            v = LastW(v)
            S_.wait(v, tok[("guard_mm", h)])
            v.reduce_max(out=gm_sb[:, 0:1], in_=PEp[0][0:1, 0:S // 8],
                         axis=AX.X)
            v.reduce_max(out=gm_sb[:, 1:2], in_=PEp[1][0:1, 0:Q // 8],
                         axis=AX.X)
            t = sigV.inc(v.last)
            S_.wait(v, t)
            v.tensor_tensor(out=gm_sb[:, 2:3], in0=gm_sb[:, 0:1],
                            in1=gm_sb[:, 1:2], op=ALU.add)
            t = sigV.inc(v.last)
            S_.wait(v, t)
            v.tensor_scalar(out=negB2b[h % 2][:, 0:1], in0=gm_sb[:, 2:3],
                            scalar1=-cscale / 2.0, scalar2=None,
                            op0=ALU.mult)
            tok[("negB", h)] = sigV.inc(v.last)

        @blk.tensor
        def _(t, h=h):
            t = LastW(t)
            S_.wait(t, tok[("negB", h)])
            t.matmul(PEp[0][:, 0:1], ones_row[:], negB2b[h % 2][:, 0:1],
                     start=True, stop=True)
            tok[("negB_mm", h)] = sigT.inc(t.last)

        @blk.vector
        def _(v, h=h):
            v = LastW(v)
            S_.wait(v, tok[("negB_mm", h)])
            v.tensor_copy(out=negB2[h % 2][:], in_=PEp[0][:, 0:1])
            t = sigV.inc(v.last)
            tok[("negB_col", h)] = t
            tok[("prep_done", h)] = t


    def b_prep(blk, h):
        b_prep_a(blk, h)
        b_prep_b1(blk, h)
        b_prep_b2(blk, h)

    def kt_main(blk, h):
        p, j = h // 2, h % 2
        hpar = h % 2
        qroff = j * Q
        sS0 = sigS.count

        @blk.tensor
        def _(t, h=h, p=p, j=j, hpar=hpar, qroff=qroff, sS0=sS0):
            t = LastW(t)
            S_.wait(t, tok[("knT_ev", h, NKC - 1)])
            S_.wait(t, tok[("negB_col", h)])
            S_.wait(t, tok[("v_ev", h, NVG - 1)])
            for ui, (kt, c) in enumerate(units):
                if ui == 1:
                    S_.wait(t, tok.get(("out_evac", h - 1)))
                if ui >= 2:
                    S_.wait(t, (sigS, sS0 + ui - 1))
                t.matmul(sPP[ui % 2][:, 0:UQW],
                         k_nT[hpar][:, kt * 128:(kt + 1) * 128],
                         q_nT[p % 2][j][:, c * UQW:(c + 1) * UQW],
                         start=True, stop=False)
                t.matmul(sPP[ui % 2][:, 0:UQW],
                         k_ropeT[:, kt * 128:(kt + 1) * 128],
                         q_rT[p % 2][:, qroff + c * UQW:qroff + (c + 1) * UQW],
                         start=False, stop=True)
                tok[("sT_mm", h, ui)] = sigT.inc(t.last)
                if ui >= 1:
                    up = ui - 1
                    ktp, cp = units[up]
                    S_.wait(t, (sigS, sS0 + up + 1))
                    t.matmul(PDq[0:1, cp * UQW:(cp + 1) * UQW], ones_col[:],
                             PT_buf[up % 2][:, 0:UQW],
                             start=(ktp == 0), stop=(ktp == KT - 1))
                    t.matmul(PCq[:, cp * UQW:(cp + 1) * UQW],
                             v_sb[hpar][:, ktp, :], PT_buf[up % 2][:, 0:UQW],
                             start=(ktp == 0), stop=(ktp == KT - 1))
            up = len(units) - 1
            ktp, cp = units[up]
            S_.wait(t, (sigS, sS0 + up + 1))
            t.matmul(PDq[0:1, cp * UQW:(cp + 1) * UQW], ones_col[:],
                     PT_buf[up % 2][:, 0:UQW],
                     start=(ktp == 0), stop=True)
            t.matmul(PCq[:, cp * UQW:(cp + 1) * UQW],
                     v_sb[hpar][:, ktp, :], PT_buf[up % 2][:, 0:UQW],
                     start=(ktp == 0), stop=True)
            tok[("pv_done", h)] = sigT.inc(t.last)
            tok[("sums_done", h)] = tok[("pv_done", h)]

        @blk.scalar
        def _(s, h=h, sS0=sS0):
            s = LastW(s)
            for ui in range(len(units)):
                S_.wait(s, tok[("sT_mm", h, ui)])
                s.activation(out=PT_buf[ui % 2][:, 0:UQW],
                             in_=sPP[ui % 2][:, 0:UQW],
                             func=AF.Exp, bias=negB2[h % 2][:], scale=cscale)
                tk = sigS.inc(s.last)
                tok[("exp", h, ui)] = tk
                assert tk[1] == sS0 + ui + 1

    def kt_tail_v1(blk, h):
        @blk.vector
        def _(v, h=h):
            v = LastW(v)
            S_.wait(v, tok[("sums_done", h)])
            v.tensor_copy(out=srow_f[:], in_=PDq[0:1, 0:Q])
            tok[("srow", h)] = sigV.inc(v.last)

    def kt_tail_t1(blk, h):
        @blk.tensor
        def _(t, h=h):
            t = LastW(t)
            S_.wait(t, tok[("srow", h)])
            S_.wait(t, tok["consts_f"])
            for qb in range(Q // 128):
                t.transpose(PDq[:, qb:qb + 1],
                            srow_f[:, qb * 128:(qb + 1) * 128],
                            ident_f[0:1, 0:1])
            tok[("scol_mm", h)] = sigT.inc(t.last)

    def kt_tail_v2(blk, h):
        @blk.vector
        def _(v, h=h):
            v = LastW(v)
            S_.wait(v, tok[("scol_mm", h)])
            v.reciprocal(out=rsumc_f[:], in_=PDq[:, 0:Q // 128])
            tok[("rsum_done", h)] = sigV.inc(v.last)

    def kt_tail_t2(blk, h):
        @blk.tensor
        def _(t, h=h):
            t = LastW(t)
            S_.wait(t, tok[("rsum_done", h)])
            for qb in range(Q // 128):
                t.transpose(PDq[0:1, qb * 128:(qb + 1) * 128],
                            rsumc_f[:, qb:qb + 1], ident_f[:])
            tok[("rsumT_mm", h)] = sigT.inc(t.last)

    def kt_tail_v3(blk, h):
        @blk.vector
        def _(v, h=h):
            v = LastW(v)
            S_.wait(v, tok[("rsumT_mm", h)])
            v.tensor_copy(out=rsum_b[:], in_=PDq[0:1, 0:Q])
            tok[("rsum8_done", h)] = sigV.inc(v.last)

    def kt_tail_t3(blk, h):
        @blk.tensor
        def _(t, h=h):
            t = LastW(t)
            S_.wait(t, tok[("rsum8_done", h)])
            for qb in range(Q // 128):
                t.matmul(PDq[:, qb * 128:(qb + 1) * 128], ones_row[:],
                         rsum_b[:, qb * 128:(qb + 1) * 128],
                         start=True, stop=True)
            tok[("bcast_mm", h)] = sigT.inc(t.last)

    def kt_tail_v4(blk, h):
        @blk.vector
        def _(v, h=h):
            v = LastW(v)
            S_.wait(v, tok[("bcast_mm", h)])
            v.tensor_copy(out=bcast_sb[:], in_=PDq[:, 0:Q])
            t = sigV.inc(v.last)
            S_.wait(v, t)
            v.tensor_tensor(out=out_all[:, h, :], in0=PCq[:, 0:Q],
                            in1=bcast_sb[:], op=ALU.mult)
            tok[("out_evac", h)] = sigV.inc(v.last)
            tok[("bcast_used", h)] = tok[("out_evac", h)]

    def kt_tail(blk, h):
        kt_tail_v1(blk, h)
        kt_tail_t1(blk, h)
        kt_tail_v2(blk, h)
        kt_tail_t2(blk, h)
        kt_tail_v3(blk, h)
        kt_tail_t3(blk, h)
        kt_tail_v4(blk, h)

    _o, _n = offreg[id(ln1T_own)]
    va2 = Cursor((_o + _n + 31) // 32 * 32, PB)
    QBW2 = QR // nqb
    wqa_buf = [va2.alloc(128, [DK, QBW2], BF16) for _ in range(2)]
    q_raw = va2.alloc(128, [QT, QR], BF16)
    qc_tmp = va2.alloc(128, [QR], BF16)
    psA2 = ExitStack()
    qP = [psA2.enter_context(nc.psum_tensor(f"qP{i}", [128, 512], F32))
          for i in range(2)]

    lastA1 = [("kcT_evac", KT - 1), ("kv_mm", KT - 1), ("lnT_done", KT - 1),
              ("lnapply", KT - 1)]

    with nc.Block() as blk:
        @blk.gpsimd
        def _(g):
            g = LastW(g)
            for sb in range(min(2, nqb)):
                S_.wait_many(g, [tok[k] for k in lastA1 if k in tok])
                tok[("wqa", sb)] = dW[sb % 2].inc(
                    g.dma_start(out=wqa_buf[sb % 2][:], in_=wqa_d[sb, :, :, :]), 16)

    with nc.Block() as blk:
      for sb in range(nqb):
        par = sb % 2
        if True:
            eS0 = sigS.count
            for ti in range(QT):
                tok[("qevac", sb, ti)] = (sigS, eS0 + ti + 1)

            @blk.tensor
            def _(t, sb=sb, par=par):
                t = LastW(t)
                S_.wait(t, tok[("wqa", sb)])
                for ti in range(QT):
                    if ti >= 2:
                        S_.wait(t, tok[("qevac", sb, ti - 2)])
                    S_.wait(t, tok.get(("qevac", sb - 1, QT - 1)))
                    S_.wait(t, tok.get(("qevac", sb - 1, QT - 2)))
                    for k in range(DK):
                        t.matmul(qP[ti % 2][:, 0:QBW2],
                                 ln1T_own[:, ti, k * 128:(k + 1) * 128],
                                 wqa_buf[par][:, k, :],
                                 start=(k == 0), stop=(k == DK - 1))
                    tok[("qmm", sb, ti)] = sigT.inc(t.last)

            @blk.scalar
            def _(s, sb=sb):
                s = LastW(s)
                for ti in range(QT):
                    S_.wait(s, tok[("qmm", sb, ti)])
                    if sb == 0 and ti == 0:
                        S_.wait_many(s, [tok[k] for k in lastA1 if k in tok])
                    s.copy(out=q_raw[:, ti, sb * QBW2:(sb + 1) * QBW2],
                           in_=qP[ti % 2][:, 0:QBW2])
                    tk = sigS.inc(s.last)
                    assert tk == tok[("qevac", sb, ti)]

            @blk.gpsimd
            def _(g, sb=sb, par=par):
                g = LastW(g)
                if sb + 2 < nqb:
                    S_.wait(g, tok[("qmm", sb, QT - 1)])
                    tok[("wqa", sb + 2)] = dW[par].inc(
                        g.dma_start(out=wqa_buf[par][:],
                                    in_=wqa_d[sb + 2, :, :, :]), 16)

      for ti in range(QT):
        par = ti % 2
        if True:
            @blk.vector
            def _(v, ti=ti):
                v = LastW(v)
                for sb in range(nqb):
                    S_.wait(v, tok[("qevac", sb, ti)])
                px = ti % 2
                nch = max(1, QR // 512)
                cw = QR // nch
                for c_ in range(nch):
                    v.bn_stats(out=stats[px][:, 6 * c_:6 * c_ + 6],
                               in_=q_raw[:, ti, c_ * cw:(c_ + 1) * cw])
                t = sigV.inc(v.last)
                S_.wait(v, t)
                v.bn_aggr(out=mv[px][:], in_=stats[px][:, 0:6 * nch])
                t = sigV.inc(v.last)
                S_.wait(v, t)
                v.tensor_scalar(out=col_c[px][:], in0=mv[px][:, 0:1], scalar1=mv[px][:, 0:1],
                                scalar2=None, op0=ALU.mult)
                t = sigV.inc(v.last)
                S_.wait(v, t)
                v.tensor_tensor(out=col_c[px][:], in0=col_c[px][:], in1=mv[px][:, 1:2],
                                op=ALU.add)
                tok[("q_ms", ti)] = sigV.inc(v.last)

            @blk.vector
            def _(v, ti=ti):
                v = LastW(v)
                S_.wait(v, tok[("q_ms", ti)])
                v_rsqrt(v, ti % 2, col_c[ti % 2][:], 1e-30)
                tok[("q_sq", ti)] = sigV.inc(v.last)

            @blk.vector
            def _(v, ti=ti):
                v = LastW(v)
                S_.wait(v, tok[("q_sq", ti)])
                px = ti % 2
                v.tensor_tensor(out=col_a[px][:], in0=col_a[px][:],
                                in1=col_c[px][:], op=ALU.mult)
                t = sigV.inc(v.last)
                S_.wait(v, t)
                v.tensor_scalar(out=col_a[px][:], in0=col_a[px][:], scalar1=RMS_EPS,
                                scalar2=None, op0=ALU.add)
                t = sigV.inc(v.last)
                S_.wait(v, t)
                v.reciprocal(out=col_a[px][:], in_=col_a[px][:])
                t = sigV.inc(v.last)
                S_.wait(v, t)
                S_.wait(v, tok.get(("qcT_mm", ti - 1)))
                v.tensor_scalar(out=qc_tmp[:], in0=q_raw[:, ti, :],
                                scalar1=col_a[px][:], scalar2=None, op0=ALU.mult)
                tok[("qc_rdy", ti)] = sigV.inc(v.last)

            @blk.tensor
            def _(t, ti=ti, par=par):
                t = LastW(t)
                S_.wait(t, tok[("qc_rdy", ti)])
                S_.wait(t, tok.get(("qcT_evac", ti - 2)))
                if ti == 0:
                    S_.wait(t, tok[("lnapply", KT - 1)])
                for k in range(QRK):
                    t.transpose(trP[par][:, k * 128:(k + 1) * 128],
                                qc_tmp[:, k * 128:(k + 1) * 128], ident_b[:])
                tok[("qcT_mm", ti)] = sigT.inc(t.last)

            @blk.vector
            def _(v, ti=ti, par=par):
                v = LastW(v)
                S_.wait(v, tok[("qcT_mm", ti)])
                S_.wait(v, tok["consts_f"])
                for k in range(QRK):
                    v.tensor_scalar(out=q_cT[:, k, ti * 128:(ti + 1) * 128],
                                    in0=trP[par][:, k * 128:(k + 1) * 128],
                                    scalar1=gqa_b[:, k:k + 1], scalar2=None,
                                    op0=ALU.mult)
                tok[("qcT_evac", ti)] = sigV.inc(v.last)

      b_overlap_ok = False

    if dbg:
        dbg_kcT = nc.dram_tensor("dbg_kcT", [128, KVK * S], BF16, kind="ExternalOutput")
        dbg_krT = nc.dram_tensor("dbg_krT", [RD, S], BF16, kind="ExternalOutput")
        dbg_qcT = nc.dram_tensor("dbg_qcT", [128, QRK * Q], BF16, kind="ExternalOutput")
        with nc.Block() as blk:
            @blk.sync
            def _(sy):
                S_.wait(sy, tok[("kcT_evac", KT - 1)])
                S_.wait(sy, tok[("qcT_evac", QT - 1)])
                sy.dma_start(out=dbg_kcT[:], in_=k_cT.rearrange("p a b -> p (a b)")).then_inc(dO[0].sem, 16)
                sy.dma_start(out=dbg_krT[:], in_=k_ropeT[:]).then_inc(dO[0].sem, 16)
                sy.dma_start(out=dbg_qcT[:], in_=q_cT.rearrange("p a b -> p (a b)")).then_inc(dO[0].sem, 16)
                dO[0].count += 48
                sy.wait_ge(dO[0].sem, dO[0].count)
    psA2.close()
    psA.close()

    psB = ExitStack()
    sPP = [psB.enter_context(nc.psum_tensor(f"sPP{i}", [128, 512], F32))
           for i in range(2)]
    PCq = psB.enter_context(nc.psum_tensor("PCq", [128, 1024], F32))
    PDq = psB.enter_context(nc.psum_tensor("PDq", [128, 1024], F32))
    PEp = [psB.enter_context(nc.psum_tensor(f"PEp{i}", [128, 512], F32))
           for i in range(2)]
    with nc.Block() as blk:
        b_preamble(blk)
        b_prep(blk, 0)
    for h in range(NH):
        with nc.Block() as blk:
            if h % 2 == 1 and (h + 3) // 2 < NP:
                b_prefetch(blk, (h + 3) // 2)
            if h >= 1:
                kt_tail_v1(blk, h - 1)
            if h + 1 < NH:
                b_prep_a(blk, h + 1)
            if h >= 1:
                kt_tail_t1(blk, h - 1)
                kt_tail_v2(blk, h - 1)
            if h + 1 < NH:
                b_prep_b1(blk, h + 1)
            if h >= 1:
                kt_tail_t2(blk, h - 1)
                kt_tail_v3(blk, h - 1)
                kt_tail_t3(blk, h - 1)
                kt_tail_v4(blk, h - 1)
            if h + 1 < NH:
                b_prep_b2(blk, h + 1)
            kt_main(blk, h)
    with nc.Block() as blk:
        kt_tail(blk, NH - 1)

    if dbg:
        dbg_oall = nc.dram_tensor("dbg_oall", [128, NH * Q], BF16, kind="ExternalOutput")
        with nc.Block() as blk:
            @blk.sync
            def _(sy):
                S_.wait(sy, tok[("out_evac", NH - 1)])
                sy.dma_start(out=dbg_oall[:], in_=out_all.rearrange("p a b -> p (a b)")).then_inc(dO[0].sem, 16)
                dO[0].count += 16
                sy.wait_ge(dO[0].sem, dO[0].count)
    psB.close()

    x1_sb = view(V0, 128, [QT, D], BF16)
    _oa0, _oan = offreg[id(out_all)]
    zc = Cursor(ZLOW, V0)
    zgap = Cursor(V0 + QT * D * 2, _oa0)

    def calloc(part, shape, dt):
        esz = 2 if dt == BF16 else 4
        n = 1
        for s in shape:
            n *= s
        if zc.off + n * esz <= zc.limit:
            return zc.alloc(part, shape, dt)
        return zgap.alloc(part, shape, dt)

    wo_buf = [calloc(128, [KHN, OBW], BF16) for _ in range(2)]
    xr_buf0 = calloc(128, [D], F32)
    xr_buf = [xr_buf0, xr_buf0]
    ln3st = calloc(128, [QT, 6 * NOB], F32)
    mvv3 = calloc(128, [QT, 2], F32)
    rstd8 = calloc(128, [QT], F32)
    negmb8 = calloc(128, [QT], F32)
    rs8x = calloc(128, [QT], F32)
    rs8t = calloc(128, [QT], F32)
    magic8 = calloc(128, [QT], F32)
    t1_buf = calloc(128, [D], BF16)
    lastB = [("out_evac", NH - 1), ("bcast_used", NH - 1)]

    psC = ExitStack()
    oP = [psC.enter_context(nc.psum_tensor(f"oP{i}", [128, OBW], F32))
          for i in range(QT)]

    cpairs = [(b, kh) for b in range(NOB) for kh in range(KH)]

    woB = [dW[0].count, dW[1].count]
    for pi in range(len(cpairs)):
        tok[("wo", pi)] = (dW[pi % 2], woB[pi % 2] + 16 * (pi // 2 + 1))
    cS0 = sigS.count
    for b in range(NOB):
        for ti in range(QT):
            tok[("o_evac", b, ti)] = (sigS, cS0 + b * QT + ti + 1)
    aS0 = cS0 + NOB * QT
    for ti in range(QT):
        tok[("ln3ap", ti)] = (sigS, aS0 + ti + 1)
    xrB = [dX[0].count, dX[1].count]
    for ti in range(QT):
        tok[("xr", ti)] = (dX[ti % 2], xrB[ti % 2] + 16 * (ti // 2 + 1))

    with nc.Block() as blk:
        @blk.tensor
        def _(t):
            t = LastW(t)
            for pi, (b, kh) in enumerate(cpairs):
                S_.wait(t, tok[("wo", pi)])
                if kh == 0:
                    for ti in range(QT):
                        S_.wait(t, tok.get(("o_evac", b - 1, ti)))
                for k2 in range(KHN):
                    k = kh * KHN + k2
                    for ti in range(QT):
                        t.matmul(oP[ti][:],
                                 out_all[:, k, ti * 128:(ti + 1) * 128],
                                 wo_buf[pi % 2][:, k2, :],
                                 start=(k == 0), stop=(k == NH - 1))
                tok[("o_mmp", pi)] = sigT.inc(t.last)
                if kh == KH - 1:
                    tok[("o_mm", b)] = tok[("o_mmp", pi)]

        @blk.vector
        def _(v):
            v = LastW(v)
            v.memset(magic8[:].bitcast(mybir.dt.int32), 0x5F3759DF)
            for b in range(NOB):
                for ti in range(QT):
                    S_.wait(v, tok[("o_evac", b, ti)])
                    v.bn_stats(out=ln3st[:, ti, 6 * b:6 * b + 6],
                               in_=x1_sb[:, ti, b * OBW:(b + 1) * OBW])
            t = sigV.inc(v.last)
            S_.wait(v, t)
            for ti in range(QT):
                v.bn_aggr(out=mvv3[:, ti, :], in_=ln3st[:, ti, :])
            t = sigV.inc(v.last)
            S_.wait(v, t)
            mvf = mvv3.rearrange("p a b -> p (a b)")
            v_rsqrtN(v, mvf[:, 1:2 * QT:2], rstd8[:], rs8x[:], rs8t[:],
                     magic8[:], LN_EPS)
            t = sigV.inc(v.last)
            S_.wait(v, t)
            v.tensor_tensor(out=negmb8[:], in0=mvf[:, 0:2 * QT:2],
                            in1=rstd8[:], op=ALU.mult)
            t = sigV.inc(v.last)
            S_.wait(v, t)
            v.tensor_scalar(out=negmb8[:], in0=negmb8[:], scalar1=-1.0,
                            scalar2=None, op0=ALU.mult)
            tok["ln3rdy"] = sigV.inc(v.last)
            for ti in range(QT):
                par = ti % 2
                S_.wait(v, tok[("ln3ap", ti)])
                S_.wait(v, tok[("xr", ti)])
                v.tensor_tensor(out=x1_sb[:, ti, :], in0=t1_buf[:],
                                in1=xr_buf[par][:], op=ALU.add)
                tok[("x1_done", ti)] = sigV.inc(v.last)

        @blk.scalar
        def _(s):
            s = LastW(s)
            for b in range(NOB):
                S_.wait(s, tok[("o_mm", b)])
                if b == 0:
                    S_.wait_many(s, [tok[k] for k in lastB])
                for ti in range(QT):
                    s.copy(out=x1_sb[:, ti, b * OBW:(b + 1) * OBW],
                           in_=oP[ti][:])
                    tk = sigS.inc(s.last)
                    assert tk == tok[("o_evac", b, ti)]
            for ti in range(QT):
                S_.wait(s, tok["ln3rdy"])
                S_.wait(s, tok.get(("x1_done", ti - 1)))
                s.activation(out=t1_buf[:], in_=x1_sb[:, ti, :],
                             func=AF.Identity,
                             scale=rstd8[:, ti:ti + 1],
                             bias=negmb8[:, ti:ti + 1])
                tk = sigS.inc(s.last)
                assert tk == tok[("ln3ap", ti)]

        @blk.gpsimd
        def _(g):
            g = LastW(g)
            S_.wait_many(g, [tok[k] for k in lastB])
            for pi, (b, kh) in enumerate(cpairs):
                if pi >= 2:
                    S_.wait(g, tok[("o_mmp", pi - 2)])
                tk = dW[pi % 2].inc(
                    g.dma_start(out=wo_buf[pi % 2][:],
                                in_=wo_d[b, :, kh * KHN:(kh + 1) * KHN, :]), 16)
                assert tk == tok[("wo", pi)]

        @blk.sync
        def _(sy):
            sy = LastW(sy)
            S_.wait_many(sy, [tok[k] for k in lastB])
            for ti in range(QT):
                S_.wait(sy, tok.get(("x1_done", ti - 1)))
                tk = dX[ti % 2].inc(
                    sy.dma_start(out=xr_buf[ti % 2][:], in_=xv[ti, :, :]), 16)
                assert tk == tok[("xr", ti)]

    if dbg:
        dbg_o = nc.dram_tensor("dbg_o", [128, QT * D], BF16, kind="ExternalOutput")
        dbg_x1 = nc.dram_tensor("dbg_x1", [128, QT * D], BF16, kind="ExternalOutput")
        with nc.Block() as blk:
            @blk.sync
            def _(sy):
                S_.wait(sy, tok[("x1_done", QT - 1)])
                sy.dma_start(out=dbg_x1[:], in_=x1_sb.rearrange("p a b -> p (a b)")).then_inc(dO[0].sem, 16)
                dO[0].count += 16
                sy.wait_ge(dO[0].sem, dO[0].count)
    psC.close()

    zd = Cursor(ZLOW, V0)
    zd2 = Cursor(V0 + QT * D * 2, PB)

    def dalloc(part, shape, dt):
        esz = 2 if dt == BF16 else 4
        n = 1
        for s in shape:
            n *= s
        if zd.off + n * esz <= zd.limit:
            return zd.alloc(part, shape, dt)
        return zd2.alloc(part, shape, dt)

    ln2T_g = dalloc(128, [DK, GT * 128], BF16)
    at_b = dalloc(128, [DK, ER], BF16)
    bflat_b = dalloc(ER, [DO], BF16)
    wrt_b = dalloc(128, [DK, NE], BF16)
    bbase_b = dalloc(1, [DO], BF16)
    lgB = dalloc(128, [GT, NE], F32)
    mskB = dalloc(128, [GT, NE], F32)
    explB = dalloc(128, [GT, NE], F32)
    m1B = dalloc(128, [GT], F32)
    m2B = dalloc(128, [GT], F32)
    rcolA = dalloc(128, [GT], F32)
    rcolB = dalloc(128, [GT], F32)
    colcB = dalloc(128, [GT], F32)
    comb_g = dalloc(128, [GT, NE], BF16)
    combT_sb = dalloc(NE, [GT * 128], BF16)
    gate_sb = dalloc(128, [GT * 128], BF16)
    hg_sb = dalloc(128, [GT * 128], BF16)
    wbt_buf = [zd2.alloc(128, [DKHN, FBW], BF16) for _ in range(2)]
    ffn_g = zd2.alloc(128, [GT, DO], BF16)
    ln2_tmp = dalloc(128, [D], BF16)
    ln4_tmp = dalloc(128, [DO], BF16)
    och_buf = [dalloc(128, [min(D, 512)], F32) for _ in range(2)]
    ln4st = dalloc(128, [GT, 6 * FOB], F32)
    mvv4 = dalloc(128, [GT, 2], F32)
    rstd4 = dalloc(128, [GT], F32)
    negmb4 = dalloc(128, [GT], F32)
    rs4x = dalloc(128, [GT], F32)
    rs4t = dalloc(128, [GT], F32)
    magic4 = dalloc(128, [GT], F32)
    assert zd2.off <= PB

    lastC = [("x1_done", QT - 1)]
    with nc.Block() as blk:
        @blk.gpsimd
        def _(g):
            g = LastW(g)
            S_.wait_many(g, [tok[k] for k in lastC])
            dW2[0].inc(g.dma_start(out=at_b[:], in_=at_d[:]), 16)
            dW2[0].inc(g.dma_start(out=bflat_b[:], in_=bflat_d[:]), 16)
            dW2[0].inc(g.dma_start(out=wrt_b[:], in_=wrt_d[:]), 16)
            tok["ffn_w"] = dW2[0].inc(
                g.dma_start(out=bbase_b[:], in_=bbase_d[:]), 16)

    ov = out_d.rearrange("(t p) d -> t p d", p=128)
    OCH = max(1, D // 512)
    ocw = D // OCH
    nw = (DK + 3) // 4
    dpairs = [(b, kh) for b in range(FOB) for kh in range(DKH)]

    def d_tail_sections(blk, grp, want_dbg):
        @blk.vector
        def _(v, grp=grp):
            v = LastW(v)
            for tg in range(GT):
                ti = grp * GT + tg
                S_.wait(v, tok[("ln4ap", ti)])
                dOb = [dO[0].count, dO[1].count]
                for c_ in range(OCH):
                    cpar = c_ % 2
                    if c_ >= 2:
                        S_.wait(v, (dO[cpar], dOb[cpar] + 16 * ((c_ - 2) // 2 + 1)))
                    else:
                        S_.wait(v, tok.get(("odma", ti - 1, OCH - 1)))
                        S_.wait(v, tok.get(("odma", ti - 1, OCH - 2)))
                    v.tensor_tensor(out=och_buf[cpar][:, 0:ocw],
                                    in0=ln4_tmp[:, c_ * ocw:(c_ + 1) * ocw],
                                    in1=x1_sb[:, ti, c_ * ocw:(c_ + 1) * ocw],
                                    op=ALU.add)
                    tok[("och", ti, c_)] = sigV.inc(v.last)
                for c_ in range(OCH):
                    tok[("odma", ti, c_)] = (dO[c_ % 2],
                                             dOb[c_ % 2] + 16 * (c_ // 2 + 1))
                    dO[c_ % 2].count += 16

        @blk.scalar
        def _(s, grp=grp):
            s = LastW(s)
            for tg in range(GT):
                ti = grp * GT + tg
                S_.wait(s, tok[("ln4rdy", grp)])
                if want_dbg:
                    S_.wait(s, tok[("dbgf", grp)])
                S_.wait(s, tok.get(("och", ti - 1, OCH - 1)))
                s.activation(out=ln4_tmp[:], in_=ffn_g[:, tg, :],
                             func=AF.Identity,
                             scale=rstd4[:, tg:tg + 1],
                             bias=negmb4[:, tg:tg + 1])
                tk = sigS.inc(s.last)
                assert tk == tok[("ln4ap", ti)], (tk, tok[("ln4ap", ti)])

        @blk.sync
        def _(sy, grp=grp):
            sy = LastW(sy)
            for tg in range(GT):
                ti = grp * GT + tg
                for c_ in range(OCH):
                    cpar = c_ % 2
                    S_.wait(sy, tok[("och", ti, c_)])
                    inst = sy.dma_start(out=ov[ti, :, c_ * ocw:(c_ + 1) * ocw],
                                        in_=och_buf[cpar][:, 0:ocw])
                    inst.then_inc(dO[cpar].sem, 16)
                    assert tok[("odma", ti, c_)][0] is dO[cpar]

    for grp in range(DGRP):
        psD = ExitStack()
        trD = [psD.enter_context(nc.psum_tensor(f"trD{grp}{i}", [128, 512], BF16))
               for i in range(2)]
        lgP = psD.enter_context(nc.psum_tensor(f"lgP{grp}", [128, NE], F32))
        hP = psD.enter_context(nc.psum_tensor(f"hP{grp}", [128, GT * 128], F32))
        gateP = psD.enter_context(nc.psum_tensor(f"gateP{grp}", [128, GT * 128], F32))

        with nc.Block() as blk:
            pS0 = sigS.count
            off = 0
            if grp > 0:
                for tg in range(GT):
                    tok[("ln4ap", (grp - 1) * GT + tg)] = (sigS, pS0 + tg + 1)
                off = GT
            for tg in range(GT):
                ti = grp * GT + tg
                base = pS0 + off + tg * (nw + 1)
                tok[("ln2a", ti)] = (sigS, base + 1)
                for wv in range(nw):
                    tok[("ln2T_ev", ti, wv)] = (sigS, base + 2 + wv)
                tok[("ln2T_done", ti)] = tok[("ln2T_ev", ti, nw - 1)]
            tok[("tk_exp", grp)] = (sigS, pS0 + off + GT * (nw + 1) + 1)
            pT0 = sigT.count
            for tg in range(GT):
                ti = grp * GT + tg
                tb = pT0 + tg * nw
                for wv in range(nw):
                    tok[("ln2T_mm", ti, wv)] = (sigT, tb + wv + 1)
                tok[("ln2T_mm_all", ti)] = tok[("ln2T_mm", ti, nw - 1)]
                tok[("lg_mm", ti)] = (sigT, pT0 + GT * nw + tg + 1)
            tb2 = pT0 + GT * (nw + 1)
            for tg in range(GT):
                tok[("combT_mm", grp * GT + tg)] = (sigT, tb2 + tg + 1)
            tok[("gate_mm", grp)] = (sigT, tb2 + GT + 1)
            tok[("h_mm", grp)] = (sigT, tb2 + GT + 2)

            if grp > 0:
                d_tail_sections(blk, grp - 1, dbg)

            @blk.vector
            def _(v, grp=grp):
                v = LastW(v)
                for tg in range(GT):
                    ti = grp * GT + tg
                    par = ti % 2
                    S_.wait(v, tok[("x1_done", ti)])
                    if grp > 0:
                        S_.wait(v, tok[("ln4ap", grp * GT - 1)])
                    S_.wait(v, tok.get(("ln2a", ti - 2)))
                    ln_stats(v, x1_sb[:, ti, :], D, ("ln2s", ti), par)
                    S_.wait(v, tok[("ln2s", ti)])
                    v_rsqrt(v, par, mv[par][:, 1:2], LN_EPS)
                    tok[("ln2sq", ti)] = sigV.inc(v.last)
                    tok[("ln2fin", ti)] = ln_finish(v, ("ln2sq", ti), par)
                for tg in range(GT):
                    ti = grp * GT + tg
                    S_.wait(v, tok[("lg_mm", ti)])
                    v.tensor_copy(out=lgB[:, tg, :], in_=lgP[:])
                    tok[("lg_used", ti)] = sigV.inc(v.last)
                S_.wait(v, tok[("lg_used", grp * GT + GT - 1)])
                v.reduce_max(out=m1B[:], in_=lgB[:], axis=AX.X)
                t = sigV.inc(v.last)
                S_.wait(v, t)
                for tg in range(GT):
                    v.tensor_scalar(out=mskB[:, tg, :], in0=lgB[:, tg, :],
                                    scalar1=m1B[:, tg:tg + 1],
                                    scalar2=None, op0=ALU.is_equal)
                t = sigV.inc(v.last)
                S_.wait(v, t)
                v.scalar_tensor_tensor(out=mskB[:], in0=mskB[:],
                                       scalar=-1e30, in1=lgB[:],
                                       op0=ALU.mult, op1=ALU.add)
                t = sigV.inc(v.last)
                S_.wait(v, t)
                v.reduce_max(out=m2B[:], in_=mskB[:], axis=AX.X)
                t = sigV.inc(v.last)
                S_.wait(v, t)
                v.tensor_scalar(out=rcolA[:], in0=m1B[:], scalar1=-1.0,
                                scalar2=None, op0=ALU.mult)
                t = sigV.inc(v.last)
                S_.wait(v, t)
                v.tensor_tensor(out=rcolB[:], in0=m2B[:], in1=rcolA[:],
                                op=ALU.add)
                tok[("tk1", grp)] = sigV.inc(v.last)
                S_.wait(v, tok[("tk_exp", grp)])
                v.tensor_scalar(out=colcB[:], in0=colcB[:],
                                scalar1=1.0, scalar2=None, op0=ALU.add)
                t = sigV.inc(v.last)
                S_.wait(v, t)
                v.reciprocal(out=colcB[:], in_=colcB[:])
                t = sigV.inc(v.last)
                S_.wait(v, t)
                for tg in range(GT):
                    v.tensor_scalar(out=mskB[:, tg, :], in0=lgB[:, tg, :],
                                    scalar1=m2B[:, tg:tg + 1],
                                    scalar2=None, op0=ALU.is_ge)
                t = sigV.inc(v.last)
                S_.wait(v, t)
                v.tensor_tensor(out=explB[:], in0=explB[:], in1=mskB[:],
                                op=ALU.mult)
                t = sigV.inc(v.last)
                S_.wait(v, t)
                for tg in range(GT):
                    ti = grp * GT + tg
                    v.tensor_scalar(out=comb_g[:, tg, :], in0=explB[:, tg, :],
                                    scalar1=colcB[:, tg:tg + 1], scalar2=2.0,
                                    op0=ALU.mult, op1=ALU.mult)
                    tok[("comb", ti)] = sigV.inc(v.last)

            vV0 = sigV.count
            for tg in range(GT):
                tok[("combT_ev", grp * GT + tg)] = (sigV, vV0 + tg + 1)
            tok[("gate_ev", grp)] = (sigV, vV0 + GT + 1)
            tok[("hg", grp)] = (sigV, vV0 + GT + 2)

            @blk.tensor
            def _(t, grp=grp):
                t = LastW(t)
                for tg in range(GT):
                    ti = grp * GT + tg
                    S_.wait(t, tok[("ln2a", ti)])
                    for wv in range(nw):
                        if wv >= 2:
                            S_.wait(t, tok[("ln2T_ev", ti, wv - 2)])
                        elif tg > 0:
                            S_.wait(t, tok[("ln2T_ev", ti - 1, nw - 2 + wv)])
                        for k in range(wv * 4, min(wv * 4 + 4, DK)):
                            t.transpose(trD[wv % 2][:, (k % 4) * 128:(k % 4 + 1) * 128],
                                        ln2_tmp[:, k * 128:(k + 1) * 128],
                                        ident_b[:])
                        tk = sigT.inc(t.last)
                        assert tk == tok[("ln2T_mm", ti, wv)]
                for tg in range(GT):
                    ti = grp * GT + tg
                    S_.wait(t, tok[("ln2T_done", ti)])
                    S_.wait(t, tok["ffn_w"])
                    S_.wait(t, tok.get(("lg_used", ti - 1)))
                    for k in range(DK):
                        t.matmul(lgP[:], ln2T_g[:, k, tg * 128:(tg + 1) * 128],
                                 wrt_b[:, k, :], start=(k == 0), stop=(k == DK - 1))
                    tk = sigT.inc(t.last)
                    assert tk == tok[("lg_mm", ti)]

            @blk.tensor
            def _(t, grp=grp):
                t = LastW(t)
                S_.wait(t, tok[("ln2T_ev", grp * GT + GT - 1, nw - 1)])
                for tg in range(GT):
                    ti = grp * GT + tg
                    S_.wait(t, tok[("comb", ti)])
                    S_.wait(t, tok.get(("combT_ev", ti - 1)))
                    t.transpose(trD[0][0:NE, 0:128], comb_g[:, tg, :], ident_b[:])
                    tk = sigT.inc(t.last)
                    assert tk == tok[("combT_mm", ti)]
                for tg in range(GT):
                    S_.wait(t, tok[("combT_ev", grp * GT + tg)])
                t.matmul(gateP[:, 0:GT * 128], emat_b[:], combT_sb[:],
                         start=True, stop=True)
                tk = sigT.inc(t.last)
                assert tk == tok[("gate_mm", grp)]
                for (g0, gw) in [(s0, min(512, GT * 128 - s0))
                                 for s0 in range(0, GT * 128, 512)]:
                    for k in range(DK):
                        t.matmul(hP[:, g0:g0 + gw], at_b[:, k, :],
                                 ln2T_g[:, k, g0:g0 + gw],
                                 start=(k == 0), stop=(k == DK - 1))
                tk = sigT.inc(t.last)
                assert tk == tok[("h_mm", grp)]

            @blk.vector
            def _(v, grp=grp):
                v = LastW(v)
                for tg in range(GT):
                    ti = grp * GT + tg
                    S_.wait(v, tok[("combT_mm", ti)])
                    v.tensor_copy(out=combT_sb[:, tg * 128:(tg + 1) * 128],
                                  in_=trD[0][0:NE, 0:128])
                    tk = sigV.inc(v.last)
                    assert tk == tok[("combT_ev", ti)]
                S_.wait(v, tok[("gate_mm", grp)])
                v.tensor_copy(out=gate_sb[:], in_=gateP[:, 0:GT * 128])
                tk = sigV.inc(v.last)
                assert tk == tok[("gate_ev", grp)]
                S_.wait(v, tk)
                S_.wait(v, tok[("h_mm", grp)])
                v.tensor_tensor(out=hg_sb[:], in0=hP[:, 0:GT * 128],
                                in1=gate_sb[:], op=ALU.mult)
                tk = sigV.inc(v.last)
                assert tk == tok[("hg", grp)]

            @blk.scalar
            def _(s, grp=grp):
                s = LastW(s)
                for tg in range(GT):
                    ti = grp * GT + tg
                    par = ti % 2
                    S_.wait(s, tok[("ln2fin", ti)])
                    S_.wait(s, tok.get(("ln2T_mm_all", ti - 1)))
                    s.activation(out=ln2_tmp[:], in_=x1_sb[:, ti, :],
                                 func=AF.Identity,
                                 scale=col_a[par][:], bias=col_b[par][:])
                    tk = sigS.inc(s.last)
                    assert tk == tok[("ln2a", ti)]
                    for wv in range(nw):
                        S_.wait(s, tok[("ln2T_mm", ti, wv)])
                        nk = min(4, DK - wv * 4)
                        for k2 in range(nk):
                            s.copy(out=ln2T_g[:, wv * 4 + k2, tg * 128:(tg + 1) * 128],
                                   in_=trD[wv % 2][:, k2 * 128:(k2 + 1) * 128])
                        tk = sigS.inc(s.last)
                        assert tk == tok[("ln2T_ev", ti, wv)]
                S_.wait(s, tok[("tk1", grp)])
                for tg in range(GT):
                    s.activation(out=explB[:, tg, :], in_=lgB[:, tg, :],
                                 func=AF.Exp, bias=rcolA[:, tg:tg + 1],
                                 scale=1.0)
                s.activation(out=colcB[:], in_=rcolB[:], func=AF.Exp)
                tk = sigS.inc(s.last)
                assert tk == tok[("tk_exp", grp)]

            @blk.gpsimd
            def _(g, grp=grp):
                g = LastW(g)
                wbB = [dW[0].count, dW[1].count]
                for pi in range(min(2, len(dpairs))):
                    b, kh = dpairs[pi]
                    S_.wait(g, tok.get(("wbt_used", grp - 1)))
                    if grp == 0 and pi == 0:
                        S_.wait(g, tok.get(("x1_done", QT - 1)))
                    tok[("wbt", grp, pi)] = dW[pi % 2].inc(
                        g.dma_start(out=wbt_buf[pi % 2][:],
                                    in_=wbt_d[b, :, kh * DKHN:(kh + 1) * DKHN, :]), 16)

        psD.close()

        psD2 = ExitStack()
        fP = [psD2.enter_context(nc.psum_tensor(f"fP{grp}{i}", [128, FBW], F32))
              for i in range(GT)]

        mS0 = sigS.count
        for b in range(FOB):
            for tg in range(GT):
                tok[("f_evac", grp, b, tg)] = (sigS, mS0 + b * GT + tg + 1)
        wbB = [dW[0].count, dW[1].count]
        for pi in range(2, len(dpairs)):
            tok[("wbt", grp, pi)] = (dW[pi % 2], wbB[pi % 2] + 16 * ((pi - 2) // 2 + 1))

        with nc.Block() as blk:
            @blk.tensor
            def _(t, grp=grp):
                t = LastW(t)
                S_.wait(t, tok[("hg", grp)])
                S_.wait(t, tok["ffn_w"])
                for pi, (b, kh) in enumerate(dpairs):
                    S_.wait(t, tok[("wbt", grp, pi)])
                    if kh == 0:
                        for tg in range(GT):
                            S_.wait(t, tok.get(("f_evac", grp, b - 1, tg)))
                    for k2 in range(DKHN):
                        k = kh * DKHN + k2
                        for tg in range(GT):
                            t.matmul(fP[tg][:],
                                     ln2T_g[:, k, tg * 128:(tg + 1) * 128],
                                     wbt_buf[pi % 2][:, k2, :],
                                     start=(k == 0), stop=False)
                    if kh == DKH - 1:
                        for tg in range(GT):
                            t.matmul(fP[tg][:], hg_sb[:, tg * 128:(tg + 1) * 128],
                                     bflat_b[:, b * FBW:(b + 1) * FBW],
                                     start=False, stop=False)
                            t.matmul(fP[tg][:], ones_row[:],
                                     bbase_b[:, b * FBW:(b + 1) * FBW],
                                     start=False, stop=True)
                    tok[("f_mmp", grp, pi)] = sigT.inc(t.last)
                    if kh == DKH - 1:
                        tok[("f_mm", grp, b)] = tok[("f_mmp", grp, pi)]
                tok[("wbt_used", grp)] = tok[("f_mmp", grp, len(dpairs) - 1)]

            @blk.scalar
            def _(s, grp=grp):
                s = LastW(s)
                for b in range(FOB):
                    S_.wait(s, tok[("f_mm", grp, b)])
                    for tg in range(GT):
                        s.copy(out=ffn_g[:, tg, b * FBW:(b + 1) * FBW],
                               in_=fP[tg][:])
                        tk = sigS.inc(s.last)
                        assert tk == tok[("f_evac", grp, b, tg)]

            @blk.gpsimd
            def _(g, grp=grp):
                g = LastW(g)
                for pi in range(2, len(dpairs)):
                    b2, kh2 = dpairs[pi]
                    S_.wait(g, tok[("f_mmp", grp, pi - 2)])
                    tk = dW[pi % 2].inc(
                        g.dma_start(out=wbt_buf[pi % 2][:],
                                    in_=wbt_d[b2, :, kh2 * DKHN:(kh2 + 1) * DKHN, :]), 16)
                    assert tk == tok[("wbt", grp, pi)]

            @blk.vector
            def _(v, grp=grp):
                v = LastW(v)
                v.memset(magic4[:].bitcast(mybir.dt.int32), 0x5F3759DF)
                for b in range(FOB):
                    for tg in range(GT):
                        S_.wait(v, tok[("f_evac", grp, b, tg)])
                        v.bn_stats(out=ln4st[:, tg, 6 * b:6 * b + 6],
                                   in_=ffn_g[:, tg, b * FBW:(b + 1) * FBW])
                t = sigV.inc(v.last)
                S_.wait(v, t)
                for tg in range(GT):
                    v.bn_aggr(out=mvv4[:, tg, :], in_=ln4st[:, tg, :])
                t = sigV.inc(v.last)
                S_.wait(v, t)
                mvf4 = mvv4.rearrange("p a b -> p (a b)")
                v_rsqrtN(v, mvf4[:, 1:2 * GT:2], rstd4[:], rs4x[:], rs4t[:],
                         magic4[:], LN_EPS)
                t = sigV.inc(v.last)
                S_.wait(v, t)
                v.tensor_tensor(out=negmb4[:], in0=mvf4[:, 0:2 * GT:2],
                                in1=rstd4[:], op=ALU.mult)
                t = sigV.inc(v.last)
                S_.wait(v, t)
                v.tensor_scalar(out=negmb4[:], in0=negmb4[:], scalar1=-1.0,
                                scalar2=None, op0=ALU.mult)
                tok[("ln4rdy", grp)] = sigV.inc(v.last)

            if grp == DGRP - 1:
                pS0f = sigS.count
                for tg in range(GT):
                    tok[("ln4ap", grp * GT + tg)] = (sigS, pS0f + tg + 1)
                d_tail_sections(blk, grp, False)
        psD2.close()

        if dbg:
            dbg_f = nc.dram_tensor(f"dbg_f{grp}", [128, GT * DO], BF16,
                                   kind="ExternalOutput")
            with nc.Block() as blk:
                @blk.sync
                def _(sy, grp=grp, dbg_f=dbg_f):
                    for b in range(FOB):
                        for tg in range(GT):
                            S_.wait(sy, tok[("f_evac", grp, b, tg)])
                    sy.dma_start(out=dbg_f[:], in_=ffn_g.rearrange("p a b -> p (a b)")).then_inc(dO[0].sem, 16)
                    dO[0].count += 16
                    sy.wait_ge(dO[0].sem, dO[0].count)
                    tok[("dbgf", grp)] = (dO[0], dO[0].count)

    with nc.Block() as blk:
        @blk.sync
        def _(sy):
            sy.wait_ge(dO[0].sem, dO[0].count)
            sy.wait_ge(dO[1].sem, dO[1].count)

    nc._layout = {"ZLOW": ZLOW, "V0": V0,
                  "ln2T_g": offreg[id(ln2T_g)], "at_b": offreg[id(at_b)],
                  "x1_sb": offreg[id(x1_sb)], "ln2_tmp": offreg[id(ln2_tmp)],
                  "t1_buf": offreg[id(t1_buf)]}
    ctx.close()
    return nc


def make_in_maps(cfg, inputs, n_cores=None):
    w = dict(
        w_qa=inputs["L0_attn_w_qa"], g_qa=inputs["L0_attn_g_qa"],
        w_qb=inputs["L0_attn_w_qb"], w_kva=inputs["L0_attn_w_kva"],
        g_kva=inputs["L0_attn_g_kva"], w_kvb=inputs["L0_attn_w_kvb"],
        w_o=inputs["L0_attn_w_o"], W_base=inputs["L0_ffn_W_base"],
        b_base=inputs["L0_ffn_b_base"], W_router=inputs["L0_ffn_W_router"],
        A_stack=inputs["L0_ffn_A_stack"], B_stack=inputs["L0_ffn_B_stack"],
    )
    wp = prep_weights(cfg, w)
    x = np.asarray(inputs["L0_attn_hidden_states"])
    Q, S, RD = cfg["Q"], cfg["S"], cfg["RD"]
    halves = S // Q
    if n_cores is None:
        n_cores = x.shape[0] * halves
    c = np.ascontiguousarray
    in_maps = []
    for ci in range(n_cores):
        b, h = divmod(ci, halves)
        xf = x[b]
        perm = np.concatenate([np.arange(h * Q, (h + 1) * Q),
                               np.arange(0, h * Q),
                               np.arange((h + 1) * Q, S)]).astype(np.int64)
        xb_perm = c(xf[perm])
        coskp = c(wp["_cos"][perm].reshape(S // 128, 128, RD).transpose(1, 0, 2))
        sinkp = c(wp["_sin"][perm].reshape(S // 128, 128, RD).transpose(1, 0, 2))
        cosq2 = c(np.concatenate([wp["_cos"][h * Q:(h + 1) * Q].T] * 2, axis=1))
        sinq2 = c(np.concatenate([wp["_sin"][h * Q:(h + 1) * Q].T] * 2, axis=1))
        in_maps.append(dict(
            xb=xb_perm, wqa=wp["wqa"], wkva=wp["wkva"], wqb=wp["wqb"],
            wkvb=wp["wkvb"], wo=wp["wo"], wbt=wp["wbt"], wrt=wp["wrt"],
            at=wp["at"], bflat=wp["bflat"], bbase=wp["bbase"],
            gqa=wp["gqa"], gkva=wp["gkva"], cosk=coskp, sink=sinkp,
            cosq2=cosq2, sinq2=sinq2, emat=wp["emat"], ident=wp["ident"]))
    return in_maps


def kernel(**inputs):
    cfg = full_cfg()
    in_maps = make_in_maps(cfg, inputs)
    nc = build_program(cfg)
    res = run_bass_kernel_spmd(nc, in_maps, core_ids=list(range(8)))
    x = np.asarray(inputs["L0_attn_hidden_states"])
    Q, S = cfg["Q"], cfg["S"]
    out = np.empty(x.shape, dtype=np.float32)
    for ci in range(8):
        b, h = divmod(ci, S // Q)
        out[b, h * Q:(h + 1) * Q, :] = res.results[ci]["out"]
    return out


# version 26
# speedup vs baseline: 1.0245x; 1.0150x over previous
import numpy as np
from contextlib import ExitStack

import concourse.bass as bass
import concourse.mybir as mybir
from concourse.bass_utils import run_bass_kernel_spmd

F32 = mybir.dt.float32
BF16 = mybir.dt.bfloat16
U8 = mybir.dt.uint8
AF = mybir.ActivationFunctionType
ALU = mybir.AluOpType
AX = mybir.AxisListType

LN_EPS = 1e-5
RMS_EPS = 1e-6
THETA = 10000.0


class Sig:
    def __init__(self, sem):
        self.sem = sem
        self.count = 0

    def inc(self, inst, k=1):
        inst.then_inc(self.sem, k)
        self.count += k
        return (self, self.count)


class SyncCtx:
    def __init__(self):
        self.waited = {}

    def wait(self, eng, token):
        if token is None:
            return
        sig, val = token
        e = getattr(eng, "_e", eng)
        m = self.waited.setdefault(id(e), {})
        if val > m.get(id(sig), 0):
            eng.wait_ge(sig.sem, val)
            m[id(sig)] = val

    def wait_many(self, eng, tokens):
        best = {}
        for t in tokens:
            if t is None:
                continue
            sig, val = t
            if id(sig) not in best or val > best[id(sig)][1]:
                best[id(sig)] = (sig, val)
        for sig, val in best.values():
            self.wait(eng, (sig, val))


class LastW:
    _SKIP = {"wait_ge", "wait_op", "sem_inc", "nop", "engine_nop", "drain",
             "wait_many"}

    def __init__(self, eng):
        object.__setattr__(self, "_e", eng)
        object.__setattr__(self, "last", None)

    def __getattr__(self, n):
        a = getattr(object.__getattribute__(self, "_e"), n)
        if callable(a) and not n.startswith("_") and n not in LastW._SKIP:
            def f(*args, **kw):
                r = a(*args, **kw)
                if hasattr(r, "then_inc"):
                    object.__setattr__(self, "last", r)
                return r
            return f
        return a


def full_cfg():
    return dict(D=4096, QR=1536, KVR=512, RD=64, ND=128, VD=128, NH=32,
                S=2048, Q=1024, NE=8, RANK=16, DO=4096)


def small_cfg():
    return dict(D=1024, QR=512, KVR=256, RD=64, ND=128, VD=128, NH=2,
                S=256, Q=256, NE=8, RANK=16, DO=1024)


def prep_weights(cfg, w):
    D, QR, KVR, RD = cfg["D"], cfg["QR"], cfg["KVR"], cfg["RD"]
    ND, VD, NH, S = cfg["ND"], cfg["VD"], cfg["NH"], cfg["S"]
    NE, RANK, DO = cfg["NE"], cfg["RANK"], cfg["DO"]
    DK, QRK, KVK = D // 128, QR // 128, KVR // 128
    NP = NH // 2
    QK = ND + RD
    ER = NE * RANK
    c = np.ascontiguousarray
    out = {}
    nqb = max(1, QR // 384) if QR % 384 == 0 else 1
    out["wqa"] = c(w["w_qa"].reshape(DK, 128, nqb, QR // nqb).transpose(2, 1, 0, 3))
    out["wkva"] = c(w["w_kva"].reshape(DK, 128, KVR + RD).transpose(1, 0, 2))
    t = w["w_qb"].reshape(QRK, 128, NP, 2, QK)
    wqb = np.concatenate([t[:, :, :, 0, :ND], t[:, :, :, 1, :ND],
                          t[:, :, :, 0, ND:], t[:, :, :, 1, ND:]], axis=-1)
    out["wqb"] = c(wqb.transpose(2, 1, 0, 3))
    wkvb = w["w_kvb"].reshape(KVK, 128, NP, 2 * (ND + VD))
    out["wkvb"] = c(wkvb.transpose(2, 1, 0, 3))
    wob = max(1, D // 512)
    out["wo"] = c(w["w_o"].reshape(NH, 128, wob, D // wob).transpose(2, 1, 0, 3))
    fob = max(1, DO // 512)
    out["wbt"] = c(w["W_base"].T.reshape(DK, 128, fob, DO // fob).transpose(2, 1, 0, 3))
    out["wrt"] = c(w["W_router"].T.reshape(DK, 128, NE).transpose(1, 0, 2))
    out["at"] = c(w["A_stack"].reshape(ER, D).T.reshape(DK, 128, ER).transpose(1, 0, 2))
    out["bflat"] = c(w["B_stack"].transpose(0, 2, 1).reshape(ER, DO))
    out["bbase"] = c(w["b_base"].reshape(1, DO))
    out["gqa"] = c(w["g_qa"].reshape(QRK, 128).T)
    out["gkva"] = c(w["g_kva"].reshape(KVK, 128).T)
    inv_freq = 1.0 / THETA ** (np.arange(0, RD, 2, dtype=np.float64) / RD)
    tt = np.arange(S, dtype=np.float64)
    emb = np.concatenate([np.outer(tt, inv_freq)] * 2, -1)
    out["_cos"] = np.cos(emb).astype(np.float32)
    out["_sin"] = np.sin(emb).astype(np.float32)
    out["emat"] = c(np.repeat(np.eye(NE, dtype=np.float32), RANK, axis=1))
    out["ident"] = np.eye(128, dtype=np.float32)
    return out


def build_program(cfg, dbg=False):
    D, QR, KVR, RD = cfg["D"], cfg["QR"], cfg["KVR"], cfg["RD"]
    ND, VD, NH, S, Q = cfg["ND"], cfg["VD"], cfg["NH"], cfg["S"], cfg["Q"]
    NE, RANK, DO = cfg["NE"], cfg["RANK"], cfg["DO"]
    DK, QRK, KVK = D // 128, QR // 128, KVR // 128
    NP = NH // 2
    ER = NE * RANK
    KT, QT = S // 128, Q // 128
    nqb = max(1, QR // 384) if QR % 384 == 0 else 1
    NOB = max(1, D // 512)
    OBW = D // NOB
    FOB = max(1, DO // 512)
    FBW = DO // FOB
    KH = max(1, NH // 16)
    KHN = NH // KH
    DKH = max(1, DK // 8)
    DKHN = DK // DKH
    cscale = float((ND + RD) ** -0.5)
    qch = [(s0, min(512, Q - s0)) for s0 in range(0, Q, 512)]
    sch = [(s0, min(512, S - s0)) for s0 in range(0, S, 512)]
    DGRP = 2 if QT >= 2 else 1
    GT = QT // DGRP
    HLF = RD // 2

    nc = bass.Bass("TRN2", target_bir_lowering=False, debug=False)

    def din(name, shape):
        return nc.dram_tensor(name, list(shape), F32, kind="ExternalInput")

    xb_d = din("xb", [S, D])
    wqa_d = din("wqa", [nqb, 128, DK, QR // nqb])
    wkva_d = din("wkva", [128, DK, KVR + RD])
    wqb_d = din("wqb", [NP, 128, QRK, 2 * ND + 2 * RD])
    wkvb_d = din("wkvb", [NP, 128, KVK, 2 * (ND + VD)])
    wo_d = din("wo", [NOB, 128, NH, OBW])
    wbt_d = din("wbt", [FOB, 128, DK, FBW])
    wrt_d = din("wrt", [128, DK, NE])
    at_d = din("at", [128, DK, ER])
    bflat_d = din("bflat", [ER, DO])
    bbase_d = din("bbase", [1, DO])
    gqa_d = din("gqa", [128, QRK])
    gkva_d = din("gkva", [128, KVK])
    cosk_d = din("cosk", [128, KT, RD])
    sink_d = din("sink", [128, KT, RD])
    cosq2_d = din("cosq2", [RD, 2 * Q])
    sinq2_d = din("sinq2", [RD, 2 * Q])
    emat_d = din("emat", [NE, 128])
    ident_d = din("ident", [128, 128])
    out_d = nc.dram_tensor("out", [Q, D], F32, kind="ExternalOutput")

    ctx = ExitStack()
    S_ = SyncCtx()

    def mksig(name):
        return Sig(ctx.enter_context(nc.semaphore(name)))

    dX = [mksig("dX0"), mksig("dX1")]
    dW = [mksig("dW0"), mksig("dW1")]
    dW2 = [mksig("dW20"), mksig("dW21")]
    dC = mksig("dC")
    dCg = mksig("dCg")
    dO = [mksig("dO0"), mksig("dO1")]
    sigT = mksig("sT")
    sigV = mksig("sV")
    sigS = mksig("sS")

    PB = (nc.sbuf_bytes_remaining - 2048) // 32 * 32
    pool = nc.alloc_sbuf_tensor("pool", [128, PB], U8)

    offreg = {}

    def view(off, part, shape, dt):
        esz = 2 if dt == BF16 else 4
        n = 1
        for s in shape:
            n *= s
        assert off % 32 == 0 and off + n * esz <= PB, (off, shape, dt)
        ap = pool[0:part, off:off + n * esz].bitcast(dt)
        if len(shape) > 1:
            pat = " ".join(f"d{i}" for i in range(len(shape)))
            kw = {f"d{i}": shape[i] for i in range(len(shape))}
            ap = ap.rearrange(f"p ({pat}) -> p {pat}", **kw)
        offreg[id(ap)] = (off, n * esz)
        return ap

    class Cursor:
        def __init__(self, start, limit):
            self.off = start
            self.limit = limit

        def alloc(self, part, shape, dt):
            esz = 2 if dt == BF16 else 4
            n = 1
            for s in shape:
                n *= s
            off = self.off
            self.off = (off + n * esz + 31) // 32 * 32
            assert self.off <= self.limit, (off, shape, self.off, self.limit)
            return view(off, part, shape, dt)

    cc = Cursor(0, 3072)

    ident_b = cc.alloc(128, [128], BF16)
    ident_f = cc.alloc(128, [128], F32)
    ones_col = cc.alloc(128, [1], BF16)
    ones_row = cc.alloc(1, [128], BF16)
    emat_b = cc.alloc(NE, [128], BF16)
    epsln = cc.alloc(128, [1], F32)
    col_a = [cc.alloc(128, [1], F32) for _ in range(2)]
    col_b = [cc.alloc(128, [1], F32) for _ in range(2)]
    col_c = [cc.alloc(128, [1], F32) for _ in range(2)]
    mv = [cc.alloc(128, [2], F32) for _ in range(2)]
    stats = [cc.alloc(128, [48], F32) for _ in range(2)]
    negB_b = cc.alloc(1, [2], BF16)
    negB_col = cc.alloc(128, [1], F32)
    gm_sb = cc.alloc(1, [4], F32)
    rs_x = [cc.alloc(128, [1], F32) for _ in range(2)]
    rs_t = [cc.alloc(128, [1], F32) for _ in range(2)]
    magic_i = cc.alloc(128, [1], F32)
    gqa_b = cc.alloc(128, [QRK], F32)
    gkva_b = cc.alloc(128, [KVK], F32)

    ct = Cursor((cc.off + 255) // 256 * 256, PB)
    cosk_b = ct.alloc(128, [KT, RD], F32)
    sink_b = ct.alloc(128, [KT, RD], F32)
    cosq2_b = ct.alloc(RD, [2 * Q], BF16)
    sinq2_b = ct.alloc(RD, [2 * Q], BF16)
    k_cT = ct.alloc(128, [KVK, S], BF16)
    k_ropeT = ct.alloc(RD, [S], BF16)
    q_cT = ct.alloc(128, [QRK, Q], BF16)
    V0 = (ct.off + 511) // 512 * 512

    ZLOW = (cc.off + 255) // 256 * 256
    tok = {}

    with nc.Block() as blk:
        @blk.sync
        def _(sy):
            sy = LastW(sy)
            dC.inc(sy.dma_start(out=cosk_b[:], in_=cosk_d[:]), 16)
            dC.inc(sy.dma_start(out=sink_b[:], in_=sink_d[:]), 16)
            dC.inc(sy.dma_start(out=gqa_b[:], in_=gqa_d[:]), 16)
            dC.inc(sy.dma_start(out=gkva_b[:], in_=gkva_d[:]), 16)
            dC.inc(sy.dma_start(out=ident_f[:], in_=ident_d[:]), 16)

        @blk.gpsimd
        def _(g):
            g = LastW(g)
            dCg.inc(g.dma_start(out=ident_b[:], in_=ident_d[:]), 16)
            dCg.inc(g.dma_start(out=emat_b[:], in_=emat_d[:]), 16)
            dCg.inc(g.dma_start(out=cosq2_b[:], in_=cosq2_d[:]), 16)
            dCg.inc(g.dma_start(out=sinq2_b[:], in_=sinq2_d[:]), 16)

        @blk.vector
        def _(v):
            v = LastW(v)
            v.memset(ones_col[:], 1.0)
            v.memset(ones_row[:], 1.0)
            v.memset(magic_i[:].bitcast(mybir.dt.int32), 0x5F3759DF)
            tok["vconsts"] = sigV.inc(v.memset(epsln[:], LN_EPS))
    tok["consts_f"] = (dC, 80)
    tok["consts_b"] = (dCg, 64)
    tok["ident_ready"] = (dCg, 64)

    def v_rsqrt(v, px, src_ap, eps):
        v.tensor_scalar(out=rs_x[px][:], in0=src_ap, scalar1=float(eps),
                        scalar2=None, op0=ALU.add)
        t = sigV.inc(v.last)
        S_.wait(v, t)
        xi = rs_x[px][:].bitcast(mybir.dt.int32)
        yi = col_a[px][:].bitcast(mybir.dt.int32)
        v.tensor_scalar(out=yi, in0=xi, scalar1=1, scalar2=None,
                        op0=ALU.arith_shift_right)
        t = sigV.inc(v.last)
        S_.wait(v, t)
        v.scalar_tensor_tensor(out=yi, in0=yi, scalar=-1.0,
                               in1=magic_i[:].bitcast(mybir.dt.int32),
                               op0=ALU.mult, op1=ALU.add)
        t = sigV.inc(v.last)
        S_.wait(v, t)
        for it_ in range(3):
            v.tensor_tensor(out=rs_t[px][:], in0=col_a[px][:],
                            in1=col_a[px][:], op=ALU.mult)
            t = sigV.inc(v.last)
            S_.wait(v, t)
            v.tensor_tensor(out=rs_t[px][:], in0=rs_t[px][:], in1=rs_x[px][:],
                            op=ALU.mult)
            t = sigV.inc(v.last)
            S_.wait(v, t)
            v.tensor_scalar(out=rs_t[px][:], in0=rs_t[px][:], scalar1=-0.5,
                            scalar2=1.5, op0=ALU.mult, op1=ALU.add)
            t = sigV.inc(v.last)
            S_.wait(v, t)
            v.tensor_tensor(out=col_a[px][:], in0=col_a[px][:],
                            in1=rs_t[px][:], op=ALU.mult)
            if it_ < 2:
                t = sigV.inc(v.last)
                S_.wait(v, t)

    def v_rsqrtN(v, src_ap, out_ap, tx, tt, mg, eps):
        v.tensor_scalar(out=tx, in0=src_ap, scalar1=float(eps),
                        scalar2=None, op0=ALU.add)
        t = sigV.inc(v.last)
        S_.wait(v, t)
        xi = tx.bitcast(mybir.dt.int32)
        yi = out_ap.bitcast(mybir.dt.int32)
        v.tensor_scalar(out=yi, in0=xi, scalar1=1, scalar2=None,
                        op0=ALU.arith_shift_right)
        t = sigV.inc(v.last)
        S_.wait(v, t)
        v.scalar_tensor_tensor(out=yi, in0=yi, scalar=-1.0,
                               in1=mg.bitcast(mybir.dt.int32),
                               op0=ALU.mult, op1=ALU.add)
        t = sigV.inc(v.last)
        S_.wait(v, t)
        for it_ in range(3):
            v.tensor_tensor(out=tt, in0=out_ap, in1=out_ap, op=ALU.mult)
            t = sigV.inc(v.last)
            S_.wait(v, t)
            v.tensor_tensor(out=tt, in0=tt, in1=tx, op=ALU.mult)
            t = sigV.inc(v.last)
            S_.wait(v, t)
            v.tensor_scalar(out=tt, in0=tt, scalar1=-0.5, scalar2=1.5,
                            op0=ALU.mult, op1=ALU.add)
            t = sigV.inc(v.last)
            S_.wait(v, t)
            v.tensor_tensor(out=out_ap, in0=out_ap, in1=tt, op=ALU.mult)
            if it_ < 2:
                t = sigV.inc(v.last)
                S_.wait(v, t)

    def ln_stats(v, src, width, key, px):
        nch = max(1, width // 512)
        cw = width // nch
        for i in range(nch):
            v.bn_stats(out=stats[px][:, 6 * i:6 * i + 6],
                       in_=src[:, i * cw:(i + 1) * cw])
        t = sigV.inc(v.last)
        S_.wait(v, t)
        v.bn_aggr(out=mv[px][:], in_=stats[px][:, 0:6 * nch])
        tok[key] = sigV.inc(v.last)

    def ln_finish(v, key_sqrt, px):
        S_.wait(v, tok[key_sqrt])
        v.tensor_scalar(out=col_b[px][:], in0=mv[px][:, 0:1], scalar1=col_a[px][:],
                        scalar2=-1.0, op0=ALU.mult, op1=ALU.mult)
        t = sigV.inc(v.last)
        S_.wait(v, t)
        return t

    va = Cursor(V0, PB)
    ln1T_own = va.alloc(128, [QT, DK * 128], BF16)
    x_buf0 = va.alloc(128, [D], F32)
    ln1_bufs = [va.alloc(128, [D], BF16) for _ in range(2)]
    ln1T_tmps = [va.alloc(128, [DK * 128], BF16)]
    wkva_b = va.alloc(128, [DK, KVR + RD], BF16)
    kc_tmps = [va.alloc(128, [KVR], BF16) for _ in range(2)]
    kr_tmpA = [va.alloc(128, [RD], F32) for _ in range(2)]
    kr_tmpB = [va.alloc(128, [RD], F32) for _ in range(2)]
    krb_tmps = [va.alloc(128, [RD], BF16) for _ in range(2)]
    _qo, _qn = offreg[id(q_cT)]
    if _qn >= D * 4 + DK * 128 * 2:
        x_buf1 = view(_qo, 128, [D], F32)
        ln1T_tmps.append(view(_qo + D * 4, 128, [DK * 128], BF16))
    else:
        x_buf1 = va.alloc(128, [D], F32)
        ln1T_tmps.append(va.alloc(128, [DK * 128], BF16))
    x_bufs = [x_buf0, x_buf1]

    psA = ExitStack()
    trP = [psA.enter_context(nc.psum_tensor(f"trP{i}", [128, 2048], BF16))
           for i in range(2)]
    psA1 = ExitStack()
    kvP = [psA1.enter_context(nc.psum_tensor(f"kvP{i}", [128, KVR + RD], F32))
           for i in range(2)]

    with nc.Block() as blk:
        @blk.gpsimd
        def _(g):
            g = LastW(g)
            tok["wkva"] = dW[0].inc(g.dma_start(out=wkva_b[:], in_=wkva_d[:]), 16)

    xv = xb_d.rearrange("(t p) d -> t p d", p=128)
    kvsplits = [(0, min(512, KVR))]
    if KVR > 512:
        kvsplits.append((512, KVR - 512))

    def a1_xload(blk, i):
        par = i % 2
        if True:
            @blk.sync
            def _(sy):
                sy = LastW(sy)
                S_.wait(sy, tok.get(("lnapply", i - 2)))
                tok[("xload", i)] = dX[par].inc(
                    sy.dma_start(out=x_bufs[par][:], in_=xv[i, :, :]), 16)

    def a1_ln(blk, i):
        par = i % 2
        if True:
            @blk.vector
            def _(v):
                v = LastW(v)
                S_.wait(v, tok[("xload", i)])
                S_.wait(v, tok["vconsts"])
                ln_stats(v, x_bufs[par], D, ("ln1s", i), par)

            @blk.vector
            def _(v):
                v = LastW(v)
                S_.wait(v, tok[("ln1s", i)])
                v_rsqrt(v, par, mv[par][:, 1:2], LN_EPS)
                tok[("ln1sq", i)] = sigV.inc(v.last)

            @blk.vector
            def _(v):
                v = LastW(v)
                tok[("ln1fin", i)] = ln_finish(v, ("ln1sq", i), par)

    def a1_transpose(blk, i):
        par = i % 2
        if True:
            @blk.tensor
            def _(t):
                t = LastW(t)
                S_.wait(t, tok[("lnapply", i)])
                S_.wait(t, tok["ident_ready"])
                nw = (DK + 15) // 16
                S_.wait(t, tok.get(("lnT_evac", i - 2, nw - 1)))
                S_.wait(t, tok.get(("kcT_evac", i - 2)))
                for wv in range(nw):
                    if wv > 0:
                        S_.wait(t, tok[("lnT_evac", i, wv - 1)])
                    for k in range(wv * 16, min(wv * 16 + 16, DK)):
                        t.transpose(trP[par][:, (k % 16) * 128:(k % 16) * 128 + 128],
                                    ln1_bufs[par][:, k * 128:(k + 1) * 128], ident_b[:])
                    tok[("lnT_mm", i, wv)] = sigT.inc(t.last)
                tok[("lnT_mm_all", i)] = tok[("lnT_mm", i, nw - 1)]

    def a1_kvmm(blk, i):
        par = i % 2
        if True:
            @blk.tensor
            def _(t):
                t = LastW(t)
                S_.wait(t, tok[("lnT_done", i)])
                S_.wait(t, tok["wkva"])
                S_.wait(t, tok.get(("kv_used", i - 2)))
                if i < QT:
                    lsrc = ln1T_own[:, i, :]
                else:
                    lsrc = ln1T_tmps[par][:]
                for (c0, w) in kvsplits:
                    for k in range(DK):
                        t.matmul(kvP[par][:, c0:c0 + w],
                                 lsrc[:, k * 128:(k + 1) * 128],
                                 wkva_b[:, k, c0:c0 + w],
                                 start=(k == 0), stop=(k == DK - 1))
                for k in range(DK):
                    t.matmul(kvP[par][:, KVR:KVR + RD],
                             lsrc[:, k * 128:(k + 1) * 128],
                             wkva_b[:, k, KVR:KVR + RD],
                             start=(k == 0), stop=(k == DK - 1))
                tok[("kv_mm", i)] = sigT.inc(t.last)

    def a1_kvchain(blk, i):
        par = i % 2
        if True:
            @blk.vector
            def _(v):
                v = LastW(v)
                S_.wait(v, tok[("kv_mm", i)])
                nch = max(1, KVR // 512)
                cw = KVR // nch
                for c_ in range(nch):
                    v.bn_stats(out=stats[par][:, 6 * c_:6 * c_ + 6],
                               in_=kvP[par][:, c_ * cw:(c_ + 1) * cw])
                t = sigV.inc(v.last)
                S_.wait(v, t)
                v.bn_aggr(out=mv[par][:], in_=stats[par][:, 0:6 * nch])
                t = sigV.inc(v.last)
                S_.wait(v, t)
                v.tensor_scalar(out=col_c[par][:], in0=mv[par][:, 0:1],
                                scalar1=mv[par][:, 0:1], scalar2=None, op0=ALU.mult)
                t = sigV.inc(v.last)
                S_.wait(v, t)
                v.tensor_tensor(out=col_c[par][:], in0=col_c[par][:],
                                in1=mv[par][:, 1:2], op=ALU.add)
                tok[("kv_ms", i)] = sigV.inc(v.last)

            @blk.vector
            def _(v):
                v = LastW(v)
                S_.wait(v, tok[("kv_ms", i)])
                if i + 2 < KT:
                    S_.wait(v, tok[("lnapply", i + 2)])
                v_rsqrt(v, par, col_c[par][:], 1e-30)
                tok[("kv_sq", i)] = sigV.inc(v.last)

            @blk.vector
            def _(v):
                v = LastW(v)
                S_.wait(v, tok[("kv_sq", i)])
                v.tensor_tensor(out=col_a[par][:], in0=col_a[par][:],
                                in1=col_c[par][:], op=ALU.mult)
                t = sigV.inc(v.last)
                S_.wait(v, t)
                v.tensor_scalar(out=col_a[par][:], in0=col_a[par][:],
                                scalar1=RMS_EPS, scalar2=None, op0=ALU.add)
                t = sigV.inc(v.last)
                S_.wait(v, t)
                v.reciprocal(out=col_a[par][:], in_=col_a[par][:])
                t = sigV.inc(v.last)
                S_.wait(v, t)
                S_.wait(v, tok.get(("kcT_mm", i - 2)))
                S_.wait(v, tok["consts_f"])
                v.tensor_scalar(out=kc_tmps[par][:], in0=kvP[par][:, 0:KVR],
                                scalar1=col_a[par][:], scalar2=None, op0=ALU.mult)
                v.tensor_tensor(out=kr_tmpA[par][:], in0=kvP[par][:, KVR:KVR + RD],
                                in1=cosk_b[:, i, :], op=ALU.mult)
                v.tensor_scalar(out=kr_tmpB[par][:, 0:HLF],
                                in0=kvP[par][:, KVR + HLF:KVR + RD],
                                scalar1=-1.0, scalar2=None, op0=ALU.mult)
                v.tensor_copy(out=kr_tmpB[par][:, HLF:RD],
                              in_=kvP[par][:, KVR:KVR + HLF])
                t = sigV.inc(v.last)
                S_.wait(v, t)
                v.tensor_tensor(out=kr_tmpB[par][:], in0=kr_tmpB[par][:],
                                in1=sink_b[:, i, :], op=ALU.mult)
                t = sigV.inc(v.last)
                S_.wait(v, t)
                v.tensor_tensor(out=krb_tmps[par][:], in0=kr_tmpA[par][:],
                                in1=kr_tmpB[par][:], op=ALU.add)
                tok[("kv_rdy", i)] = sigV.inc(v.last)
                tok[("kv_used", i)] = tok[("kv_rdy", i)]

            @blk.tensor
            def _(t):
                t = LastW(t)
                S_.wait(t, tok[("kv_rdy", i)])
                S_.wait(t, tok.get(("kcT_evac", i - 2)))
                S_.wait(t, tok.get(("lnT_done", i)))
                for k in range(KVK):
                    t.transpose(trP[par][:, k * 128:(k + 1) * 128],
                                kc_tmps[par][:, k * 128:(k + 1) * 128], ident_b[:])
                t.transpose(trP[par][0:RD, KVK * 128:KVK * 128 + 128],
                            krb_tmps[par][:], ident_b[:])
                tok[("kcT_mm", i)] = sigT.inc(t.last)

            @blk.vector
            def _(v):
                v = LastW(v)
                S_.wait(v, tok[("kcT_mm", i)])
                for k in range(KVK):
                    v.tensor_scalar(out=k_cT[:, k, i * 128:(i + 1) * 128],
                                    in0=trP[par][:, k * 128:(k + 1) * 128],
                                    scalar1=gkva_b[:, k:k + 1], scalar2=None,
                                    op0=ALU.mult)
                v.tensor_copy(out=k_ropeT[:, i * 128:(i + 1) * 128],
                              in_=trP[par][0:RD, KVK * 128:KVK * 128 + 128])
                tok[("kcT_evac", i)] = sigV.inc(v.last)

    nwA = (DK + 15) // 16
    aS0 = sigS.count
    for i in range(KT):
        base = aS0 + i * (nwA + 1)
        tok[("lnapply", i)] = (sigS, base + 1)
        for wv in range(nwA):
            tok[("lnT_evac", i, wv)] = (sigS, base + 2 + wv)
        tok[("lnT_done", i)] = tok[("lnT_evac", i, nwA - 1)]

    with nc.Block() as blk:
        a1_xload(blk, 0)
        for it in range(KT + 2):
            if it + 1 < KT:
                a1_xload(blk, it + 1)
            if it < KT:
                a1_ln(blk, it)
            if 1 <= it <= KT:
                a1_kvmm(blk, it - 1)
            if 2 <= it:
                a1_kvchain(blk, it - 2)
            if it < KT:
                a1_transpose(blk, it)

        @blk.scalar
        def _(s):
            s = LastW(s)
            for i in range(KT):
                par = i % 2
                S_.wait(s, tok[("ln1fin", i)])
                S_.wait(s, tok[("xload", i)])
                S_.wait(s, tok.get(("lnT_mm_all", i - 2)))
                s.activation(out=ln1_bufs[par][:], in_=x_bufs[par][:],
                             func=AF.Identity,
                             scale=col_a[par][:], bias=col_b[par][:])
                tk = sigS.inc(s.last)
                assert tk == tok[("lnapply", i)]
                for wv in range(nwA):
                    S_.wait(s, tok[("lnT_mm", i, wv)])
                    nk = min(16, DK - wv * 16)
                    if i < QT:
                        dst = ln1T_own[:, i, wv * 16 * 128:(wv * 16 + nk) * 128]
                    else:
                        dst = ln1T_tmps[par][:, wv * 16 * 128:(wv * 16 + nk) * 128]
                    s.copy(out=dst, in_=trP[par][:, 0:nk * 128])
                    tk = sigS.inc(s.last)
                    assert tk == tok[("lnT_evac", i, wv)]
    psA1.close()
    vb = Cursor(V0, PB)
    wqb_buf = [vb.alloc(128, [QRK, 2 * ND + 2 * RD], BF16) for _ in range(2)]
    wkvb_buf = [vb.alloc(128, [KVK, 2 * (ND + VD)], BF16) for _ in range(2)]
    q_nT = [[vb.alloc(128, [Q], BF16) for _ in range(2)] for _ in range(2)]
    q_rT = [vb.alloc(RD, [2 * Q], BF16) for _ in range(2)]
    ropes_b = vb.alloc(RD, [2 * Q], BF16)
    rot_b = vb.alloc(RD, [2 * Q], BF16)
    ropet1 = vb.alloc(RD, [2 * Q], BF16)
    k_nT = [vb.alloc(128, [S], BF16) for _ in range(2)]
    v_sb = [vb.alloc(128, [KT, VD], BF16) for _ in range(2)]
    PT_buf = [vb.alloc(128, [512], BF16) for _ in range(2)]
    sq_s = vb.alloc(128, [S // 8], BF16)
    sq_q = vb.alloc(128, [Q // 8], BF16)
    sq_krs = vb.alloc(RD, [S // 8], BF16)
    sq_qrs = vb.alloc(RD, [Q // 8], BF16)
    rsumc_f = vb.alloc(128, [Q // 128], F32)
    rsumc = vb.alloc(128, [Q // 128], BF16)
    bcast_sb = vb.alloc(128, [Q], F32)
    if KT * RD * 8 >= Q * 6:
        srow_f = view(ZLOW, 1, [Q], F32)
        rsum_b = view(ZLOW + Q * 4, 1, [Q], BF16)
    else:
        srow_f = vb.alloc(1, [Q], F32)
        rsum_b = vb.alloc(1, [Q], BF16)
    negB2 = [vb.alloc(128, [1], F32) for _ in range(2)]
    negB2b = [vb.alloc(1, [2], BF16) for _ in range(2)]
    out_all = vb.alloc(128, [NH, Q], BF16)
    assert vb.off <= PB, (vb.off, PB)

    def b_preamble(blk):
        @blk.scalar
        def _(s):
            s = LastW(s)
            S_.wait(s, tok[("kcT_evac", KT - 1)])
            S_.wait(s, tok[("qevac", nqb - 1, QT - 1)])
            s.activation(out=sq_krs[:], in_=k_ropeT[:, 0:S:8], func=AF.Square)
            tok["sq_krs"] = sigS.inc(s.last)

        @blk.gpsimd
        def _(g):
            g = LastW(g)
            S_.wait(g, tok[("qevac", nqb - 1, QT - 1)])
            for pp in range(min(2, NP)):
                tok[("wqb", pp)] = dW[pp].inc(
                    g.dma_start(out=wqb_buf[pp][:], in_=wqb_d[pp, :, :, :]), 16)
                tok[("wkvb", pp)] = dW2[pp].inc(
                    g.dma_start(out=wkvb_buf[pp][:], in_=wkvb_d[pp, :, :, :]), 16)

    NUQ = Q // 512 if Q >= 512 else 1
    UQW = Q // NUQ
    units = [(kt, c) for kt in range(KT) for c in range(NUQ)]
    NKC = max(1, S // 512)
    KCW = S // NKC
    NVG = max(1, KT // 4)
    VGW = KT // NVG

    def b_prefetch(blk, p1):
        @blk.gpsimd
        def _(g, p1=p1):
            g = LastW(g)
            S_.wait(g, tok[("prep_mm", 2 * (p1 - 2), 3 * NUQ - 1)])
            S_.wait(g, tok[("v_mm", 2 * p1 - 3, NVG - 1)])
            tok[("wqb", p1)] = dW[p1 % 2].inc(
                g.dma_start(out=wqb_buf[p1 % 2][:], in_=wqb_d[p1, :, :, :]), 16)
            tok[("wkvb", p1)] = dW2[p1 % 2].inc(
                g.dma_start(out=wkvb_buf[p1 % 2][:], in_=wkvb_d[p1, :, :, :]), 16)

    def b_prep_a(blk, h):
        p, j = h // 2, h % 2
        par = p % 2
        hpar = h % 2
        ppar = p % 2
        sbase = sigS.count
        if j == 0:
            for u in range(3 * NUQ):
                tok[("prep_ev", 2 * p, u)] = (sigS, sbase + u + 1)
            sbase += 3 * NUQ
        for c in range(NKC):
            tok[("knT_ev", h, c)] = (sigS, sbase + c + 1)

        if j == 0:
            for jj in (0, 1):
                for c in range(NUQ):
                    u = jj * NUQ + c

                    @blk.tensor
                    def _(t, p=p, par=par, jj=jj, c=c, u=u):
                        t = LastW(t)
                        S_.wait(t, tok[("wqb", p)])
                        S_.wait(t, tok[("qcT_evac", QT - 1)])
                        if u >= 2:
                            S_.wait(t, tok[("prep_ev", 2 * p, u - 2)])
                        S_.wait(t, tok.get(("prep_done", 2 * p - 1)))
                        for k in range(QRK):
                            t.matmul(PEp[u % 2][:, 0:UQW],
                                     wqb_buf[par][:, k, jj * ND:jj * ND + ND],
                                     q_cT[:, k, c * UQW:(c + 1) * UQW],
                                     start=(k == 0), stop=(k == QRK - 1))
                        tok[("prep_mm", 2 * p, u)] = sigT.inc(t.last)
            for c in range(NUQ):
                u = 2 * NUQ + c

                @blk.tensor
                def _(t, p=p, par=par, c=c, u=u):
                    t = LastW(t)
                    S_.wait(t, tok[("prep_ev", 2 * p, u - 2)])
                    for k in range(QRK):
                        t.matmul(PEp[u % 2][:, 0:UQW],
                                 wqb_buf[par][:, k, 2 * ND:2 * ND + 2 * RD],
                                 q_cT[:, k, c * UQW:(c + 1) * UQW],
                                 start=(k == 0), stop=(k == QRK - 1))
                    tok[("prep_mm", 2 * p, u)] = sigT.inc(t.last)

        for c in range(NKC):
            @blk.tensor
            def _(t, h=h, p=p, j=j, par=par, c=c):
                t = LastW(t)
                S_.wait(t, tok[("wkvb", p)])
                if c >= 2:
                    S_.wait(t, tok[("knT_ev", h, c - 2)])
                elif j == 0:
                    lastu = max(u for u in range(3 * NUQ) if u % 2 == c % 2)
                    S_.wait(t, tok[("prep_ev", 2 * p, lastu)])
                if c == 0 and j == 1:
                    S_.wait(t, tok.get(("prep_done", h - 1)))
                koff = j * (ND + VD)
                for k in range(KVK):
                    t.matmul(PEp[c % 2][:, 0:KCW],
                             wkvb_buf[par][:, k, koff:koff + ND],
                             k_cT[:, k, c * KCW:(c + 1) * KCW],
                             start=(k == 0), stop=(k == KVK - 1))
                tok[("knT_mm", h, c)] = sigT.inc(t.last)


    def b_prep_b1(blk, h):
        p, j = h // 2, h % 2
        par = p % 2
        hpar = h % 2
        ppar = p % 2
        for g_ in range(NVG):
            @blk.tensor
            def _(t, h=h, p=p, j=j, par=par, c=g_):
                t = LastW(t)
                if c >= 2:
                    S_.wait(t, tok[("v_ev", h, c - 2)])
                elif c == 0:
                    S_.wait(t, tok[("knT_ev", h, NKC - 1)])
                koff = j * (ND + VD)
                for c_ in range(VGW):
                    kt = c * VGW + c_
                    for k in range(KVK):
                        t.matmul(PEp[c % 2][:, c_ * 128:(c_ + 1) * 128],
                                 k_cT[:, k, kt * 128:(kt + 1) * 128],
                                 wkvb_buf[par][:, k, koff + ND:koff + ND + VD],
                                 start=(k == 0), stop=(k == KVK - 1))
                tok[("v_mm", h, c)] = sigT.inc(t.last)

            @blk.vector
            def _(v, h=h, hpar=hpar, c=g_):
                v = LastW(v)
                S_.wait(v, tok[("v_mm", h, c)])
                v.tensor_copy(
                    out=v_sb[hpar].rearrange("p t d -> p (t d)")[
                        :, c * VGW * VD:(c + 1) * VGW * VD],
                    in_=PEp[c % 2][:, 0:VGW * 128])
                tok[("v_ev", h, c)] = sigV.inc(v.last)

        if j == 0:
            @blk.vector
            def _(v, p=p, ppar=ppar):
                v = LastW(v)
                S_.wait(v, tok[("prep_ev", 2 * p, 3 * NUQ - 1)])
                v.tensor_scalar(out=rot_b[0:HLF, :], in0=ropes_b[HLF:RD, :],
                                scalar1=-1.0, scalar2=None, op0=ALU.mult)
                v.tensor_copy(out=rot_b[HLF:RD, :], in_=ropes_b[0:HLF, :])
                t = sigV.inc(v.last)
                S_.wait(v, t)
                v.tensor_tensor(out=ropet1[:], in0=ropes_b[:], in1=cosq2_b[:],
                                op=ALU.mult)
                v.tensor_tensor(out=rot_b[:], in0=rot_b[:], in1=sinq2_b[:],
                                op=ALU.mult)
                t = sigV.inc(v.last)
                S_.wait(v, t)
                v.tensor_tensor(out=q_rT[ppar][:], in0=ropet1[:],
                                in1=rot_b[:], op=ALU.add)
                tok[("rope_rdy", p)] = sigV.inc(v.last)
                tok[("rope_used", p)] = tok[("rope_rdy", p)]

    def b_prep_b2(blk, h):
        p, j = h // 2, h % 2
        par = p % 2
        hpar = h % 2
        ppar = p % 2
        @blk.scalar
        def _(s, h=h, p=p, j=j, hpar=hpar, ppar=ppar):
            s = LastW(s)
            if j == 0:
                for jj in (0, 1):
                    for c in range(NUQ):
                        u = jj * NUQ + c
                        S_.wait(s, tok[("prep_mm", 2 * p, u)])
                        s.copy(out=q_nT[ppar][jj][:, c * UQW:(c + 1) * UQW],
                               in_=PEp[u % 2][:, 0:UQW])
                        tk = sigS.inc(s.last)
                        assert tk == tok[("prep_ev", 2 * p, u)]
                for c in range(NUQ):
                    u = 2 * NUQ + c
                    S_.wait(s, tok[("prep_mm", 2 * p, u)])
                    S_.wait(s, tok.get(("rope_used", p - 1)))
                    s.copy(out=ropes_b[:, c * UQW:(c + 1) * UQW],
                           in_=PEp[u % 2][0:RD, 0:UQW])
                    s.copy(out=ropes_b[:, Q + c * UQW:Q + (c + 1) * UQW],
                           in_=PEp[u % 2][RD:2 * RD, 0:UQW])
                    tk = sigS.inc(s.last)
                    assert tk == tok[("prep_ev", 2 * p, u)]
            for c in range(NKC):
                S_.wait(s, tok[("knT_mm", h, c)])
                s.copy(out=k_nT[hpar][:, c * KCW:(c + 1) * KCW],
                       in_=PEp[c % 2][:, 0:KCW])
                tk = sigS.inc(s.last)
                assert tk == tok[("knT_ev", h, c)]
            S_.wait(s, tok[("knT_ev", h, NKC - 1)])
            S_.wait(s, tok.get(("guard_mm", h - 1)))
            s.activation(out=sq_s[:], in_=k_nT[hpar][:, 0:S:8],
                         func=AF.Square)
            S_.wait(s, tok[("prep_ev", 2 * p, (1 + j) * NUQ - 1)])
            S_.wait(s, tok[("rope_rdy", p)])
            s.activation(out=sq_q[:], in_=q_nT[p % 2][j][:, 0:Q:8],
                         func=AF.Square)
            s.activation(out=sq_qrs[:],
                         in_=q_rT[p % 2][:, j * Q:(j + 1) * Q:8],
                         func=AF.Square)
            tok[("sqk", h)] = sigS.inc(s.last)

        @blk.tensor
        def _(t, h=h):
            t = LastW(t)
            S_.wait(t, tok[("sqk", h)])
            S_.wait(t, tok["sq_krs"])
            S_.wait(t, tok[("v_ev", h, NVG - 1)])
            t.matmul(PEp[0][0:1, 0:S // 8], ones_col[:], sq_s[:],
                     start=True, stop=False)
            t.matmul(PEp[0][0:1, 0:S // 8], ones_col[0:RD, :], sq_krs[:],
                     start=False, stop=True)
            t.matmul(PEp[1][0:1, 0:Q // 8], ones_col[:], sq_q[:],
                     start=True, stop=False)
            t.matmul(PEp[1][0:1, 0:Q // 8], ones_col[0:RD, :], sq_qrs[:],
                     start=False, stop=True)
            tok[("guard_mm", h)] = sigT.inc(t.last)

        @blk.vector
        def _(v, h=h):
            v = LastW(v)
            S_.wait(v, tok[("guard_mm", h)])
            v.reduce_max(out=gm_sb[:, 0:1], in_=PEp[0][0:1, 0:S // 8],
                         axis=AX.X)
            v.reduce_max(out=gm_sb[:, 1:2], in_=PEp[1][0:1, 0:Q // 8],
                         axis=AX.X)
            t = sigV.inc(v.last)
            S_.wait(v, t)
            v.tensor_tensor(out=gm_sb[:, 2:3], in0=gm_sb[:, 0:1],
                            in1=gm_sb[:, 1:2], op=ALU.add)
            t = sigV.inc(v.last)
            S_.wait(v, t)
            v.tensor_scalar(out=negB2b[h % 2][:, 0:1], in0=gm_sb[:, 2:3],
                            scalar1=-cscale / 2.0, scalar2=None,
                            op0=ALU.mult)
            tok[("negB", h)] = sigV.inc(v.last)

        @blk.tensor
        def _(t, h=h):
            t = LastW(t)
            S_.wait(t, tok[("negB", h)])
            t.matmul(PEp[0][:, 0:1], ones_row[:], negB2b[h % 2][:, 0:1],
                     start=True, stop=True)
            tok[("negB_mm", h)] = sigT.inc(t.last)

        @blk.vector
        def _(v, h=h):
            v = LastW(v)
            S_.wait(v, tok[("negB_mm", h)])
            v.tensor_copy(out=negB2[h % 2][:], in_=PEp[0][:, 0:1])
            t = sigV.inc(v.last)
            tok[("negB_col", h)] = t
            tok[("prep_done", h)] = t


    def b_prep(blk, h):
        b_prep_a(blk, h)
        b_prep_b1(blk, h)
        b_prep_b2(blk, h)

    def kt_main(blk, h):
        p, j = h // 2, h % 2
        hpar = h % 2
        qroff = j * Q
        sS0 = sigS.count

        @blk.tensor
        def _(t, h=h, p=p, j=j, hpar=hpar, qroff=qroff, sS0=sS0):
            t = LastW(t)
            S_.wait(t, tok[("knT_ev", h, NKC - 1)])
            S_.wait(t, tok[("negB_col", h)])
            S_.wait(t, tok[("v_ev", h, NVG - 1)])
            for ui, (kt, c) in enumerate(units):
                if ui == 1:
                    S_.wait(t, tok.get(("out_evac", h - 1)))
                if ui >= 2:
                    S_.wait(t, (sigS, sS0 + ui - 1))
                t.matmul(sPP[ui % 2][:, 0:UQW],
                         k_nT[hpar][:, kt * 128:(kt + 1) * 128],
                         q_nT[p % 2][j][:, c * UQW:(c + 1) * UQW],
                         start=True, stop=False)
                t.matmul(sPP[ui % 2][:, 0:UQW],
                         k_ropeT[:, kt * 128:(kt + 1) * 128],
                         q_rT[p % 2][:, qroff + c * UQW:qroff + (c + 1) * UQW],
                         start=False, stop=True)
                tok[("sT_mm", h, ui)] = sigT.inc(t.last)
                if ui >= 1:
                    up = ui - 1
                    ktp, cp = units[up]
                    S_.wait(t, (sigS, sS0 + up + 1))
                    t.matmul(PDq[0:1, cp * UQW:(cp + 1) * UQW], ones_col[:],
                             PT_buf[up % 2][:, 0:UQW],
                             start=(ktp == 0), stop=(ktp == KT - 1))
                    t.matmul(PCq[:, cp * UQW:(cp + 1) * UQW],
                             v_sb[hpar][:, ktp, :], PT_buf[up % 2][:, 0:UQW],
                             start=(ktp == 0), stop=(ktp == KT - 1))
            up = len(units) - 1
            ktp, cp = units[up]
            S_.wait(t, (sigS, sS0 + up + 1))
            t.matmul(PDq[0:1, cp * UQW:(cp + 1) * UQW], ones_col[:],
                     PT_buf[up % 2][:, 0:UQW],
                     start=(ktp == 0), stop=True)
            t.matmul(PCq[:, cp * UQW:(cp + 1) * UQW],
                     v_sb[hpar][:, ktp, :], PT_buf[up % 2][:, 0:UQW],
                     start=(ktp == 0), stop=True)
            tok[("pv_done", h)] = sigT.inc(t.last)
            tok[("sums_done", h)] = tok[("pv_done", h)]

        @blk.scalar
        def _(s, h=h, sS0=sS0):
            s = LastW(s)
            for ui in range(len(units)):
                S_.wait(s, tok[("sT_mm", h, ui)])
                s.activation(out=PT_buf[ui % 2][:, 0:UQW],
                             in_=sPP[ui % 2][:, 0:UQW],
                             func=AF.Exp, bias=negB2[h % 2][:], scale=cscale)
                tk = sigS.inc(s.last)
                tok[("exp", h, ui)] = tk
                assert tk[1] == sS0 + ui + 1

    def kt_tail_v1(blk, h):
        @blk.vector
        def _(v, h=h):
            v = LastW(v)
            S_.wait(v, tok[("sums_done", h)])
            v.tensor_copy(out=srow_f[:], in_=PDq[0:1, 0:Q])
            t = sigV.inc(v.last)
            S_.wait(v, t)
            v.reciprocal(out=bcast_sb[0:1, 0:Q], in_=srow_f[:])
            t = sigV.inc(v.last)
            S_.wait(v, t)
            v.tensor_copy(out=rsum_b[:], in_=bcast_sb[0:1, 0:Q])
            tok[("rsum8_done", h)] = sigV.inc(v.last)

    def kt_tail_t3(blk, h):
        @blk.tensor
        def _(t, h=h):
            t = LastW(t)
            S_.wait(t, tok[("rsum8_done", h)])
            for qb in range(Q // 128):
                t.matmul(PDq[:, qb * 128:(qb + 1) * 128], ones_row[:],
                         rsum_b[:, qb * 128:(qb + 1) * 128],
                         start=True, stop=True)
            tok[("bcast_mm", h)] = sigT.inc(t.last)

    def kt_tail_v4(blk, h):
        @blk.vector
        def _(v, h=h):
            v = LastW(v)
            S_.wait(v, tok[("bcast_mm", h)])
            v.tensor_copy(out=bcast_sb[:], in_=PDq[:, 0:Q])
            t = sigV.inc(v.last)
            S_.wait(v, t)
            v.tensor_tensor(out=out_all[:, h, :], in0=PCq[:, 0:Q],
                            in1=bcast_sb[:], op=ALU.mult)
            tok[("out_evac", h)] = sigV.inc(v.last)
            tok[("bcast_used", h)] = tok[("out_evac", h)]

    def kt_tail(blk, h):
        kt_tail_v1(blk, h)
        kt_tail_t3(blk, h)
        kt_tail_v4(blk, h)

    _o, _n = offreg[id(ln1T_own)]
    va2 = Cursor((_o + _n + 31) // 32 * 32, PB)
    QBW2 = QR // nqb
    wqa_buf = [va2.alloc(128, [DK, QBW2], BF16) for _ in range(2)]
    q_raw = va2.alloc(128, [QT, QR], BF16)
    qc_tmp = va2.alloc(128, [QR], BF16)
    psA2 = ExitStack()
    qP = [psA2.enter_context(nc.psum_tensor(f"qP{i}", [128, 512], F32))
          for i in range(2)]

    lastA1 = [("kcT_evac", KT - 1), ("kv_mm", KT - 1), ("lnT_done", KT - 1),
              ("lnapply", KT - 1)]

    with nc.Block() as blk:
        @blk.gpsimd
        def _(g):
            g = LastW(g)
            for sb in range(min(2, nqb)):
                S_.wait_many(g, [tok[k] for k in lastA1 if k in tok])
                tok[("wqa", sb)] = dW[sb % 2].inc(
                    g.dma_start(out=wqa_buf[sb % 2][:], in_=wqa_d[sb, :, :, :]), 16)

    with nc.Block() as blk:
      for sb in range(nqb):
        par = sb % 2
        if True:
            eS0 = sigS.count
            for ti in range(QT):
                tok[("qevac", sb, ti)] = (sigS, eS0 + ti + 1)

            @blk.tensor
            def _(t, sb=sb, par=par):
                t = LastW(t)
                S_.wait(t, tok[("wqa", sb)])
                for ti in range(QT):
                    if ti >= 2:
                        S_.wait(t, tok[("qevac", sb, ti - 2)])
                    S_.wait(t, tok.get(("qevac", sb - 1, QT - 1)))
                    S_.wait(t, tok.get(("qevac", sb - 1, QT - 2)))
                    for k in range(DK):
                        t.matmul(qP[ti % 2][:, 0:QBW2],
                                 ln1T_own[:, ti, k * 128:(k + 1) * 128],
                                 wqa_buf[par][:, k, :],
                                 start=(k == 0), stop=(k == DK - 1))
                    tok[("qmm", sb, ti)] = sigT.inc(t.last)

            @blk.scalar
            def _(s, sb=sb):
                s = LastW(s)
                for ti in range(QT):
                    S_.wait(s, tok[("qmm", sb, ti)])
                    if sb == 0 and ti == 0:
                        S_.wait_many(s, [tok[k] for k in lastA1 if k in tok])
                    s.copy(out=q_raw[:, ti, sb * QBW2:(sb + 1) * QBW2],
                           in_=qP[ti % 2][:, 0:QBW2])
                    tk = sigS.inc(s.last)
                    assert tk == tok[("qevac", sb, ti)]

            @blk.gpsimd
            def _(g, sb=sb, par=par):
                g = LastW(g)
                if sb + 2 < nqb:
                    S_.wait(g, tok[("qmm", sb, QT - 1)])
                    tok[("wqa", sb + 2)] = dW[par].inc(
                        g.dma_start(out=wqa_buf[par][:],
                                    in_=wqa_d[sb + 2, :, :, :]), 16)

      for ti in range(QT):
        par = ti % 2
        if True:
            @blk.vector
            def _(v, ti=ti):
                v = LastW(v)
                for sb in range(nqb):
                    S_.wait(v, tok[("qevac", sb, ti)])
                px = ti % 2
                nch = max(1, QR // 512)
                cw = QR // nch
                for c_ in range(nch):
                    v.bn_stats(out=stats[px][:, 6 * c_:6 * c_ + 6],
                               in_=q_raw[:, ti, c_ * cw:(c_ + 1) * cw])
                t = sigV.inc(v.last)
                S_.wait(v, t)
                v.bn_aggr(out=mv[px][:], in_=stats[px][:, 0:6 * nch])
                t = sigV.inc(v.last)
                S_.wait(v, t)
                v.tensor_scalar(out=col_c[px][:], in0=mv[px][:, 0:1], scalar1=mv[px][:, 0:1],
                                scalar2=None, op0=ALU.mult)
                t = sigV.inc(v.last)
                S_.wait(v, t)
                v.tensor_tensor(out=col_c[px][:], in0=col_c[px][:], in1=mv[px][:, 1:2],
                                op=ALU.add)
                tok[("q_ms", ti)] = sigV.inc(v.last)

            @blk.vector
            def _(v, ti=ti):
                v = LastW(v)
                S_.wait(v, tok[("q_ms", ti)])
                v_rsqrt(v, ti % 2, col_c[ti % 2][:], 1e-30)
                tok[("q_sq", ti)] = sigV.inc(v.last)

            @blk.vector
            def _(v, ti=ti):
                v = LastW(v)
                S_.wait(v, tok[("q_sq", ti)])
                px = ti % 2
                v.tensor_tensor(out=col_a[px][:], in0=col_a[px][:],
                                in1=col_c[px][:], op=ALU.mult)
                t = sigV.inc(v.last)
                S_.wait(v, t)
                v.tensor_scalar(out=col_a[px][:], in0=col_a[px][:], scalar1=RMS_EPS,
                                scalar2=None, op0=ALU.add)
                t = sigV.inc(v.last)
                S_.wait(v, t)
                v.reciprocal(out=col_a[px][:], in_=col_a[px][:])
                t = sigV.inc(v.last)
                S_.wait(v, t)
                S_.wait(v, tok.get(("qcT_mm", ti - 1)))
                v.tensor_scalar(out=qc_tmp[:], in0=q_raw[:, ti, :],
                                scalar1=col_a[px][:], scalar2=None, op0=ALU.mult)
                tok[("qc_rdy", ti)] = sigV.inc(v.last)

            @blk.tensor
            def _(t, ti=ti, par=par):
                t = LastW(t)
                S_.wait(t, tok[("qc_rdy", ti)])
                S_.wait(t, tok.get(("qcT_evac", ti - 2)))
                if ti == 0:
                    S_.wait(t, tok[("lnapply", KT - 1)])
                for k in range(QRK):
                    t.transpose(trP[par][:, k * 128:(k + 1) * 128],
                                qc_tmp[:, k * 128:(k + 1) * 128], ident_b[:])
                tok[("qcT_mm", ti)] = sigT.inc(t.last)

            @blk.vector
            def _(v, ti=ti, par=par):
                v = LastW(v)
                S_.wait(v, tok[("qcT_mm", ti)])
                S_.wait(v, tok["consts_f"])
                for k in range(QRK):
                    v.tensor_scalar(out=q_cT[:, k, ti * 128:(ti + 1) * 128],
                                    in0=trP[par][:, k * 128:(k + 1) * 128],
                                    scalar1=gqa_b[:, k:k + 1], scalar2=None,
                                    op0=ALU.mult)
                tok[("qcT_evac", ti)] = sigV.inc(v.last)

      b_overlap_ok = False

    if dbg:
        dbg_kcT = nc.dram_tensor("dbg_kcT", [128, KVK * S], BF16, kind="ExternalOutput")
        dbg_krT = nc.dram_tensor("dbg_krT", [RD, S], BF16, kind="ExternalOutput")
        dbg_qcT = nc.dram_tensor("dbg_qcT", [128, QRK * Q], BF16, kind="ExternalOutput")
        with nc.Block() as blk:
            @blk.sync
            def _(sy):
                S_.wait(sy, tok[("kcT_evac", KT - 1)])
                S_.wait(sy, tok[("qcT_evac", QT - 1)])
                sy.dma_start(out=dbg_kcT[:], in_=k_cT.rearrange("p a b -> p (a b)")).then_inc(dO[0].sem, 16)
                sy.dma_start(out=dbg_krT[:], in_=k_ropeT[:]).then_inc(dO[0].sem, 16)
                sy.dma_start(out=dbg_qcT[:], in_=q_cT.rearrange("p a b -> p (a b)")).then_inc(dO[0].sem, 16)
                dO[0].count += 48
                sy.wait_ge(dO[0].sem, dO[0].count)
    psA2.close()
    psA.close()

    psB = ExitStack()
    sPP = [psB.enter_context(nc.psum_tensor(f"sPP{i}", [128, 512], F32))
           for i in range(2)]
    PCq = psB.enter_context(nc.psum_tensor("PCq", [128, 1024], F32))
    PDq = psB.enter_context(nc.psum_tensor("PDq", [128, 1024], F32))
    PEp = [psB.enter_context(nc.psum_tensor(f"PEp{i}", [128, 512], F32))
           for i in range(2)]
    with nc.Block() as blk:
        b_preamble(blk)
        b_prep(blk, 0)
    for h in range(NH):
        with nc.Block() as blk:
            if h % 2 == 1 and (h + 3) // 2 < NP:
                b_prefetch(blk, (h + 3) // 2)
            if h >= 1:
                kt_tail_v1(blk, h - 1)
            if h + 1 < NH:
                b_prep_a(blk, h + 1)
                b_prep_b1(blk, h + 1)
            if h >= 1:
                kt_tail_t3(blk, h - 1)
                kt_tail_v4(blk, h - 1)
            if h + 1 < NH:
                b_prep_b2(blk, h + 1)
            kt_main(blk, h)
    with nc.Block() as blk:
        kt_tail(blk, NH - 1)

    if dbg:
        dbg_oall = nc.dram_tensor("dbg_oall", [128, NH * Q], BF16, kind="ExternalOutput")
        with nc.Block() as blk:
            @blk.sync
            def _(sy):
                S_.wait(sy, tok[("out_evac", NH - 1)])
                sy.dma_start(out=dbg_oall[:], in_=out_all.rearrange("p a b -> p (a b)")).then_inc(dO[0].sem, 16)
                dO[0].count += 16
                sy.wait_ge(dO[0].sem, dO[0].count)
    psB.close()

    x1_sb = view(V0, 128, [QT, D], BF16)
    _oa0, _oan = offreg[id(out_all)]
    zc = Cursor(ZLOW, V0)
    zgap = Cursor(V0 + QT * D * 2, _oa0)

    def calloc(part, shape, dt):
        esz = 2 if dt == BF16 else 4
        n = 1
        for s in shape:
            n *= s
        if zc.off + n * esz <= zc.limit:
            return zc.alloc(part, shape, dt)
        return zgap.alloc(part, shape, dt)

    wo_buf = [calloc(128, [KHN, OBW], BF16) for _ in range(2)]
    xr_buf0 = calloc(128, [D], F32)
    xr_buf = [xr_buf0, xr_buf0]
    ln3st = calloc(128, [QT, 6 * NOB], F32)
    mvv3 = calloc(128, [QT, 2], F32)
    rstd8 = calloc(128, [QT], F32)
    negmb8 = calloc(128, [QT], F32)
    rs8x = calloc(128, [QT], F32)
    rs8t = calloc(128, [QT], F32)
    magic8 = calloc(128, [QT], F32)
    t1_buf = calloc(128, [D], BF16)
    lastB = [("out_evac", NH - 1), ("bcast_used", NH - 1)]

    psC = ExitStack()
    oP = [psC.enter_context(nc.psum_tensor(f"oP{i}", [128, OBW], F32))
          for i in range(QT)]

    cpairs = [(b, kh) for b in range(NOB) for kh in range(KH)]

    woB = [dW[0].count, dW[1].count]
    for pi in range(len(cpairs)):
        tok[("wo", pi)] = (dW[pi % 2], woB[pi % 2] + 16 * (pi // 2 + 1))
    cS0 = sigS.count
    for b in range(NOB):
        for ti in range(QT):
            tok[("o_evac", b, ti)] = (sigS, cS0 + b * QT + ti + 1)
    aS0 = cS0 + NOB * QT
    for ti in range(QT):
        tok[("ln3ap", ti)] = (sigS, aS0 + ti + 1)
    xrB = [dX[0].count, dX[1].count]
    for ti in range(QT):
        tok[("xr", ti)] = (dX[ti % 2], xrB[ti % 2] + 16 * (ti // 2 + 1))

    with nc.Block() as blk:
        @blk.tensor
        def _(t):
            t = LastW(t)
            for pi, (b, kh) in enumerate(cpairs):
                S_.wait(t, tok[("wo", pi)])
                if kh == 0:
                    for ti in range(QT):
                        S_.wait(t, tok.get(("o_evac", b - 1, ti)))
                for k2 in range(KHN):
                    k = kh * KHN + k2
                    for ti in range(QT):
                        t.matmul(oP[ti][:],
                                 out_all[:, k, ti * 128:(ti + 1) * 128],
                                 wo_buf[pi % 2][:, k2, :],
                                 start=(k == 0), stop=(k == NH - 1))
                tok[("o_mmp", pi)] = sigT.inc(t.last)
                if kh == KH - 1:
                    tok[("o_mm", b)] = tok[("o_mmp", pi)]

        @blk.vector
        def _(v):
            v = LastW(v)
            v.memset(magic8[:].bitcast(mybir.dt.int32), 0x5F3759DF)
            for b in range(NOB):
                for ti in range(QT):
                    S_.wait(v, tok[("o_evac", b, ti)])
                    v.bn_stats(out=ln3st[:, ti, 6 * b:6 * b + 6],
                               in_=x1_sb[:, ti, b * OBW:(b + 1) * OBW])
            t = sigV.inc(v.last)
            S_.wait(v, t)
            for ti in range(QT):
                v.bn_aggr(out=mvv3[:, ti, :], in_=ln3st[:, ti, :])
            t = sigV.inc(v.last)
            S_.wait(v, t)
            mvf = mvv3.rearrange("p a b -> p (a b)")
            v_rsqrtN(v, mvf[:, 1:2 * QT:2], rstd8[:], rs8x[:], rs8t[:],
                     magic8[:], LN_EPS)
            t = sigV.inc(v.last)
            S_.wait(v, t)
            v.tensor_tensor(out=negmb8[:], in0=mvf[:, 0:2 * QT:2],
                            in1=rstd8[:], op=ALU.mult)
            t = sigV.inc(v.last)
            S_.wait(v, t)
            v.tensor_scalar(out=negmb8[:], in0=negmb8[:], scalar1=-1.0,
                            scalar2=None, op0=ALU.mult)
            tok["ln3rdy"] = sigV.inc(v.last)
            for ti in range(QT):
                par = ti % 2
                S_.wait(v, tok[("ln3ap", ti)])
                S_.wait(v, tok[("xr", ti)])
                v.tensor_tensor(out=x1_sb[:, ti, :], in0=t1_buf[:],
                                in1=xr_buf[par][:], op=ALU.add)
                tok[("x1_done", ti)] = sigV.inc(v.last)

        @blk.scalar
        def _(s):
            s = LastW(s)
            for b in range(NOB):
                S_.wait(s, tok[("o_mm", b)])
                if b == 0:
                    S_.wait_many(s, [tok[k] for k in lastB])
                for ti in range(QT):
                    s.copy(out=x1_sb[:, ti, b * OBW:(b + 1) * OBW],
                           in_=oP[ti][:])
                    tk = sigS.inc(s.last)
                    assert tk == tok[("o_evac", b, ti)]
            for ti in range(QT):
                S_.wait(s, tok["ln3rdy"])
                S_.wait(s, tok.get(("x1_done", ti - 1)))
                s.activation(out=t1_buf[:], in_=x1_sb[:, ti, :],
                             func=AF.Identity,
                             scale=rstd8[:, ti:ti + 1],
                             bias=negmb8[:, ti:ti + 1])
                tk = sigS.inc(s.last)
                assert tk == tok[("ln3ap", ti)]

        @blk.gpsimd
        def _(g):
            g = LastW(g)
            S_.wait_many(g, [tok[k] for k in lastB])
            for pi, (b, kh) in enumerate(cpairs):
                if pi >= 2:
                    S_.wait(g, tok[("o_mmp", pi - 2)])
                tk = dW[pi % 2].inc(
                    g.dma_start(out=wo_buf[pi % 2][:],
                                in_=wo_d[b, :, kh * KHN:(kh + 1) * KHN, :]), 16)
                assert tk == tok[("wo", pi)]

        @blk.sync
        def _(sy):
            sy = LastW(sy)
            S_.wait_many(sy, [tok[k] for k in lastB])
            for ti in range(QT):
                S_.wait(sy, tok.get(("x1_done", ti - 1)))
                tk = dX[ti % 2].inc(
                    sy.dma_start(out=xr_buf[ti % 2][:], in_=xv[ti, :, :]), 16)
                assert tk == tok[("xr", ti)]

    if dbg:
        dbg_o = nc.dram_tensor("dbg_o", [128, QT * D], BF16, kind="ExternalOutput")
        dbg_x1 = nc.dram_tensor("dbg_x1", [128, QT * D], BF16, kind="ExternalOutput")
        with nc.Block() as blk:
            @blk.sync
            def _(sy):
                S_.wait(sy, tok[("x1_done", QT - 1)])
                sy.dma_start(out=dbg_x1[:], in_=x1_sb.rearrange("p a b -> p (a b)")).then_inc(dO[0].sem, 16)
                dO[0].count += 16
                sy.wait_ge(dO[0].sem, dO[0].count)
    psC.close()

    zd = Cursor(ZLOW, V0)
    zd2 = Cursor(V0 + QT * D * 2, PB)

    def dalloc(part, shape, dt):
        esz = 2 if dt == BF16 else 4
        n = 1
        for s in shape:
            n *= s
        if zd.off + n * esz <= zd.limit:
            return zd.alloc(part, shape, dt)
        return zd2.alloc(part, shape, dt)

    ln2T_g = dalloc(128, [DK, GT * 128], BF16)
    at_b = dalloc(128, [DK, ER], BF16)
    bflat_b = dalloc(ER, [DO], BF16)
    wrt_b = dalloc(128, [DK, NE], BF16)
    bbase_b = dalloc(1, [DO], BF16)
    lgB = dalloc(128, [GT, NE], F32)
    mskB = dalloc(128, [GT, NE], F32)
    explB = dalloc(128, [GT, NE], F32)
    m1B = dalloc(128, [GT], F32)
    m2B = dalloc(128, [GT], F32)
    rcolA = dalloc(128, [GT], F32)
    rcolB = dalloc(128, [GT], F32)
    colcB = dalloc(128, [GT], F32)
    comb_g = dalloc(128, [GT, NE], BF16)
    combT_sb = dalloc(NE, [GT * 128], BF16)
    gate_sb = dalloc(128, [GT * 128], BF16)
    hg_sb = dalloc(128, [GT * 128], BF16)
    wbt_buf = [zd2.alloc(128, [DKHN, FBW], BF16) for _ in range(2)]
    ffn_g = zd2.alloc(128, [GT, DO], BF16)
    ln2_tmp = dalloc(128, [D], BF16)
    ln4_tmp = dalloc(128, [DO], BF16)
    och_buf = [dalloc(128, [min(D, 512)], F32) for _ in range(2)]
    ln4st = dalloc(128, [GT, 6 * FOB], F32)
    mvv4 = dalloc(128, [GT, 2], F32)
    rstd4 = dalloc(128, [GT], F32)
    negmb4 = dalloc(128, [GT], F32)
    rs4x = dalloc(128, [GT], F32)
    rs4t = dalloc(128, [GT], F32)
    magic4 = dalloc(128, [GT], F32)
    assert zd2.off <= PB

    lastC = [("x1_done", QT - 1)]
    with nc.Block() as blk:
        @blk.gpsimd
        def _(g):
            g = LastW(g)
            S_.wait_many(g, [tok[k] for k in lastC])
            dW2[0].inc(g.dma_start(out=at_b[:], in_=at_d[:]), 16)
            dW2[0].inc(g.dma_start(out=bflat_b[:], in_=bflat_d[:]), 16)
            dW2[0].inc(g.dma_start(out=wrt_b[:], in_=wrt_d[:]), 16)
            tok["ffn_w"] = dW2[0].inc(
                g.dma_start(out=bbase_b[:], in_=bbase_d[:]), 16)

    ov = out_d.rearrange("(t p) d -> t p d", p=128)
    OCH = max(1, D // 512)
    ocw = D // OCH
    nw = (DK + 3) // 4
    dpairs = [(b, kh) for b in range(FOB) for kh in range(DKH)]

    def d_tail_sections(blk, grp, want_dbg):
        @blk.vector
        def _(v, grp=grp):
            v = LastW(v)
            for tg in range(GT):
                ti = grp * GT + tg
                S_.wait(v, tok[("ln4ap", ti)])
                dOb = [dO[0].count, dO[1].count]
                for c_ in range(OCH):
                    cpar = c_ % 2
                    if c_ >= 2:
                        S_.wait(v, (dO[cpar], dOb[cpar] + 16 * ((c_ - 2) // 2 + 1)))
                    else:
                        S_.wait(v, tok.get(("odma", ti - 1, OCH - 1)))
                        S_.wait(v, tok.get(("odma", ti - 1, OCH - 2)))
                    v.tensor_tensor(out=och_buf[cpar][:, 0:ocw],
                                    in0=ln4_tmp[:, c_ * ocw:(c_ + 1) * ocw],
                                    in1=x1_sb[:, ti, c_ * ocw:(c_ + 1) * ocw],
                                    op=ALU.add)
                    tok[("och", ti, c_)] = sigV.inc(v.last)
                for c_ in range(OCH):
                    tok[("odma", ti, c_)] = (dO[c_ % 2],
                                             dOb[c_ % 2] + 16 * (c_ // 2 + 1))
                    dO[c_ % 2].count += 16

        @blk.scalar
        def _(s, grp=grp):
            s = LastW(s)
            for tg in range(GT):
                ti = grp * GT + tg
                S_.wait(s, tok[("ln4rdy", grp)])
                if want_dbg:
                    S_.wait(s, tok[("dbgf", grp)])
                S_.wait(s, tok.get(("och", ti - 1, OCH - 1)))
                s.activation(out=ln4_tmp[:], in_=ffn_g[:, tg, :],
                             func=AF.Identity,
                             scale=rstd4[:, tg:tg + 1],
                             bias=negmb4[:, tg:tg + 1])
                tk = sigS.inc(s.last)
                assert tk == tok[("ln4ap", ti)], (tk, tok[("ln4ap", ti)])

        @blk.sync
        def _(sy, grp=grp):
            sy = LastW(sy)
            for tg in range(GT):
                ti = grp * GT + tg
                for c_ in range(OCH):
                    cpar = c_ % 2
                    S_.wait(sy, tok[("och", ti, c_)])
                    inst = sy.dma_start(out=ov[ti, :, c_ * ocw:(c_ + 1) * ocw],
                                        in_=och_buf[cpar][:, 0:ocw])
                    inst.then_inc(dO[cpar].sem, 16)
                    assert tok[("odma", ti, c_)][0] is dO[cpar]

    for grp in range(DGRP):
        psD = ExitStack()
        trD = [psD.enter_context(nc.psum_tensor(f"trD{grp}{i}", [128, 512], BF16))
               for i in range(2)]
        lgP = psD.enter_context(nc.psum_tensor(f"lgP{grp}", [128, NE], F32))
        hP = psD.enter_context(nc.psum_tensor(f"hP{grp}", [128, GT * 128], F32))
        gateP = psD.enter_context(nc.psum_tensor(f"gateP{grp}", [128, GT * 128], F32))

        with nc.Block() as blk:
            pS0 = sigS.count
            off = 0
            if grp > 0:
                for tg in range(GT):
                    tok[("ln4ap", (grp - 1) * GT + tg)] = (sigS, pS0 + tg + 1)
                off = GT
            for tg in range(GT):
                ti = grp * GT + tg
                base = pS0 + off + tg * (nw + 1)
                tok[("ln2a", ti)] = (sigS, base + 1)
                for wv in range(nw):
                    tok[("ln2T_ev", ti, wv)] = (sigS, base + 2 + wv)
                tok[("ln2T_done", ti)] = tok[("ln2T_ev", ti, nw - 1)]
            tok[("tk_exp", grp)] = (sigS, pS0 + off + GT * (nw + 1) + 1)
            pT0 = sigT.count
            for tg in range(GT):
                ti = grp * GT + tg
                tb = pT0 + tg * nw
                for wv in range(nw):
                    tok[("ln2T_mm", ti, wv)] = (sigT, tb + wv + 1)
                tok[("ln2T_mm_all", ti)] = tok[("ln2T_mm", ti, nw - 1)]
                tok[("lg_mm", ti)] = (sigT, pT0 + GT * nw + tg + 1)
            tb2 = pT0 + GT * (nw + 1)
            for tg in range(GT):
                tok[("combT_mm", grp * GT + tg)] = (sigT, tb2 + tg + 1)
            tok[("gate_mm", grp)] = (sigT, tb2 + GT + 1)
            tok[("h_mm", grp)] = (sigT, tb2 + GT + 2)

            if grp > 0:
                d_tail_sections(blk, grp - 1, dbg)

            @blk.vector
            def _(v, grp=grp):
                v = LastW(v)
                for tg in range(GT):
                    ti = grp * GT + tg
                    par = ti % 2
                    S_.wait(v, tok[("x1_done", ti)])
                    if grp > 0:
                        S_.wait(v, tok[("ln4ap", grp * GT - 1)])
                    S_.wait(v, tok.get(("ln2a", ti - 2)))
                    ln_stats(v, x1_sb[:, ti, :], D, ("ln2s", ti), par)
                    S_.wait(v, tok[("ln2s", ti)])
                    v_rsqrt(v, par, mv[par][:, 1:2], LN_EPS)
                    tok[("ln2sq", ti)] = sigV.inc(v.last)
                    tok[("ln2fin", ti)] = ln_finish(v, ("ln2sq", ti), par)
                for tg in range(GT):
                    ti = grp * GT + tg
                    S_.wait(v, tok[("lg_mm", ti)])
                    v.tensor_copy(out=lgB[:, tg, :], in_=lgP[:])
                    tok[("lg_used", ti)] = sigV.inc(v.last)
                S_.wait(v, tok[("lg_used", grp * GT + GT - 1)])
                v.reduce_max(out=m1B[:], in_=lgB[:], axis=AX.X)
                t = sigV.inc(v.last)
                S_.wait(v, t)
                for tg in range(GT):
                    v.tensor_scalar(out=mskB[:, tg, :], in0=lgB[:, tg, :],
                                    scalar1=m1B[:, tg:tg + 1],
                                    scalar2=None, op0=ALU.is_equal)
                t = sigV.inc(v.last)
                S_.wait(v, t)
                v.scalar_tensor_tensor(out=mskB[:], in0=mskB[:],
                                       scalar=-1e30, in1=lgB[:],
                                       op0=ALU.mult, op1=ALU.add)
                t = sigV.inc(v.last)
                S_.wait(v, t)
                v.reduce_max(out=m2B[:], in_=mskB[:], axis=AX.X)
                t = sigV.inc(v.last)
                S_.wait(v, t)
                v.tensor_scalar(out=rcolA[:], in0=m1B[:], scalar1=-1.0,
                                scalar2=None, op0=ALU.mult)
                t = sigV.inc(v.last)
                S_.wait(v, t)
                v.tensor_tensor(out=rcolB[:], in0=m2B[:], in1=rcolA[:],
                                op=ALU.add)
                tok[("tk1", grp)] = sigV.inc(v.last)
                S_.wait(v, tok[("tk_exp", grp)])
                v.tensor_scalar(out=colcB[:], in0=colcB[:],
                                scalar1=1.0, scalar2=None, op0=ALU.add)
                t = sigV.inc(v.last)
                S_.wait(v, t)
                v.reciprocal(out=colcB[:], in_=colcB[:])
                t = sigV.inc(v.last)
                S_.wait(v, t)
                for tg in range(GT):
                    v.tensor_scalar(out=mskB[:, tg, :], in0=lgB[:, tg, :],
                                    scalar1=m2B[:, tg:tg + 1],
                                    scalar2=None, op0=ALU.is_ge)
                t = sigV.inc(v.last)
                S_.wait(v, t)
                v.tensor_tensor(out=explB[:], in0=explB[:], in1=mskB[:],
                                op=ALU.mult)
                t = sigV.inc(v.last)
                S_.wait(v, t)
                for tg in range(GT):
                    ti = grp * GT + tg
                    v.tensor_scalar(out=comb_g[:, tg, :], in0=explB[:, tg, :],
                                    scalar1=colcB[:, tg:tg + 1], scalar2=2.0,
                                    op0=ALU.mult, op1=ALU.mult)
                    tok[("comb", ti)] = sigV.inc(v.last)

            vV0 = sigV.count
            for tg in range(GT):
                tok[("combT_ev", grp * GT + tg)] = (sigV, vV0 + tg + 1)
            tok[("gate_ev", grp)] = (sigV, vV0 + GT + 1)
            tok[("hg", grp)] = (sigV, vV0 + GT + 2)

            @blk.tensor
            def _(t, grp=grp):
                t = LastW(t)
                for tg in range(GT):
                    ti = grp * GT + tg
                    S_.wait(t, tok[("ln2a", ti)])
                    for wv in range(nw):
                        if wv >= 2:
                            S_.wait(t, tok[("ln2T_ev", ti, wv - 2)])
                        elif tg > 0:
                            S_.wait(t, tok[("ln2T_ev", ti - 1, nw - 2 + wv)])
                        for k in range(wv * 4, min(wv * 4 + 4, DK)):
                            t.transpose(trD[wv % 2][:, (k % 4) * 128:(k % 4 + 1) * 128],
                                        ln2_tmp[:, k * 128:(k + 1) * 128],
                                        ident_b[:])
                        tk = sigT.inc(t.last)
                        assert tk == tok[("ln2T_mm", ti, wv)]
                for tg in range(GT):
                    ti = grp * GT + tg
                    S_.wait(t, tok[("ln2T_done", ti)])
                    S_.wait(t, tok["ffn_w"])
                    S_.wait(t, tok.get(("lg_used", ti - 1)))
                    for k in range(DK):
                        t.matmul(lgP[:], ln2T_g[:, k, tg * 128:(tg + 1) * 128],
                                 wrt_b[:, k, :], start=(k == 0), stop=(k == DK - 1))
                    tk = sigT.inc(t.last)
                    assert tk == tok[("lg_mm", ti)]

            @blk.tensor
            def _(t, grp=grp):
                t = LastW(t)
                S_.wait(t, tok[("ln2T_ev", grp * GT + GT - 1, nw - 1)])
                for tg in range(GT):
                    ti = grp * GT + tg
                    S_.wait(t, tok[("comb", ti)])
                    S_.wait(t, tok.get(("combT_ev", ti - 1)))
                    t.transpose(trD[0][0:NE, 0:128], comb_g[:, tg, :], ident_b[:])
                    tk = sigT.inc(t.last)
                    assert tk == tok[("combT_mm", ti)]
                for tg in range(GT):
                    S_.wait(t, tok[("combT_ev", grp * GT + tg)])
                t.matmul(gateP[:, 0:GT * 128], emat_b[:], combT_sb[:],
                         start=True, stop=True)
                tk = sigT.inc(t.last)
                assert tk == tok[("gate_mm", grp)]
                for (g0, gw) in [(s0, min(512, GT * 128 - s0))
                                 for s0 in range(0, GT * 128, 512)]:
                    for k in range(DK):
                        t.matmul(hP[:, g0:g0 + gw], at_b[:, k, :],
                                 ln2T_g[:, k, g0:g0 + gw],
                                 start=(k == 0), stop=(k == DK - 1))
                tk = sigT.inc(t.last)
                assert tk == tok[("h_mm", grp)]

            @blk.vector
            def _(v, grp=grp):
                v = LastW(v)
                for tg in range(GT):
                    ti = grp * GT + tg
                    S_.wait(v, tok[("combT_mm", ti)])
                    v.tensor_copy(out=combT_sb[:, tg * 128:(tg + 1) * 128],
                                  in_=trD[0][0:NE, 0:128])
                    tk = sigV.inc(v.last)
                    assert tk == tok[("combT_ev", ti)]
                S_.wait(v, tok[("gate_mm", grp)])
                v.tensor_copy(out=gate_sb[:], in_=gateP[:, 0:GT * 128])
                tk = sigV.inc(v.last)
                assert tk == tok[("gate_ev", grp)]
                S_.wait(v, tk)
                S_.wait(v, tok[("h_mm", grp)])
                v.tensor_tensor(out=hg_sb[:], in0=hP[:, 0:GT * 128],
                                in1=gate_sb[:], op=ALU.mult)
                tk = sigV.inc(v.last)
                assert tk == tok[("hg", grp)]

            @blk.scalar
            def _(s, grp=grp):
                s = LastW(s)
                for tg in range(GT):
                    ti = grp * GT + tg
                    par = ti % 2
                    S_.wait(s, tok[("ln2fin", ti)])
                    S_.wait(s, tok.get(("ln2T_mm_all", ti - 1)))
                    s.activation(out=ln2_tmp[:], in_=x1_sb[:, ti, :],
                                 func=AF.Identity,
                                 scale=col_a[par][:], bias=col_b[par][:])
                    tk = sigS.inc(s.last)
                    assert tk == tok[("ln2a", ti)]
                    for wv in range(nw):
                        S_.wait(s, tok[("ln2T_mm", ti, wv)])
                        nk = min(4, DK - wv * 4)
                        for k2 in range(nk):
                            s.copy(out=ln2T_g[:, wv * 4 + k2, tg * 128:(tg + 1) * 128],
                                   in_=trD[wv % 2][:, k2 * 128:(k2 + 1) * 128])
                        tk = sigS.inc(s.last)
                        assert tk == tok[("ln2T_ev", ti, wv)]
                S_.wait(s, tok[("tk1", grp)])
                for tg in range(GT):
                    s.activation(out=explB[:, tg, :], in_=lgB[:, tg, :],
                                 func=AF.Exp, bias=rcolA[:, tg:tg + 1],
                                 scale=1.0)
                s.activation(out=colcB[:], in_=rcolB[:], func=AF.Exp)
                tk = sigS.inc(s.last)
                assert tk == tok[("tk_exp", grp)]

            @blk.gpsimd
            def _(g, grp=grp):
                g = LastW(g)
                wbB = [dW[0].count, dW[1].count]
                for pi in range(min(2, len(dpairs))):
                    b, kh = dpairs[pi]
                    S_.wait(g, tok.get(("wbt_used", grp - 1)))
                    if grp == 0 and pi == 0:
                        S_.wait(g, tok.get(("x1_done", QT - 1)))
                    tok[("wbt", grp, pi)] = dW[pi % 2].inc(
                        g.dma_start(out=wbt_buf[pi % 2][:],
                                    in_=wbt_d[b, :, kh * DKHN:(kh + 1) * DKHN, :]), 16)

        psD.close()

        psD2 = ExitStack()
        fP = [psD2.enter_context(nc.psum_tensor(f"fP{grp}{i}", [128, FBW], F32))
              for i in range(GT)]

        mS0 = sigS.count
        for b in range(FOB):
            for tg in range(GT):
                tok[("f_evac", grp, b, tg)] = (sigS, mS0 + b * GT + tg + 1)
        wbB = [dW[0].count, dW[1].count]
        for pi in range(2, len(dpairs)):
            tok[("wbt", grp, pi)] = (dW[pi % 2], wbB[pi % 2] + 16 * ((pi - 2) // 2 + 1))

        with nc.Block() as blk:
            @blk.tensor
            def _(t, grp=grp):
                t = LastW(t)
                S_.wait(t, tok[("hg", grp)])
                S_.wait(t, tok["ffn_w"])
                for pi, (b, kh) in enumerate(dpairs):
                    S_.wait(t, tok[("wbt", grp, pi)])
                    if kh == 0:
                        for tg in range(GT):
                            S_.wait(t, tok.get(("f_evac", grp, b - 1, tg)))
                    for k2 in range(DKHN):
                        k = kh * DKHN + k2
                        for tg in range(GT):
                            t.matmul(fP[tg][:],
                                     ln2T_g[:, k, tg * 128:(tg + 1) * 128],
                                     wbt_buf[pi % 2][:, k2, :],
                                     start=(k == 0), stop=False)
                    if kh == DKH - 1:
                        for tg in range(GT):
                            t.matmul(fP[tg][:], hg_sb[:, tg * 128:(tg + 1) * 128],
                                     bflat_b[:, b * FBW:(b + 1) * FBW],
                                     start=False, stop=False)
                            t.matmul(fP[tg][:], ones_row[:],
                                     bbase_b[:, b * FBW:(b + 1) * FBW],
                                     start=False, stop=True)
                    tok[("f_mmp", grp, pi)] = sigT.inc(t.last)
                    if kh == DKH - 1:
                        tok[("f_mm", grp, b)] = tok[("f_mmp", grp, pi)]
                tok[("wbt_used", grp)] = tok[("f_mmp", grp, len(dpairs) - 1)]

            @blk.scalar
            def _(s, grp=grp):
                s = LastW(s)
                for b in range(FOB):
                    S_.wait(s, tok[("f_mm", grp, b)])
                    for tg in range(GT):
                        s.copy(out=ffn_g[:, tg, b * FBW:(b + 1) * FBW],
                               in_=fP[tg][:])
                        tk = sigS.inc(s.last)
                        assert tk == tok[("f_evac", grp, b, tg)]

            @blk.gpsimd
            def _(g, grp=grp):
                g = LastW(g)
                for pi in range(2, len(dpairs)):
                    b2, kh2 = dpairs[pi]
                    S_.wait(g, tok[("f_mmp", grp, pi - 2)])
                    tk = dW[pi % 2].inc(
                        g.dma_start(out=wbt_buf[pi % 2][:],
                                    in_=wbt_d[b2, :, kh2 * DKHN:(kh2 + 1) * DKHN, :]), 16)
                    assert tk == tok[("wbt", grp, pi)]

            @blk.vector
            def _(v, grp=grp):
                v = LastW(v)
                v.memset(magic4[:].bitcast(mybir.dt.int32), 0x5F3759DF)
                for b in range(FOB):
                    for tg in range(GT):
                        S_.wait(v, tok[("f_evac", grp, b, tg)])
                        v.bn_stats(out=ln4st[:, tg, 6 * b:6 * b + 6],
                                   in_=ffn_g[:, tg, b * FBW:(b + 1) * FBW])
                t = sigV.inc(v.last)
                S_.wait(v, t)
                for tg in range(GT):
                    v.bn_aggr(out=mvv4[:, tg, :], in_=ln4st[:, tg, :])
                t = sigV.inc(v.last)
                S_.wait(v, t)
                mvf4 = mvv4.rearrange("p a b -> p (a b)")
                v_rsqrtN(v, mvf4[:, 1:2 * GT:2], rstd4[:], rs4x[:], rs4t[:],
                         magic4[:], LN_EPS)
                t = sigV.inc(v.last)
                S_.wait(v, t)
                v.tensor_tensor(out=negmb4[:], in0=mvf4[:, 0:2 * GT:2],
                                in1=rstd4[:], op=ALU.mult)
                t = sigV.inc(v.last)
                S_.wait(v, t)
                v.tensor_scalar(out=negmb4[:], in0=negmb4[:], scalar1=-1.0,
                                scalar2=None, op0=ALU.mult)
                tok[("ln4rdy", grp)] = sigV.inc(v.last)

            if grp == DGRP - 1:
                pS0f = sigS.count
                for tg in range(GT):
                    tok[("ln4ap", grp * GT + tg)] = (sigS, pS0f + tg + 1)
                d_tail_sections(blk, grp, False)
        psD2.close()

        if dbg:
            dbg_f = nc.dram_tensor(f"dbg_f{grp}", [128, GT * DO], BF16,
                                   kind="ExternalOutput")
            with nc.Block() as blk:
                @blk.sync
                def _(sy, grp=grp, dbg_f=dbg_f):
                    for b in range(FOB):
                        for tg in range(GT):
                            S_.wait(sy, tok[("f_evac", grp, b, tg)])
                    sy.dma_start(out=dbg_f[:], in_=ffn_g.rearrange("p a b -> p (a b)")).then_inc(dO[0].sem, 16)
                    dO[0].count += 16
                    sy.wait_ge(dO[0].sem, dO[0].count)
                    tok[("dbgf", grp)] = (dO[0], dO[0].count)

    with nc.Block() as blk:
        @blk.sync
        def _(sy):
            sy.wait_ge(dO[0].sem, dO[0].count)
            sy.wait_ge(dO[1].sem, dO[1].count)

    nc._layout = {"ZLOW": ZLOW, "V0": V0,
                  "ln2T_g": offreg[id(ln2T_g)], "at_b": offreg[id(at_b)],
                  "x1_sb": offreg[id(x1_sb)], "ln2_tmp": offreg[id(ln2_tmp)],
                  "t1_buf": offreg[id(t1_buf)]}
    ctx.close()
    return nc


def make_in_maps(cfg, inputs, n_cores=None):
    w = dict(
        w_qa=inputs["L0_attn_w_qa"], g_qa=inputs["L0_attn_g_qa"],
        w_qb=inputs["L0_attn_w_qb"], w_kva=inputs["L0_attn_w_kva"],
        g_kva=inputs["L0_attn_g_kva"], w_kvb=inputs["L0_attn_w_kvb"],
        w_o=inputs["L0_attn_w_o"], W_base=inputs["L0_ffn_W_base"],
        b_base=inputs["L0_ffn_b_base"], W_router=inputs["L0_ffn_W_router"],
        A_stack=inputs["L0_ffn_A_stack"], B_stack=inputs["L0_ffn_B_stack"],
    )
    wp = prep_weights(cfg, w)
    x = np.asarray(inputs["L0_attn_hidden_states"])
    Q, S, RD = cfg["Q"], cfg["S"], cfg["RD"]
    halves = S // Q
    if n_cores is None:
        n_cores = x.shape[0] * halves
    c = np.ascontiguousarray
    in_maps = []
    for ci in range(n_cores):
        b, h = divmod(ci, halves)
        xf = x[b]
        perm = np.concatenate([np.arange(h * Q, (h + 1) * Q),
                               np.arange(0, h * Q),
                               np.arange((h + 1) * Q, S)]).astype(np.int64)
        xb_perm = c(xf[perm])
        coskp = c(wp["_cos"][perm].reshape(S // 128, 128, RD).transpose(1, 0, 2))
        sinkp = c(wp["_sin"][perm].reshape(S // 128, 128, RD).transpose(1, 0, 2))
        cosq2 = c(np.concatenate([wp["_cos"][h * Q:(h + 1) * Q].T] * 2, axis=1))
        sinq2 = c(np.concatenate([wp["_sin"][h * Q:(h + 1) * Q].T] * 2, axis=1))
        in_maps.append(dict(
            xb=xb_perm, wqa=wp["wqa"], wkva=wp["wkva"], wqb=wp["wqb"],
            wkvb=wp["wkvb"], wo=wp["wo"], wbt=wp["wbt"], wrt=wp["wrt"],
            at=wp["at"], bflat=wp["bflat"], bbase=wp["bbase"],
            gqa=wp["gqa"], gkva=wp["gkva"], cosk=coskp, sink=sinkp,
            cosq2=cosq2, sinq2=sinq2, emat=wp["emat"], ident=wp["ident"]))
    return in_maps


def kernel(**inputs):
    cfg = full_cfg()
    in_maps = make_in_maps(cfg, inputs)
    nc = build_program(cfg)
    res = run_bass_kernel_spmd(nc, in_maps, core_ids=list(range(8)))
    x = np.asarray(inputs["L0_attn_hidden_states"])
    Q, S = cfg["Q"], cfg["S"]
    out = np.empty(x.shape, dtype=np.float32)
    for ci in range(8):
        b, h = divmod(ci, S // Q)
        out[b, h * Q:(h + 1) * Q, :] = res.results[ci]["out"]
    return out
